# version 07
# baseline (speedup 1.0000x reference)
import contextlib
import numpy as np
import ml_dtypes

import concourse.bass as bass
import concourse.mybir as mybir
from concourse.bass_utils import run_bass_kernel_spmd

F32 = mybir.dt.float32
BF16 = mybir.dt.bfloat16
AF = mybir.ActivationFunctionType
ALU = mybir.AluOpType
AX = mybir.AxisListType
BF = ml_dtypes.bfloat16


class Cfg:
    def __init__(self, small=False, tiny=False):
        if tiny:
            self.D, self.TC, self.SS, self.QCH = 256, 128, 128, 128
            self.NEXP, self.INTER, self.NCORE = 4, 256, 2
        elif small:
            self.D, self.TC, self.SS, self.QCH = 256, 256, 128, 128
            self.NEXP, self.INTER, self.NCORE = 4, 256, 2
        else:
            self.D, self.TC, self.SS, self.QCH = 2048, 2048, 512, 256
            self.NEXP, self.INTER, self.NCORE = 16, 1408, 8
        self.SINTER = 2 * self.INTER
        self.L = 64
        self.NB = self.D // 128
        self.NST = self.TC // self.SS
        self.NPASS = self.D // self.QCH
        self.PBLK = self.QCH // 128
        self.NCHK = self.SS // self.L
        self.DMIX = 32
        self.DD = 64
        self.IB = self.INTER // 128
        self.GT = self.TC // 2
        self.NSLAB = self.GT // self.SS
        self.EPS_GN = 1e-5 * 64.0
        self.EPS_LN = 1e-5


class Sem:
    __slots__ = ("name", "val", "h", "dense", "targets")

    def __init__(self, name, dense=False):
        self.name, self.val, self.h = name, 0, None
        self.dense = dense
        self.targets = set()


class B:

    ENGS = ("tensor", "vector", "scalar", "gpsimd", "sync")

    def __init__(self):
        self.mark_tensor = None
        self.mark_vector = None
        self.mark_gpsimd = None
        self.q = {e: [] for e in self.ENGS}
        self.sems = [Sem("sd", True), Sem("sgd", True), Sem("tt"), Sem("vv"),
                     Sem("sa"), Sem("scc", True), Sem("gg")]
        (self.sd, self.sgd, self.tt, self.vv,
         self.sa, self.scc, self.gg) = self.sems

    def wait(self, eng, sem, val=None):
        v = sem.val if val is None else val
        if v <= 0:
            return
        if not sem.dense:
            sem.targets.add(v)
        self.q[eng].append(("w", sem, v))

    def op(self, eng, fn, inc=None, n=1):
        if inc is not None:
            inc.val += n
            if not inc.dense:
                inc.targets.add(inc.val)
            self.q[eng].append(("o", fn, inc, inc.val))
        else:
            self.q[eng].append(("o", fn, None, 0))

    def dma(self, eng, out, in_, **kw):
        s = self.sgd if eng == "gpsimd" else self.sd
        self.q[eng].append(
            ("o", lambda e, _o=out, _i=in_, _kw=kw: e.dma_start(
                out=_o, in_=_i, **_kw), s, -16))
        s.val += 16
        return s.val

    def vsync(self):
        self.wait("vector", self.vv)

    def barrier(self):
        self.op("tensor", self.mark_tensor, inc=self.tt)
        self.op("vector", self.mark_vector, inc=self.vv)
        self.op("gpsimd", self.mark_gpsimd, inc=self.gg)
        svals = [(self.tt, self.tt.val), (self.vv, self.vv.val),
                 (self.gg, self.gg.val), (self.sd, self.sd.val),
                 (self.sgd, self.sgd.val), (self.sa, self.sa.val),
                 (self.scc, self.scc.val)]
        for e in self.ENGS:
            for sem, v in svals:
                self.wait(e, sem, v)

    def lower(self, eng_handle, eng_name):
        ranks = {}
        for s in self.sems:
            if not s.dense:
                ranks[s.name] = {t: i + 1 for i, t in
                                 enumerate(sorted(s.targets))}
        for rec in self.q[eng_name]:
            if rec[0] == "w":
                _, sem, v = rec
                if sem.dense:
                    eng_handle.wait_ge(sem.h, v)
                else:
                    r = ranks[sem.name].get(v)
                    assert r is not None
                    eng_handle.wait_ge(sem.h, r)
            else:
                _, fn, sem, cnt = rec
                if sem is None:
                    fn(eng_handle)
                elif cnt == -16:
                    fn(eng_handle).then_inc(sem.h, 16)
                elif sem.dense:
                    fn(eng_handle).then_inc(sem.h, cnt - (cnt - 1))
                else:
                    if cnt in sem.targets:
                        fn(eng_handle).then_inc(sem.h, 1)
                    else:
                        fn(eng_handle)


def build_program(cfg, dbg=False, stop_after=None):
    c = cfg
    nc = bass.Bass("TRN2", target_bir_lowering=False)
    D, TC, SS, L, NB = c.D, c.TC, c.SS, c.L, c.NB
    NST, NPASS, PBLK, NCHK, QCH = c.NST, c.NPASS, c.PBLK, c.NCHK, c.QCH
    NEXP, IB, GT, NSLAB = c.NEXP, c.IB, c.GT, c.NSLAB

    dt = nc.dram_tensor
    xT = dt("xT", [D, TC + 1], F32, kind="ExternalInput").ap()
    maa_vecs = dt("maa_vecs", [128, 6 * NB], F32, kind="ExternalInput").ap()
    maa_w1 = dt("maa_w1", [D, 5 * c.DMIX], F32, kind="ExternalInput").ap()
    maa_w2 = dt("maa_w2", [c.DMIX, 5 * D], F32, kind="ExternalInput").ap()
    decay_vec = dt("decay_vec", [128, NB], F32, kind="ExternalInput").ap()
    decay_w1 = dt("decay_w1", [D, c.DD], F32, kind="ExternalInput").ap()
    decay_w2 = dt("decay_w2", [c.DD, D], F32, kind="ExternalInput").ap()
    u_vec = dt("u_vec", [128, NB], F32, kind="ExternalInput").ap()
    WqS = {s: dt("W%sS" % s, [NPASS, 128, NB * QCH], F32,
                 kind="ExternalInput").ap() for s in ("r", "k", "v", "g")}
    WoS = dt("WoS", [128, NB * D], F32, kind="ExternalInput").ap()
    gam = dt("gam", [128, NB], F32, kind="ExternalInput").ap()
    bet = dt("bet", [128, NB], F32, kind="ExternalInput").ap()
    gateT = dt("gateT", [D, NEXP], F32, kind="ExternalInput").ap()
    w1S = dt("w1S", [NEXP, IB, 128, D], F32, kind="ExternalInput").ap()
    w3S = dt("w3S", [NEXP, IB, 128, D], F32, kind="ExternalInput").ap()
    w2S = dt("w2S", [NEXP, NB, 128, c.INTER], F32,
             kind="ExternalInput").ap()
    sw1S = dt("sw1S", [2, IB, 128, D], F32, kind="ExternalInput").ap()
    sw3S = dt("sw3S", [2, IB, 128, D], F32, kind="ExternalInput").ap()
    sw2S = dt("sw2S", [2, NB, 128, c.INTER], F32, kind="ExternalInput").ap()
    cmask = dt("cmask", [128, 1], F32, kind="ExternalInput").ap()
    outT = dt("outT", [D, TC], F32, kind="ExternalOutput").ap()

    spk = "ExternalOutput" if dbg else "Internal"
    sp_xs = dt("sp_xs", [5, NST, 128, NB * SS], BF16, kind=spk)
    sp_y = dt("sp_y", [D, TC], F32, kind=spk)
    sp_rh = dt("sp_rh", [D, TC], BF16, kind=spk)
    sp_g = dt("sp_g", [D, TC], BF16, kind=spk)
    sp_h = dt("sp_h", [D, TC], F32, kind=spk)
    if dbg:
        d_r = dt("d_r", [D, TC], BF16, kind="ExternalOutput")
        d_k = dt("d_k", [D, TC], BF16, kind="ExternalOutput")
        d_v = dt("d_v", [D, TC], BF16, kind="ExternalOutput")
        d_pd = dt("d_pd", [D, TC], F32, kind="ExternalOutput")
        d_td = dt("d_td", [c.DD, TC], BF16, kind="ExternalOutput")
        d_cs = dt("d_cs", [D, TC], F32, kind="ExternalOutput")
        d_csr = dt("d_csr", [D, TC], F32, kind="ExternalOutput")
        d_bv = dt("d_bv", [D, TC // c.L], F32, kind="ExternalOutput")
        d_e1 = dt("d_e1", [D, TC], F32, kind="ExternalOutput")
        d_e5 = dt("d_e5", [D, TC // c.L], F32, kind="ExternalOutput")
        d_rt = dt("d_rt", [D, TC], BF16, kind="ExternalOutput")
        d_web = dt("d_web", [NEXP, 2 * NSLAB * SS], F32, kind="ExternalOutput")
        d_wtok = dt("d_wtok", [128, 2 * NSLAB * (SS // 128) * NEXP], F32,
                    kind="ExternalOutput")
    cc_in = dt("cc_in", [128, NB * L], F32)
    cc_out = dt("cc_out", [256, NB * L], F32)

    STREAMS = ("w", "k", "v", "r", "g")
    SMAA = {"w": 1, "k": 2, "v": 3, "r": 4, "g": 5}
    SF = {"w": 0, "k": 1, "v": 2, "r": 3, "g": 4}

    b = B()

    def blk3(ap_, nseg, seg):
        return ap_.rearrange("p (a t) -> p a t", a=nseg)

    with contextlib.ExitStack() as es:
        sb = lambda n, s, d=F32: es.enter_context(nc.sbuf_tensor(n, s, d))
        ps = lambda n, s, d=F32: es.enter_context(nc.psum_tensor(n, s, d))

        t_maa = sb("t_maa", [128, 6 * NB])
        t_dec = sb("t_dec", [128, NB])
        t_u = sb("t_u", [128, NB], BF16)
        t_gam = sb("t_gam", [128, NB])
        t_bet = sb("t_bet", [128, NB])
        t_ident = sb("t_ident", [128, 128], BF16)
        t_identf = sb("t_identf", [128, 128])
        t_maskU = sb("t_maskU", [128, L])
        t_ones1f = sb("t_ones1f", [1, 128])
        t_onescol = sb("t_onescol", [128, 1])
        t_hones = sb("t_hones", [128, 2])
        t_hpat = sb("t_hpat", [2, 128])
        t_cmask = sb("t_cmask", [128, 1])
        t_eps = sb("t_eps", [128, 2])
        t_zeroSS = sb("t_zeroSS", [128, SS])
        t_S = sb("t_S", [128, NB * L])
        t_Sin = sb("t_Sin", [128, NB * L], BF16)

        pA = ps("pA", [128, SS])
        pB = ps("pB", [128, SS])
        pY = ps("pY", [128, SS])
        pAT = ps("pAT", [128, SS])
        pTP = ps("pTP", [128, 1024], BF16)
        pU = ps("pU", [128, SS])
        pS1 = ps("pS1", [128, SS])
        pS2 = ps("pS2", [128, SS])

        for s in b.sems:
            s.h = es.enter_context(nc.semaphore(s.name))
        block = es.enter_context(nc.Block())

        maskU_np = np.triu(np.ones((64, 64), np.float32), 1)
        maskU_np = np.concatenate([maskU_np, maskU_np], 0)
        hones_np = np.zeros((128, 2), np.float32)
        hones_np[0:64, 0] = 1.0
        hones_np[64:128, 1] = 1.0
        consts = [
            (t_ident, nc.inline_tensor(np.eye(128, dtype=BF), "c_ident")),
            (t_identf, nc.inline_tensor(np.eye(128, dtype=np.float32), "c_identf")),
            (t_maskU, nc.inline_tensor(maskU_np, "c_maskU")),
            (t_ones1f, nc.inline_tensor(np.ones((1, 128), np.float32), "c_o1f")),
            (t_onescol, nc.inline_tensor(np.ones((128, 1), np.float32), "c_oc")),
            (t_hones, nc.inline_tensor(hones_np, "c_hones")),
            (t_hpat, nc.inline_tensor(np.ascontiguousarray(hones_np.T), "c_hpat")),
            (t_eps, nc.inline_tensor(np.broadcast_to(np.array(
                [c.EPS_GN, c.EPS_LN], np.float32), (128, 2)).copy(), "c_eps")),
        ]

        t_mark = sb("t_mark", [1, 2])
        b.mark_tensor = lambda e: e.matmul(
            pS1[0:1, 0:1], t_ones1f[0:1, 0:1], t_ones1f[0:1, 0:1],
            start=True, stop=True)
        b.mark_vector = lambda e: e.memset(t_mark[0:1, 0:1], 0.0)
        b.mark_gpsimd = lambda e: e.memset(t_mark[0:1, 1:2], 0.0)

        for dst, src in ((t_maa, maa_vecs), (t_dec, decay_vec),
                         (t_gam, gam), (t_bet, bet), (t_cmask, cmask)):
            b.dma("sync", dst[:, :], src[:, :])
        for dst, dr in consts:
            b.dma("sync", dst[:, :], dr.ap()[:, :])
        b.dma("gpsimd", t_u[:, :], u_vec[:, :])
        b.wait("vector", b.sd)
        b.op("vector", lambda e: e.memset(t_zeroSS[:, :], 0.0), inc=b.vv)
        b.op("vector", lambda e: e.memset(t_S[:, :], 0.0), inc=b.vv)

        if stop_after == "P0":
            pass
        with contextlib.ExitStack() as es1:
            sb1 = lambda n, s, d=F32: es1.enter_context(nc.sbuf_tensor(n, s, d))
            t_maaw1 = sb1("t_maaw1", [128, NB * 5 * c.DMIX], BF16)
            t_maaw2 = sb1("t_maaw2", [c.DMIX, 5 * D], BF16)
            t_dw1 = sb1("t_dw1", [128, NB * c.DD], BF16)
            t_dw2 = sb1("t_dw2", [c.DD, D], BF16)
            t_wq = {s: sb1("t_wq_%s" % s, [128, NB * QCH], BF16)
                    for s in ("r", "k", "v", "g")}
            t_xfull = sb1("t_xfull", [128, NB * (SS + 1)], BF16)
            t_xx = sb1("t_xx", [128, NB * SS], BF16)
            t_stream = [sb1("t_stream%d" % i, [128, NB * SS], BF16)
                        for i in range(2)]
            t_xxx = t_stream[1]
            t_a = sb1("t_a", [c.DMIX, 5 * SS], BF16)
            t_td = sb1("t_td", [c.DD, SS], BF16)
            t_rb = [sb1("t_rb%d" % i, [128, PBLK * SS], BF16) for i in range(2)]
            t_kb = [sb1("t_kb%d" % i, [128, PBLK * SS], BF16) for i in range(2)]
            t_vb = [sb1("t_vb%d" % i, [128, PBLK * SS], BF16) for i in range(2)]
            t_gqb = [sb1("t_gqb%d" % i, [128, PBLK * SS], BF16)
                     for i in range(2)]
            t_pdb = [sb1("t_pdb%d" % i, [128, PBLK * SS]) for i in range(2)]
            t_cs = sb1("t_cs", [128, SS])
            t_csr = sb1("t_csr", [128, SS])
            t_csx = sb1("t_csx", [128, SS])
            t_bx = sb1("t_bx", [128, SS])
            t_e1 = sb1("t_e1", [128, SS])
            t_rtb = [sb1("t_rtb%d" % i, [128, SS], BF16) for i in range(2)]
            t_ktb = [sb1("t_ktb%d" % i, [128, SS], BF16) for i in range(2)]
            t_khb = [sb1("t_khb%d" % i, [128, SS], BF16) for i in range(2)]
            t_rkb = [sb1("t_rkb%d" % i, [128, SS], BF16) for i in range(2)]
            t_rhatb = [sb1("t_rhatb%d" % i, [128, SS], BF16) for i in range(2)]
            t_bvec = sb1("t_bvec", [128, NCHK])
            t_evec = sb1("t_evec", [128, NCHK])
            t_plvb = [sb1("t_plvb%d" % i, [128, NCHK]) for i in range(2)]
            t_e5v = sb1("t_e5v", [128, NCHK])
            t_carryb = [sb1("t_carryb%d" % i, [128, PBLK]) for i in range(2)]
            t_Sbf = sb1("t_Sbf", [128, L], BF16)
            t_amt = sb1("t_amt", [128, SS], BF16)
            t_vtok = sb1("t_vtok", [128, 128], BF16)
            t_ktok = sb1("t_ktok", [128, 128], BF16)
            t_sdrow = sb1("t_sdrow", [1, 2 * SS])
            t_dv = sb1("t_dv", [128, SS], BF16)
            t_yob = [sb1("t_yob%d" % i, [128, SS]) for i in range(2)]

            b.dma("gpsimd", t_maaw1[:, :],
                  maa_w1.rearrange("(b p) f -> p b f", p=128))
            b.dma("gpsimd", t_maaw2[:, :], maa_w2[:, :])
            b.dma("gpsimd", t_dw1[:, :],
                  decay_w1.rearrange("(b p) f -> p b f", p=128))
            b.dma("gpsimd", t_dw2[:, :], decay_w2[:, :])

            last_proj = [0]
            wq_ld = [0]
            spill_sd = {}
            proj_tt = {}
            pa_free = {"vv": 0, "sa": 0}
            pb_free = {"vv": 0, "sa": 0}
            pd_free = {0: 0, 1: 0}
            gq_sd = {0: 0, 1: 0}
            post_tt = {0: 0, 1: 0}
            post_vv = [0]
            prods_bb = {0: 0, 1: 0}

            def wbank(bk):
                b.wait("tensor", b.vv, bk["vv"])
                b.wait("tensor", b.sa, bk["sa"])

            def emit_prologue(q):
                b.wait("gpsimd", b.tt, last_proj[0])
                for s in ("r", "k", "v", "g"):
                    b.dma("gpsimd", t_wq[s][:, :], WqS[s][q])
                wq_ld[0] = b.sgd.val
                b.wait("vector", b.vv)
                b.op("vector", lambda e, _q=q: e.memset(
                    t_carryb[_q % 2][:, :], 0.0), inc=b.vv)

            def emit_proj(q, st, par):
                qc0 = q * QCH
                tok0 = st * SS
                t_pd = t_pdb[par]
                t_r, t_k, t_v = t_rb[par], t_kb[par], t_vb[par]
                t_gq = t_gqb[par]
                wq_loaded = wq_ld[0]
                if True:
                    if q == 0:
                        b.wait("gpsimd", b.vv)
                        xf_d = b.dma("gpsimd", t_xfull[:, :],
                                     xT.rearrange("(b p) t -> p b t", p=128)
                                     [:, :, tok0:tok0 + SS + 1])
                        b.wait("vector", b.sgd, xf_d)
                        b.wait("vector", b.tt, proj_tt.get(1, 0))
                        x3 = blk3(t_xfull[:, :], NB, SS + 1)
                        b.op("vector", lambda e, _x=x3: e.tensor_tensor(
                            blk3(t_xx[:, :], NB, SS), _x[:, :, 0:SS],
                            _x[:, :, 1:SS + 1], ALU.subtract), inc=b.vv)
                        for bb in range(NB):
                            b.op("vector", lambda e, _b=bb: e.scalar_tensor_tensor(
                                t_xxx[:, _b * SS:(_b + 1) * SS],
                                t_xx[:, _b * SS:(_b + 1) * SS],
                                t_maa[:, _b:_b + 1],
                                t_xfull[:, _b * (SS + 1) + 1:(_b + 1) * (SS + 1)],
                                ALU.mult, ALU.add), inc=b.vv)
                        xxx_done = b.vv.val
                        b.wait("tensor", b.vv, xxx_done)
                        wbank(pa_free)
                        for f in range(5):
                            for kc in range(NB):
                                b.op("tensor", lambda e, _f=f, _kc=kc: e.matmul(
                                    pA[0:c.DMIX, 0:SS],
                                    t_maaw1[:, _kc * 5 * c.DMIX + _f * c.DMIX:
                                            _kc * 5 * c.DMIX + (_f + 1) * c.DMIX],
                                    t_xxx[:, _kc * SS:(_kc + 1) * SS],
                                    start=(_kc == 0), stop=(_kc == NB - 1)),
                                    inc=(b.tt if kc == NB - 1 else None))
                            b.wait("scalar", b.tt)
                            b.op("scalar", lambda e, _f=f: e.activation(
                                t_a[:, _f * SS:(_f + 1) * SS],
                                pA[0:c.DMIX, 0:SS], AF.Tanh), inc=b.sa)
                            b.wait("tensor", b.sa)
                        a_done = b.sa.val
                        pa_free["sa"] = a_done

                    rkvg_dst = {"r": t_r, "k": t_k, "v": t_v}
                    for si, s in enumerate(STREAMS):
                        sbuf = t_stream[si % 2]
                        if q == 0:
                            b.wait("vector", b.sd, spill_sd.get(si % 2, 0))
                            b.wait("vector", b.tt, proj_tt.get(si % 2, 0))
                            for bb in range(NB):
                                bpp = pB if bb % 2 == 0 else pA
                                bbk = pb_free if bb % 2 == 0 else pa_free
                                wbank(bbk)
                                b.op("tensor", lambda e, _s=s, _b=bb,
                                     _pp=bpp: e.matmul(
                                    _pp[:, 0:SS],
                                    t_maaw2[:, SF[_s] * D + _b * 128:
                                            SF[_s] * D + (_b + 1) * 128],
                                    t_a[:, SF[_s] * SS:(SF[_s] + 1) * SS],
                                    start=True, stop=True), inc=b.tt)
                                b.wait("vector", b.tt)
                                b.op("vector", lambda e, _s=s, _b=bb,
                                     _pp=bpp: e.scalar_tensor_tensor(
                                         t_bx[:, 0:SS], _pp[:, 0:SS],
                                         t_maa[:, SMAA[_s] * NB + _b:
                                               SMAA[_s] * NB + _b + 1],
                                         t_xx[:, _b * SS:(_b + 1) * SS],
                                         ALU.add, ALU.mult), inc=b.vv)
                                bbk["vv"] = b.vv.val
                                b.op("vector", lambda e, _sb=sbuf, _b=bb:
                                     e.tensor_tensor(
                                         _sb[:, _b * SS:(_b + 1) * SS],
                                         t_bx[:, 0:SS],
                                         t_xfull[:, _b * (SS + 1) + 1:
                                                 (_b + 1) * (SS + 1)],
                                         ALU.add), inc=b.vv)
                            built = b.vv.val
                            b.wait("sync", b.vv, built)
                            spill_sd[si % 2] = b.dma(
                                "sync", sp_xs.ap()[SF[s], st], sbuf[:, :])
                            b.wait("tensor", b.vv, built)
                        else:
                            b.wait("sync", b.tt, proj_tt.get(si % 2, 0))
                            spill_sd[si % 2] = b.dma(
                                "sync", sbuf[:, :], sp_xs.ap()[SF[s], st])
                            b.wait("tensor", b.sd, spill_sd[si % 2])

                        b.wait("tensor", b.sgd, wq_loaded)
                        if s == "w":
                            wbank(pa_free)
                            for kc in range(NB):
                                b.op("tensor", lambda e, _kc=kc, _sb=sbuf:
                                     e.matmul(
                                         pA[0:c.DD, 0:SS],
                                         t_dw1[:, _kc * c.DD:(_kc + 1) * c.DD],
                                         _sb[:, _kc * SS:(_kc + 1) * SS],
                                         start=(_kc == 0), stop=(_kc == NB - 1)),
                                     inc=(b.tt if kc == NB - 1 else None))
                            b.wait("scalar", b.tt)
                            b.op("scalar", lambda e: e.activation(
                                t_td[:, :], pA[0:c.DD, 0:SS], AF.Tanh),
                                inc=b.sa)
                            pa_free["sa"] = b.sa.val
                            b.wait("tensor", b.sa)
                            if dbg:
                                b.wait("sync", b.sa)
                                b.dma("sync", d_td.ap()[:, tok0:tok0 + SS],
                                      t_td[:, :])
                            b.wait("scalar", b.vv, pd_free[par])
                            for ob in range(PBLK):
                                wpp = pB if ob % 2 == 0 else pA
                                wbk = pb_free if ob % 2 == 0 else pa_free
                                wbank(wbk)
                                b.op("tensor", lambda e, _ob=ob, _q0=qc0,
                                     _pp=wpp: e.matmul(
                                         _pp[:, 0:SS],
                                         t_dw2[:, _q0 + _ob * 128:
                                               _q0 + (_ob + 1) * 128],
                                         t_td[:, :], start=True, stop=True),
                                     inc=b.tt)
                                b.wait("scalar", b.tt)
                                b.op("scalar", lambda e, _ob=ob, _q=q,
                                     _pp=wpp: e.activation(
                                         t_pd[:, _ob * SS:(_ob + 1) * SS],
                                         _pp[:, 0:SS], AF.Exp,
                                         bias=t_dec[:, _q * PBLK + _ob:
                                                    _q * PBLK + _ob + 1]),
                                     inc=b.sa)
                                wbk["sa"] = b.sa.val
                        else:
                            for ob in range(PBLK):
                                pp = pA if ob % 2 == 0 else pB
                                pbk = pa_free if ob % 2 == 0 else pb_free
                                wbank(pbk)
                                for kc in range(NB):
                                    b.op("tensor", lambda e, _s=s, _ob=ob,
                                         _kc=kc, _pp=pp, _sb=sbuf: e.matmul(
                                             _pp[:, 0:SS],
                                             t_wq[_s][:, _kc * QCH + _ob * 128:
                                                      _kc * QCH + (_ob + 1) * 128],
                                             _sb[:, _kc * SS:(_kc + 1) * SS],
                                             start=(_kc == 0),
                                             stop=(_kc == NB - 1)),
                                         inc=(b.tt if kc == NB - 1 else None))
                                if s == "g":
                                    if ob == 0:
                                        b.wait("scalar", b.sd, gq_sd[par])
                                    b.wait("scalar", b.tt)
                                    b.op("scalar", lambda e, _ob=ob, _pp=pp,
                                         _gq=t_gq: e.activation(
                                             _gq[:, _ob * SS:(_ob + 1) * SS],
                                             _pp[:, 0:SS], AF.Silu), inc=b.sa)
                                    pbk["sa"] = b.sa.val
                                else:
                                    b.wait("vector", b.tt)
                                    b.op("vector", lambda e, _d=rkvg_dst[s],
                                         _ob=ob, _pp=pp: e.tensor_copy(
                                             _d[:, _ob * SS:(_ob + 1) * SS],
                                             _pp[:, 0:SS]), inc=b.vv)
                                    pbk["vv"] = b.vv.val
                        proj_tt[si % 2] = b.tt.val
                        if si == len(STREAMS) - 1:
                            last_proj[0] = b.tt.val
                    b.wait("sync", b.sa)
                    gq_sd[par] = b.dma(
                        "sync", sp_g.ap().rearrange("(b p) t -> p b t", p=128)
                        [:, q * PBLK:(q + 1) * PBLK, tok0:tok0 + SS],
                        t_gq[:, :])

                    if dbg:
                        b.wait("sync", b.vv)
                        b.wait("sync", b.sa)
                        for _nm, _t in (("d_r", t_r), ("d_k", t_k),
                                        ("d_v", t_v), ("d_pd", t_pd)):
                            b.dma("sync", locals()["_t"] if False else
                                  dict(d_r=d_r, d_k=d_k, d_v=d_v,
                                       d_pd=d_pd)[_nm].ap()
                                  .rearrange("(b p) t -> p b t", p=128)
                                  [:, q * PBLK:(q + 1) * PBLK,
                                   tok0:tok0 + SS], _t[:, :])

            def emit_pre_bb(q, st, par, bb):
                tok0 = st * SS
                gb = q * PBLK + bb
                t_pd = t_pdb[par]
                t_r, t_k = t_rb[par], t_kb[par]
                t_carry = t_carryb[q % 2]
                t_rt, t_kt = t_rtb[bb], t_ktb[bb]
                t_kh, t_rk = t_khb[bb], t_rkb[bb]
                t_rhat, t_plv = t_rhatb[bb], t_plvb[bb]
                b.wait("vector", b.tt, post_tt[bb])
                if True:
                    if True:
                        b.op("vector", lambda e, _b=bb: e.tensor_tensor_scan(
                            t_cs[:, :], t_pd[:, _b * SS:(_b + 1) * SS],
                            t_zeroSS[:, :], 0.0, ALU.add, ALU.add), inc=b.vv)
                        b.vsync()
                        b.op("vector", lambda e: e.memset(t_bvec[:, 0:1], 0.0),
                             inc=b.vv)
                        if NCHK > 1:
                            b.op("vector", lambda e: e.tensor_copy(
                                t_bvec[:, 1:NCHK],
                                blk3(t_cs[:, :], NCHK, L)
                                [:, 0:NCHK - 1, L - 1:L].squeeze(2)), inc=b.vv)
                        b.vsync()
                        b.op("vector", lambda e: e.tensor_tensor(
                            blk3(t_csr[:, :], NCHK, L),
                            blk3(t_cs[:, :], NCHK, L),
                            t_bvec[:, :].unsqueeze(2)
                            .broadcast_to([128, NCHK, L]), ALU.subtract),
                            inc=b.vv)
                        b.op("vector", lambda e, _b=bb: e.tensor_tensor(
                            t_csx[:, :], t_csr[:, :],
                            t_pd[:, _b * SS:(_b + 1) * SS], ALU.subtract),
                            inc=b.vv)
                        if dbg:
                            b.wait("sync", b.vv)
                            b.dma("sync", d_cs.ap()
                                  .rearrange("(b p) t -> p b t", p=128)
                                  [:, gb, tok0:tok0 + SS], t_cs[:, :])
                            b.dma("sync", d_csr.ap()
                                  .rearrange("(b p) t -> p b t", p=128)
                                  [:, gb, tok0:tok0 + SS], t_csr[:, :])
                            b.dma("sync", d_bv.ap()
                                  .rearrange("(b p) t -> p b t", p=128)
                                  [:, gb, tok0 // c.L:tok0 // c.L + NCHK],
                                  t_bvec[:, :])
                        b.op("vector", lambda e: e.tensor_copy(
                            t_evec[:, :], blk3(t_csr[:, :], NCHK, L)
                            [:, :, L - 1:L].squeeze(2)), inc=b.vv)
                        b.op("vector", lambda e, _b=bb: e.tensor_scalar(
                            t_e5v[:, :], t_bvec[:, :],
                            t_carry[:, _b:_b + 1], None, ALU.add), inc=b.vv)
                        b.op("vector", lambda e, _b=bb: e.tensor_tensor(
                            t_carry[:, _b:_b + 1], t_carry[:, _b:_b + 1],
                            t_cs[:, SS - 1:SS], ALU.add), inc=b.vv)
                        pre_exp = b.vv.val
                        b.wait("scalar", b.vv, pre_exp)
                        b.op("scalar", lambda e: e.activation(
                            t_e1[:, :], t_csx[:, :], AF.Exp, scale=-1.0),
                            inc=b.sa)
                        b.op("scalar", lambda e: e.activation(
                            t_csx[:, :], t_csr[:, :], AF.Exp), inc=b.sa)
                        b.op("scalar", lambda e: e.activation(
                            t_plv[:, :], t_evec[:, :], AF.Exp, scale=-1.0),
                            inc=b.sa)
                        b.op("scalar", lambda e: e.activation(
                            t_e5v[:, :], t_e5v[:, :], AF.Exp, scale=-1.0),
                            inc=b.sa)
                        exps1 = b.sa.val
                        b.wait("vector", b.sa, exps1)
                        b.vsync()
                        b.op("vector", lambda e: e.tensor_tensor(
                            blk3(t_csr[:, :], NCHK, L),
                            blk3(t_csr[:, :], NCHK, L),
                            t_evec[:, :].unsqueeze(2)
                            .broadcast_to([128, NCHK, L]), ALU.subtract),
                            inc=b.vv)
                        b.wait("scalar", b.vv)
                        b.op("scalar", lambda e: e.activation(
                            t_csr[:, :], t_csr[:, :], AF.Exp), inc=b.sa)
                        e3_done = b.sa.val
                        if dbg:
                            b.wait("sync", b.sa, e3_done)
                            b.dma("sync", d_e1.ap()
                                  .rearrange("(b p) t -> p b t", p=128)
                                  [:, gb, tok0:tok0 + SS], t_e1[:, :])
                            b.dma("sync", d_e5.ap()
                                  .rearrange("(b p) t -> p b t", p=128)
                                  [:, gb, tok0 // c.L:tok0 // c.L + NCHK],
                                  t_e5v[:, :])
                        b.wait("vector", b.sa, e3_done)
                        b.op("vector", lambda e, _b=bb: e.tensor_tensor(
                            t_rt[:, :], t_r[:, _b * SS:(_b + 1) * SS],
                            t_e1[:, :], ALU.mult), inc=b.vv)
                        b.op("vector", lambda e, _b=bb: e.tensor_tensor(
                            t_kt[:, :], t_k[:, _b * SS:(_b + 1) * SS],
                            t_csx[:, :], ALU.mult), inc=b.vv)
                        b.op("vector", lambda e, _b=bb: e.tensor_tensor(
                            t_kh[:, :], t_k[:, _b * SS:(_b + 1) * SS],
                            t_csr[:, :], ALU.mult), inc=b.vv)
                        b.op("vector", lambda e, _b=bb: e.tensor_tensor(
                            t_rk[:, :], t_r[:, _b * SS:(_b + 1) * SS],
                            t_k[:, _b * SS:(_b + 1) * SS], ALU.mult), inc=b.vv)
                        b.op("vector", lambda e: e.tensor_tensor(
                            blk3(t_rhat[:, :], NCHK, L),
                            blk3(t_rt[:, :], NCHK, L),
                            t_e5v[:, :].unsqueeze(2)
                            .broadcast_to([128, NCHK, L]), ALU.mult), inc=b.vv)
                        prods = b.vv.val
                        prods_bb[bb] = prods
                        pd_free[par] = prods
                        if dbg:
                            b.wait("sync", b.vv, prods)
                            b.dma("sync", d_rt.ap()
                                  .rearrange("(b p) t -> p b t", p=128)
                                  [:, gb, tok0:tok0 + SS], t_rt[:, :])
                        b.wait("sync", b.vv, prods)
                        b.dma("sync", sp_rh.ap()
                              .rearrange("(b p) t -> p b t", p=128)
                              [:, gb, tok0:tok0 + SS], t_rhat[:, :])

            def emit_post_bb(q, st, par, bb):
                tok0 = st * SS
                gb = q * PBLK + bb
                t_v = t_vb[par]
                t_rt, t_kt = t_rtb[bb], t_ktb[bb]
                t_kh, t_rk = t_khb[bb], t_rkb[bb]
                t_plv = t_plvb[bb]
                t_yo = t_yob[bb]
                prods = prods_bb[bb]
                if True:
                    if True:
                        b.wait("tensor", b.vv, prods)
                        b.wait("tensor", b.vv, post_vv[0])
                        b.op("tensor", lambda e, _gb=gb: e.matmul(
                            pS1[0:1, 0:SS], t_u[0:64, _gb:_gb + 1],
                            t_rk[0:64, :], start=True, stop=True), inc=None)
                        b.op("tensor", lambda e, _gb=gb: e.matmul(
                            pS2[0:1, 0:SS], t_u[64:128, _gb:_gb + 1],
                            t_rk[64:128, :], start=True, stop=True,
                            tile_position=(64, 0)), inc=None)
                        for ch in range(NCHK):
                            c0 = ch * L
                            b.op("tensor", lambda e, _c0=c0: e.matmul(
                                pAT[0:64, _c0:_c0 + L],
                                t_kt[0:64, _c0:_c0 + L],
                                t_rt[0:64, _c0:_c0 + L],
                                start=True, stop=True), inc=None)
                            b.op("tensor", lambda e, _c0=c0: e.matmul(
                                pAT[64:128, _c0:_c0 + L],
                                t_kt[64:128, _c0:_c0 + L],
                                t_rt[64:128, _c0:_c0 + L], start=True,
                                stop=True, tile_position=(64, 64)),
                                inc=(b.tt if ch == NCHK - 1 else None))
                        at_done = b.tt.val
                        b.wait("vector", b.tt, at_done)
                        b.op("vector", lambda e: e.tensor_tensor(
                            blk3(t_amt[:, :], NCHK, L),
                            blk3(pAT[:, 0:SS], NCHK, L),
                            t_maskU[:, :].unsqueeze(1)
                            .broadcast_to([128, NCHK, L]), ALU.mult), inc=b.vv)
                        b.op("vector", lambda e: e.tensor_copy(
                            t_sdrow[0:1, 0:SS], pS1[0:1, 0:SS]), inc=b.vv)
                        b.op("vector", lambda e: e.tensor_copy(
                            t_sdrow[0:1, SS:2 * SS], pS2[0:1, 0:SS]), inc=b.vv)
                        amt_done = b.vv.val
                        b.wait("tensor", b.vv, amt_done)
                        b.op("tensor", lambda e: e.matmul(
                            pAT[0:64, 0:SS], t_ones1f[0:1, 0:64],
                            t_sdrow[0:1, 0:SS], start=True, stop=True),
                            inc=None)
                        b.op("tensor", lambda e: e.matmul(
                            pAT[64:128, 0:SS], t_ones1f[0:1, 0:64],
                            t_sdrow[0:1, SS:2 * SS], start=True, stop=True,
                            tile_position=(0, 64)), inc=b.tt)
                        sdbc = b.tt.val
                        b.wait("vector", b.tt, sdbc)
                        b.op("vector", lambda e, _b=bb: e.tensor_tensor(
                            t_dv[:, :], pAT[:, 0:SS],
                            t_v[:, _b * SS:(_b + 1) * SS], ALU.mult),
                            inc=b.vv)
                        dv_done = b.vv.val

                        for ch in range(NCHK):
                            c0 = ch * L
                            b.op("vector", lambda e, _gb=gb: e.tensor_copy(
                                t_Sbf[:, :], t_S[:, _gb * L:(_gb + 1) * L]),
                                inc=b.vv)
                            scast = b.vv.val
                            b.wait("tensor", b.vv, scast)
                            b.op("tensor", lambda e, _c0=c0, _b=bb: e.transpose(
                                pTP[0:64, 0:128],
                                t_v[:, _b * SS + _c0:_b * SS + _c0 + L],
                                t_ident[:, :]), inc=None)
                            b.op("tensor", lambda e, _c0=c0, _b=bb: e.transpose(
                                pTP[64:128, 0:128],
                                t_v[:, _b * SS + _c0:_b * SS + _c0 + L],
                                t_ident[:, :], tile_position=(0, 64)),
                                inc=None)
                            b.op("tensor", lambda e, _c0=c0: e.transpose(
                                pTP[0:64, 128:256], t_kh[:, _c0:_c0 + L],
                                t_ident[:, :]), inc=None)
                            b.op("tensor", lambda e, _c0=c0: e.transpose(
                                pTP[64:128, 128:256], t_kh[:, _c0:_c0 + L],
                                t_ident[:, :], tile_position=(0, 64)),
                                inc=b.tt)
                            tp_done = b.tt.val
                            b.wait("vector", b.tt, tp_done)
                            b.op("vector", lambda e: e.tensor_copy(
                                t_vtok[:, :], pTP[:, 0:128]), inc=b.vv)
                            b.op("vector", lambda e: e.tensor_copy(
                                t_ktok[:, :], pTP[:, 128:256]), inc=b.vv)
                            tok_done = b.vv.val
                            b.wait("tensor", b.vv, tok_done)
                            b.op("tensor", lambda e, _c0=c0: e.matmul(
                                pY[0:64, _c0:_c0 + L], t_vtok[0:64, 0:64],
                                t_amt[0:64, _c0:_c0 + L],
                                start=True, stop=False), inc=None)
                            b.op("tensor", lambda e, _c0=c0: e.matmul(
                                pY[0:64, _c0:_c0 + L], t_Sbf[0:64, :],
                                t_rt[0:64, _c0:_c0 + L],
                                start=False, stop=True), inc=None)
                            b.op("tensor", lambda e, _c0=c0: e.matmul(
                                pY[64:128, _c0:_c0 + L],
                                t_vtok[64:128, 64:128],
                                t_amt[64:128, _c0:_c0 + L], start=True,
                                stop=False, tile_position=(64, 64)), inc=None)
                            b.op("tensor", lambda e, _c0=c0: e.matmul(
                                pY[64:128, _c0:_c0 + L], t_Sbf[64:128, :],
                                t_rt[64:128, _c0:_c0 + L], start=False,
                                stop=True, tile_position=(64, 64)), inc=None)
                            b.op("tensor", lambda e, _c0=c0: e.matmul(
                                pU[0:64, _c0:_c0 + L], t_ktok[0:64, 0:64],
                                t_vtok[0:64, 0:64], start=True, stop=True),
                                inc=None)
                            b.op("tensor", lambda e, _c0=c0: e.matmul(
                                pU[64:128, _c0:_c0 + L],
                                t_ktok[64:128, 64:128],
                                t_vtok[64:128, 64:128], start=True, stop=True,
                                tile_position=(64, 64)), inc=b.tt)
                            yu_done = b.tt.val
                            b.wait("vector", b.tt, yu_done)
                            b.op("vector", lambda e, _gb=gb, _ch=ch:
                                 e.tensor_scalar(
                                     t_S[:, _gb * L:(_gb + 1) * L],
                                     t_S[:, _gb * L:(_gb + 1) * L],
                                     t_plv[:, _ch:_ch + 1], None, ALU.mult),
                                 inc=b.vv)
                            b.op("vector", lambda e, _gb=gb, _c0=c0:
                                 e.tensor_tensor(
                                     t_S[:, _gb * L:(_gb + 1) * L],
                                     t_S[:, _gb * L:(_gb + 1) * L],
                                     pU[:, _c0:_c0 + L], ALU.add), inc=b.vv)
                        b.wait("vector", b.tt)
                        b.op("vector", lambda e: e.tensor_tensor(
                            t_yo[:, :], pY[:, 0:SS], t_dv[:, :], ALU.add),
                            inc=b.vv)
                        yo_done = b.vv.val
                        b.wait("sync", b.vv, yo_done)
                        b.dma("sync", sp_y.ap()
                              .rearrange("(b p) t -> p b t", p=128)
                              [:, gb, tok0:tok0 + SS], t_yo[:, :])
                post_vv[0] = b.vv.val
                post_tt[bb] = b.tt.val

            prev = None
            for q in range(NPASS):
                emit_prologue(q)
                for st in range(NST):
                    par = (q * NST + st) % 2
                    if prev is not None:
                        for bb in range(PBLK):
                            emit_pre_bb(prev[0], prev[1], prev[2], bb)
                    emit_proj(q, st, par)
                    if prev is not None:
                        for bb in range(PBLK):
                            emit_post_bb(prev[0], prev[1], prev[2], bb)
                    prev = (q, st, par)
            for bb in range(PBLK):
                emit_pre_bb(prev[0], prev[1], prev[2], bb)
            for bb in range(PBLK):
                emit_post_bb(prev[0], prev[1], prev[2], bb)

            b.barrier()
            b.dma("gpsimd", cc_in.ap()[:, :], t_S[:, :])
            b.wait("gpsimd", b.sgd)
            if stop_after == "NOCC":
                b.dma("gpsimd", cc_out.ap()[0:128, :], cc_in.ap()[:, :])
                b.dma("gpsimd", cc_out.ap()[128:256, :], cc_in.ap()[:, :])
            else:
                b.op("gpsimd", lambda e: e.collective_compute(
                    "AllGather", ALU.bypass,
                    replica_groups=[[2 * i, 2 * i + 1]
                                    for i in range(c.NCORE // 2)],
                    ins=[cc_in.ap().opt()], outs=[cc_out.ap().opt()]),
                    inc=b.scc)
                b.wait("gpsimd", b.scc)
            b.dma("gpsimd", t_S[:, :], cc_out.ap()[0:128, :])
            b.wait("vector", b.sgd)
            b.op("vector", lambda e: e.tensor_scalar(
                t_S[:, :], t_S[:, :], t_cmask[:, 0:1], None, ALU.mult),
                inc=b.vv)
            b.op("vector", lambda e: e.tensor_copy(t_Sin[:, :], t_S[:, :]),
                 inc=b.vv)
            b.barrier()

        if stop_after in ("P1", "P2", "NOCC"):
            b.wait("sync", b.sd)
            b.wait("sync", b.sgd)
            _finish(b, block)
            return nc
        with contextlib.ExitStack() as es3:
            sb3 = lambda n, s, d=F32: es3.enter_context(nc.sbuf_tensor(n, s, d))
            TS3 = SS // 2
            NST3 = TC // TS3
            t_wo = sb3("t_wo", [128, NB * D], BF16)
            t_ytile = sb3("t_ytile", [128, NB * TS3])
            t_gtile = sb3("t_gtile", [128, NB * TS3], BF16)
            t_rhtile = sb3("t_rhtile", [128, NB * TS3], BF16)
            t_yng = sb3("t_yng", [128, NB * TS3], BF16)
            t_xf3 = sb3("t_xf3", [128, NB * TS3])
            t_sq = [sb3("t_sq%d" % i, [128, TS3]) for i in range(2)]
            t_mu = sb3("t_mu", [128, TS3])
            t_mu2 = sb3("t_mu2", [128, TS3])
            t_rstd = sb3("t_rstd", [128, TS3])
            t_row1 = sb3("t_row1", [1, TS3])
            t_row2 = sb3("t_row2", [1, TS3])
            t_strip1 = sb3("t_strip1", [2, TS3])
            t_strip2 = sb3("t_strip2", [2, TS3])

            b.dma("gpsimd", t_wo[:, :], WoS[:, :])
            wo_loaded = b.sgd.val

            for st in range(NST3):
                tok0 = st * TS3
                b.wait("sync", b.vv)
                b.wait("sync", b.tt)
                b.dma("sync", t_ytile[:, :],
                      sp_y.ap().rearrange("(b p) t -> p b t", p=128)
                      [:, :, tok0:tok0 + TS3])
                b.dma("sync", t_rhtile[:, :],
                      sp_rh.ap().rearrange("(b p) t -> p b t", p=128)
                      [:, :, tok0:tok0 + TS3])
                b.dma("sync", t_gtile[:, :],
                      sp_g.ap().rearrange("(b p) t -> p b t", p=128)
                      [:, :, tok0:tok0 + TS3])
                b.dma("sync", t_xf3[:, :],
                      xT.rearrange("(b p) t -> p b t", p=128)
                      [:, :, tok0 + 1:tok0 + TS3 + 1])
                ld_done = b.sd.val

                for gb in range(NB):
                    b.wait("tensor", b.sd, ld_done)
                    b.wait("tensor", b.vv)
                    b.wait("tensor", b.sa)
                    b.op("tensor", lambda e, _gb=gb: e.matmul(
                        pAT[0:64, 0:TS3],
                        t_Sin[0:64, _gb * L:(_gb + 1) * L],
                        t_rhtile[0:64, _gb * TS3:(_gb + 1) * TS3],
                        start=True, stop=True), inc=None)
                    b.op("tensor", lambda e, _gb=gb: e.matmul(
                        pAT[64:128, 0:TS3],
                        t_Sin[64:128, _gb * L:(_gb + 1) * L],
                        t_rhtile[64:128, _gb * TS3:(_gb + 1) * TS3],
                        start=True, stop=True, tile_position=(64, 64)),
                        inc=b.tt)
                    corr_done = b.tt.val
                    b.wait("vector", b.tt, corr_done)
                    b.op("vector", lambda e, _gb=gb: e.tensor_tensor(
                        t_ytile[:, _gb * TS3:(_gb + 1) * TS3],
                        t_ytile[:, _gb * TS3:(_gb + 1) * TS3],
                        pAT[:, 0:TS3], ALU.add), inc=b.vv)
                    yf_done = b.vv.val
                    b.wait("scalar", b.vv, yf_done)
                    b.op("scalar", lambda e, _gb=gb: e.activation(
                        t_sq[0][:, :], t_ytile[:, _gb * TS3:(_gb + 1) * TS3],
                        AF.Square), inc=b.sa)
                    sq_done = b.sa.val
                    b.wait("tensor", b.vv, yf_done)
                    b.wait("tensor", b.sa, sq_done)
                    b.op("tensor", lambda e, _gb=gb: e.matmul(
                        pS1[0:2, 0:TS3], t_hones[:, :],
                        t_ytile[:, _gb * TS3:(_gb + 1) * TS3],
                        start=True, stop=True), inc=None)
                    b.op("tensor", lambda e: e.matmul(
                        pS2[0:2, 0:TS3], t_hones[:, :], t_sq[0][:, :],
                        start=True, stop=True), inc=b.tt)
                    stats_done = b.tt.val
                    b.wait("vector", b.tt, stats_done)
                    b.op("vector", lambda e: e.tensor_copy(
                        t_strip1[:, :], pS1[0:2, 0:TS3]), inc=b.vv)
                    b.op("vector", lambda e: e.tensor_copy(
                        t_strip2[:, :], pS2[0:2, 0:TS3]), inc=b.vv)
                    strips = b.vv.val
                    b.wait("tensor", b.vv, strips)
                    b.op("tensor", lambda e: e.matmul(
                        pS1[:, 0:TS3], t_hpat[:, :], t_strip1[:, :],
                        start=True, stop=True), inc=None)
                    b.op("tensor", lambda e: e.matmul(
                        pS2[:, 0:TS3], t_hpat[:, :], t_strip2[:, :],
                        start=True, stop=True), inc=b.tt)
                    bc_done = b.tt.val
                    b.wait("scalar", b.tt, bc_done)
                    b.op("scalar", lambda e: e.activation(
                        t_mu[:, :], pS1[:, 0:TS3], AF.Copy,
                        scale=1.0 / 64.0), inc=b.sa)
                    mu_done = b.sa.val
                    b.wait("vector", b.sa, mu_done)
                    b.op("vector", lambda e: e.tensor_tensor(
                        t_mu2[:, :], t_mu[:, :], t_mu[:, :], ALU.mult),
                        inc=b.vv)
                    b.op("vector", lambda e: e.scalar_tensor_tensor(
                        t_rstd[:, :], pS2[:, 0:TS3], 1.0 / 64.0, t_mu2[:, :],
                        ALU.mult, ALU.subtract), inc=b.vv)
                    q_done = b.vv.val
                    b.wait("scalar", b.vv, q_done)
                    b.op("scalar", lambda e: e.activation(
                        t_rstd[:, :], t_rstd[:, :], AF.Ln,
                        bias=t_eps[:, 0:1]), inc=b.sa)
                    b.op("scalar", lambda e: e.activation(
                        t_rstd[:, :], t_rstd[:, :], AF.Exp, scale=-0.5),
                        inc=b.sa)
                    rstd_done = b.sa.val
                    b.wait("vector", b.sa, rstd_done)
                    b.op("vector", lambda e, _gb=gb: e.tensor_tensor(
                        t_mu2[:, :], t_ytile[:, _gb * TS3:(_gb + 1) * TS3],
                        t_mu[:, :], ALU.subtract), inc=b.vv)
                    b.op("vector", lambda e: e.tensor_tensor(
                        t_mu2[:, :], t_mu2[:, :], t_rstd[:, :], ALU.mult),
                        inc=b.vv)
                    b.op("vector", lambda e, _gb=gb: e.tensor_scalar(
                        t_mu2[:, :], t_mu2[:, :], t_gam[:, _gb:_gb + 1],
                        t_bet[:, _gb:_gb + 1], ALU.mult, ALU.add), inc=b.vv)
                    b.op("vector", lambda e, _gb=gb: e.tensor_tensor(
                        t_yng[:, _gb * TS3:(_gb + 1) * TS3], t_mu2[:, :],
                        t_gtile[:, _gb * TS3:(_gb + 1) * TS3], ALU.mult),
                        inc=b.vv)
                yng_done = b.vv.val

                b.wait("tensor", b.sgd, wo_loaded)
                b.wait("tensor", b.vv, yng_done)
                for ob in range(NB):
                    pp = pA if ob % 2 == 0 else pB
                    for kc in range(NB):
                        b.op("tensor", lambda e, _ob=ob, _kc=kc, _pp=pp:
                             e.matmul(
                                 _pp[:, 0:TS3],
                                 t_wo[:, _kc * D + _ob * 128:
                                      _kc * D + (_ob + 1) * 128],
                                 t_yng[:, _kc * TS3:(_kc + 1) * TS3],
                                 start=(_kc == 0), stop=(_kc == NB - 1)),
                             inc=(b.tt if kc == NB - 1 else None))
                    b.wait("vector", b.tt)
                    b.op("vector", lambda e, _ob=ob, _pp=pp:
                         e.tensor_tensor(
                             t_ytile[:, _ob * TS3:(_ob + 1) * TS3],
                             _pp[:, 0:TS3],
                             t_xf3[:, _ob * TS3:(_ob + 1) * TS3], ALU.add),
                         inc=b.vv)
                    b.wait("tensor", b.vv)
                h_done = b.vv.val

                sq_tt = [0, 0]
                for bb2 in range(NB):
                    b.wait("scalar", b.vv, h_done)
                    b.wait("scalar", b.tt, sq_tt[bb2 % 2])
                    b.op("scalar", lambda e, _b=bb2: e.activation(
                        t_sq[_b % 2][:, :],
                        t_ytile[:, _b * TS3:(_b + 1) * TS3], AF.Square),
                        inc=b.sa)
                    b.wait("tensor", b.sa)
                    b.op("tensor", lambda e, _b=bb2: e.matmul(
                        pS1[0:1, 0:TS3], t_onescol[:, :],
                        t_ytile[:, _b * TS3:(_b + 1) * TS3],
                        start=(_b == 0), stop=(_b == NB - 1)), inc=None)
                    b.op("tensor", lambda e, _b=bb2: e.matmul(
                        pS2[0:1, 0:TS3], t_onescol[:, :], t_sq[_b % 2][:, :],
                        start=(_b == 0), stop=(_b == NB - 1)), inc=b.tt)
                    sq_tt[bb2 % 2] = b.tt.val
                lnstats = b.tt.val
                b.wait("vector", b.tt, lnstats)
                b.op("vector", lambda e: e.tensor_copy(
                    t_row1[:, :], pS1[0:1, 0:TS3]), inc=b.vv)
                b.op("vector", lambda e: e.tensor_copy(
                    t_row2[:, :], pS2[0:1, 0:TS3]), inc=b.vv)
                rows = b.vv.val
                b.wait("tensor", b.vv, rows)
                b.op("tensor", lambda e: e.matmul(
                    pS1[:, 0:TS3], t_ones1f[0:1, :], t_row1[:, :],
                    start=True, stop=True), inc=None)
                b.op("tensor", lambda e: e.matmul(
                    pS2[:, 0:TS3], t_ones1f[0:1, :], t_row2[:, :],
                    start=True, stop=True), inc=b.tt)
                lnbc = b.tt.val
                b.wait("scalar", b.tt, lnbc)
                b.op("scalar", lambda e: e.activation(
                    t_mu[:, :], pS1[:, 0:TS3], AF.Copy, scale=1.0 / D),
                    inc=b.sa)
                lnmu = b.sa.val
                b.wait("vector", b.sa, lnmu)
                b.op("vector", lambda e: e.tensor_tensor(
                    t_mu2[:, :], t_mu[:, :], t_mu[:, :], ALU.mult), inc=b.vv)
                b.op("vector", lambda e: e.scalar_tensor_tensor(
                    t_rstd[:, :], pS2[:, 0:TS3], 1.0 / D, t_mu2[:, :],
                    ALU.mult, ALU.subtract), inc=b.vv)
                lnq = b.vv.val
                b.wait("scalar", b.vv, lnq)
                b.op("scalar", lambda e: e.activation(
                    t_rstd[:, :], t_rstd[:, :], AF.Ln,
                    bias=t_eps[:, 1:2]), inc=b.sa)
                b.op("scalar", lambda e: e.activation(
                    t_rstd[:, :], t_rstd[:, :], AF.Exp, scale=-0.5), inc=b.sa)
                lnr = b.sa.val
                b.wait("vector", b.sa, lnr)
                for bb2 in range(NB):
                    b.op("vector", lambda e, _b=bb2: e.tensor_tensor(
                        t_ytile[:, _b * TS3:(_b + 1) * TS3],
                        t_ytile[:, _b * TS3:(_b + 1) * TS3], t_mu[:, :],
                        ALU.subtract), inc=b.vv)
                    b.op("vector", lambda e, _b=bb2: e.tensor_tensor(
                        t_ytile[:, _b * TS3:(_b + 1) * TS3],
                        t_ytile[:, _b * TS3:(_b + 1) * TS3], t_rstd[:, :],
                        ALU.mult), inc=b.vv)
                hln = b.vv.val
                b.wait("sync", b.vv, hln)
                b.dma("sync", sp_h.ap().rearrange("(b p) t -> p b t", p=128)
                      [:, :, tok0:tok0 + TS3], t_ytile[:, :])
            b.barrier()

        if stop_after == "P3":
            b.wait("sync", b.sd)
            b.wait("sync", b.sgd)
            _finish(b, block)
            return nc
        with contextlib.ExitStack() as es4:
            sb4 = lambda n, s, d=F32: es4.enter_context(nc.sbuf_tensor(n, s, d))
            t_acc = sb4("t_acc", [128, NB * GT])
            t_hb = sb4("t_hb", [128, NB * GT], BF16)
            t_gate = sb4("t_gate", [128, NB * NEXP], BF16)
            t_w13 = [sb4("t_w13_%d" % i, [128, NB * 128], BF16)
                     for i in range(4)]
            t_w2ob = [sb4("t_w2ob%d" % i, [128, IB * 128], BF16)
                      for i in range(2)]
            t_p = sb4("t_p", [128, NSLAB * IB * SS], BF16)
            t_q0 = sb4("t_q0", [128, SS], BF16)
            t_sc16 = sb4("t_sc16", [NEXP, SS])
            t_wtok = sb4("t_wtok", [128, (SS // 128) * NEXP])
            t_wem2 = sb4("t_wem2", [NEXP, NSLAB * SS])
            t_wstage = sb4("t_wstage", [1, SS])
            t_webc = [sb4("t_webc%d" % i, [128, SS]) for i in range(NSLAB)]
            t_gneg = sb4("t_gneg", [128, 1])
            t_grec = sb4("t_grec", [128, 1])
            t_gm1 = sb4("t_gm1", [128, 1])
            t_gm2 = sb4("t_gm2", [128, 1])
            t_gsum = sb4("t_gsum", [128, 1])
            t_geq = sb4("t_geq", [128, NEXP])
            t_gmk = sb4("t_gmk", [128, NEXP])
            t_gge = sb4("t_gge", [128, NEXP])
            t_wmask = sb4("t_wmask", [128, (SS // 128) * NEXP])
            q_sq = [sb4("u4_sq%d" % i, [128, SS]) for i in range(2)]
            q_mu = sb4("u4_mu", [128, SS])
            q_mu2 = sb4("u4_mu2", [128, SS])
            q_rstd = sb4("u4_rstd", [128, SS])
            q_row1 = sb4("u4_row1", [1, SS])
            q_row2 = sb4("u4_row2", [1, SS])

            b.dma("gpsimd", t_gate[:, :],
                  gateT.rearrange("(b p) e -> p b e", p=128))
            gate_loaded = b.sgd.val

            sched = [("sh", 0), ("sh", 1)] + [("ex", e) for e in range(NEXP)]
            flat = [(kind, ei, ib) for kind, ei in sched for ib in range(IB)]
            w13_tt = {0: 0, 1: 0}
            w13_sd = {}
            w2ob_tt = [0, 0]

            def issue_w13(j):
                kind, ei, ib = flat[j]
                bi = (j % 2) * 2
                b.wait("gpsimd", b.tt, w13_tt[j % 2])
                if kind == "sh":
                    b.dma("gpsimd", t_w13[bi][:, :], sw1S[ei, ib])
                    b.dma("gpsimd", t_w13[bi + 1][:, :], sw3S[ei, ib])
                else:
                    b.dma("gpsimd", t_w13[bi][:, :], w1S[ei, ib])
                    b.dma("gpsimd", t_w13[bi + 1][:, :], w3S[ei, ib])
                w13_sd[j] = b.sgd.val

            for grp in range(2):
                g0 = grp * GT
                b.wait("sync", b.vv)
                b.wait("sync", b.tt)
                b.wait("sync", b.sa)
                hload = b.dma("sync", t_acc[:, :],
                              sp_h.ap().rearrange("(b p) t -> p b t", p=128)
                              [:, :, g0:g0 + GT])
                issue_w13(0)
                issue_w13(1)
                silu_sa = [b.sa.val, b.sa.val]
                pm_vv = [b.vv.val, b.vv.val]
                acc_vv = [b.vv.val, b.vv.val]
                it_h = 0
                it_w = 0
                b.wait("scalar", b.sd, hload)
                for bb2 in range(NB):
                    b.op("scalar", lambda e, _b=bb2: e.activation(
                        t_hb[:, _b * GT:(_b + 1) * GT],
                        t_acc[:, _b * GT:(_b + 1) * GT], AF.Copy), inc=b.sa)
                hb_done = b.sa.val

                b.wait("tensor", b.sgd, gate_loaded)
                for sl in range(NSLAB):
                    ls0 = sl * SS
                    b.wait("tensor", b.sa, hb_done)
                    b.wait("tensor", b.vv)
                    for kc in range(NB):
                        b.op("tensor", lambda e, _kc=kc, _s=ls0: e.matmul(
                            pU[0:NEXP, 0:SS],
                            t_gate[:, _kc * NEXP:(_kc + 1) * NEXP],
                            t_hb[:, _kc * GT + _s:_kc * GT + _s + SS],
                            start=(_kc == 0), stop=(_kc == NB - 1)),
                            inc=(b.tt if kc == NB - 1 else None))
                    gmm = b.tt.val
                    b.wait("vector", b.tt, gmm)
                    b.op("vector", lambda e: e.tensor_copy(
                        t_sc16[:, :], pU[0:NEXP, 0:SS]), inc=b.vv)
                    scd = b.vv.val
                    b.wait("tensor", b.vv, scd)
                    for sub in range(SS // 128):
                        b.op("tensor", lambda e, _s=sub: e.transpose(
                            pU[:, _s * NEXP:(_s + 1) * NEXP],
                            t_sc16[:, _s * 128:(_s + 1) * 128],
                            t_identf[0:NEXP, 0:NEXP]),
                            inc=(b.tt if sub == SS // 128 - 1 else None))
                    gtp = b.tt.val
                    b.wait("vector", b.tt, gtp)
                    b.op("vector", lambda e: e.tensor_copy(
                        t_wtok[:, :], pU[:, 0:(SS // 128) * NEXP]), inc=b.vv)
                    b.vsync()
                    for sub in range(SS // 128):
                        sl_ap = t_wtok[:, sub * NEXP:(sub + 1) * NEXP]
                        wm_ap = t_wmask[:, sub * NEXP:(sub + 1) * NEXP]
                        b.op("vector", lambda e, _a=sl_ap: e.reduce_max(
                            t_gneg[:, :], _a, axis=AX.X, negate=True),
                            inc=b.vv)
                        b.wait("scalar", b.vv)
                        b.op("scalar", lambda e, _a=sl_ap: e.activation(
                            _a, _a, AF.Exp, bias=t_gneg[:, 0:1],
                            accum_out=t_gsum[:, 0:1]), inc=b.sa)
                        b.wait("vector", b.sa)
                        b.op("vector", lambda e: e.reciprocal(
                            t_grec[:, :], t_gsum[:, :]), inc=b.vv)
                        b.vsync()
                        b.op("vector", lambda e, _a=sl_ap: e.tensor_scalar(
                            _a, _a, t_grec[:, 0:1], None, ALU.mult), inc=b.vv)
                        b.vsync()
                        b.op("vector", lambda e, _a=sl_ap: e.reduce_max(
                            t_gm1[:, :], _a, axis=AX.X), inc=b.vv)
                        b.vsync()
                        b.op("vector", lambda e, _a=sl_ap: e.tensor_scalar(
                            t_geq[:, :], _a, t_gm1[:, 0:1], None,
                            ALU.is_equal), inc=b.vv)
                        b.vsync()
                        b.op("vector", lambda e, _a=sl_ap:
                             e.scalar_tensor_tensor(
                                 t_gmk[:, :], t_geq[:, :], -1e30, _a,
                                 ALU.mult, ALU.add), inc=b.vv)
                        b.vsync()
                        b.op("vector", lambda e: e.reduce_max(
                            t_gm2[:, :], t_gmk[:, :], axis=AX.X), inc=b.vv)
                        b.vsync()
                        b.op("vector", lambda e, _a=sl_ap: e.tensor_scalar(
                            t_gge[:, :], _a, t_gm2[:, 0:1], None,
                            ALU.is_ge), inc=b.vv)
                        b.vsync()
                        b.op("vector", lambda e, _a=sl_ap, _w=wm_ap:
                             e.tensor_tensor(
                                 _w, _a, t_gge[:, :], ALU.mult), inc=b.vv)
                    w_done = b.vv.val
                    if dbg:
                        b.wait("sync", b.vv, w_done)
                        nsub4 = SS // 128
                        b.dma("sync", d_wtok.ap()
                              [:, (grp * NSLAB + sl) * nsub4 * NEXP:
                               (grp * NSLAB + sl + 1) * nsub4 * NEXP],
                              t_wtok[:, :])
                    b.wait("tensor", b.vv, w_done)
                    for sub in range(SS // 128):
                        b.op("tensor", lambda e, _s=sub: e.transpose(
                            pU[0:NEXP, _s * 128:(_s + 1) * 128],
                            t_wmask[:, _s * NEXP:(_s + 1) * NEXP],
                            t_identf[:, :]),
                            inc=(b.tt if sub == SS // 128 - 1 else None))
                    wtb = b.tt.val
                    b.wait("vector", b.tt, wtb)
                    b.op("vector", lambda e, _sl=sl: e.tensor_copy(
                        t_wem2[:, _sl * SS:(_sl + 1) * SS],
                        pU[0:NEXP, 0:SS]), inc=b.vv)
                gate_vv = b.vv.val
                acc_vv = [gate_vv, gate_vv]

                for isched, (kind, ei) in enumerate(sched):
                    shared = kind == "sh"
                    if not shared:
                        for sl in range(NSLAB):
                            b.wait("sync", b.vv, gate_vv)
                            b.wait("sync", b.tt)
                            st_sd = b.dma(
                                "sync", t_wstage[0:1, :],
                                t_wem2[ei:ei + 1, sl * SS:(sl + 1) * SS])
                            b.wait("tensor", b.sd, st_sd)
                            b.wait("tensor", b.vv)
                            b.op("tensor", lambda e: e.matmul(
                                pS1[:, 0:SS], t_ones1f[0:1, :],
                                t_wstage[0:1, :],
                                start=True, stop=True), inc=b.tt)
                            b.wait("vector", b.tt)
                            b.op("vector", lambda e, _sl=sl: e.tensor_copy(
                                t_webc[_sl][:, :], pS1[:, 0:SS]), inc=b.vv)
                    if dbg and not shared:
                        for sl in range(NSLAB):
                            b.wait("sync", b.vv)
                            b.dma("sync",
                                  d_web.ap()[ei:ei + 1,
                                             (grp * NSLAB + sl) * SS:
                                             (grp * NSLAB + sl + 1) * SS],
                                  t_webc[sl][0:1, :])
                    webc_done = b.vv.val

                    for ib in range(IB):
                        j = isched * IB + ib
                        bi = (j % 2) * 2
                        for sl in range(NSLAB):
                            ls0 = sl * SS
                            hp = it_h % 2
                            pH1 = (pA, pY)[hp]
                            pH3 = (pB, pAT)[hp]
                            b.wait("tensor", b.sgd, w13_sd[j])
                            b.wait("tensor", b.sa,
                                   max(hb_done, silu_sa[hp]))
                            b.wait("tensor", b.vv, pm_vv[hp])
                            for kc in range(NB):
                                b.op("tensor", lambda e, _kc=kc, _s=ls0,
                                     _bi=bi, _pp=pH1: e.matmul(
                                         _pp[:, 0:SS],
                                         t_w13[_bi][:, _kc * 128:
                                                    (_kc + 1) * 128],
                                         t_hb[:, _kc * GT + _s:
                                              _kc * GT + _s + SS],
                                         start=(_kc == 0),
                                         stop=(_kc == NB - 1)),
                                     inc=(b.tt if kc == NB - 1 else None))
                            h1_tt = b.tt.val
                            b.wait("scalar", b.tt, h1_tt)
                            b.wait("scalar", b.vv)
                            b.op("scalar", lambda e, _pp=pH1: e.activation(
                                t_q0[:, :], _pp[:, 0:SS], AF.Silu), inc=b.sa)
                            silu_d = b.sa.val
                            silu_sa[hp] = silu_d
                            for kc in range(NB):
                                b.op("tensor", lambda e, _kc=kc, _s=ls0,
                                     _bi=bi, _pp=pH3: e.matmul(
                                         _pp[:, 0:SS],
                                         t_w13[_bi + 1][:, _kc * 128:
                                                        (_kc + 1) * 128],
                                         t_hb[:, _kc * GT + _s:
                                              _kc * GT + _s + SS],
                                         start=(_kc == 0),
                                         stop=(_kc == NB - 1)),
                                     inc=(b.tt if kc == NB - 1 else None))
                            h3_tt = b.tt.val
                            if not shared:
                                b.wait("gpsimd", b.sa, silu_d)
                                b.wait("gpsimd", b.vv, webc_done)
                                b.op("gpsimd", lambda e, _sl=sl:
                                     e.tensor_tensor(
                                         t_q0[:, :], t_q0[:, :],
                                         t_webc[_sl][:, :], ALU.mult),
                                     inc=b.gg)
                                b.wait("vector", b.gg)
                            else:
                                b.wait("vector", b.sa, silu_d)
                            b.wait("vector", b.tt, h3_tt)
                            b.op("vector", lambda e, _sl=sl, _ib=ib,
                                 _pp=pH3: e.tensor_tensor(
                                     t_p[:, (_sl * IB + _ib) * SS:
                                         (_sl * IB + _ib + 1) * SS],
                                     _pp[:, 0:SS], t_q0[:, :], ALU.mult),
                                 inc=b.vv)
                            pm_vv[hp] = b.vv.val
                            it_h += 1
                        w13_tt[j % 2] = b.tt.val
                        if j + 2 < len(flat):
                            issue_w13(j + 2)
                    p_done = b.vv.val

                    b.wait("tensor", b.vv, p_done)
                    for ob in range(NB):
                        wb = t_w2ob[ob % 2]
                        b.wait("gpsimd", b.tt, w2ob_tt[ob % 2])
                        if shared:
                            b.dma("gpsimd", wb[:, :], sw2S[ei, ob])
                        else:
                            b.dma("gpsimd", wb[:, :], w2S[ei, ob])
                        w2ob_sd = b.sgd.val
                        b.wait("tensor", b.sgd, w2ob_sd)
                        for sl in range(NSLAB):
                            ls0 = sl * SS
                            wp = it_w % 2
                            pp = (pU, pS2)[wp]
                            b.wait("tensor", b.vv, acc_vv[wp])
                            for ikc in range(IB):
                                b.op("tensor", lambda e, _ik=ikc,
                                     _sl=sl, _pp=pp, _wb=wb: e.matmul(
                                         _pp[:, 0:SS],
                                         _wb[:, _ik * 128:(_ik + 1) * 128],
                                         t_p[:, (_sl * IB + _ik) * SS:
                                             (_sl * IB + _ik + 1) * SS],
                                         start=(_ik == 0),
                                         stop=(_ik == IB - 1)),
                                     inc=(b.tt if ikc == IB - 1 else None))
                            y_tt = b.tt.val
                            b.wait("vector", b.tt, y_tt)
                            b.op("vector", lambda e, _ob=ob, _s=ls0, _pp=pp:
                                 e.tensor_tensor(
                                     t_acc[:, _ob * GT + _s:
                                           _ob * GT + _s + SS],
                                     t_acc[:, _ob * GT + _s:
                                           _ob * GT + _s + SS],
                                     _pp[:, 0:SS], ALU.add), inc=b.vv)
                            acc_vv[wp] = b.vv.val
                            it_w += 1
                        w2ob_tt[ob % 2] = b.tt.val

                for sl in range(NSLAB):
                    ls0 = sl * SS
                    sq_tt = [0, 0]
                    acc_done = b.vv.val
                    for bb2 in range(NB):
                        b.wait("scalar", b.vv, acc_done)
                        b.wait("scalar", b.tt, sq_tt[bb2 % 2])
                        b.op("scalar", lambda e, _b=bb2, _s=ls0: e.activation(
                            q_sq[_b % 2][:, :],
                            t_acc[:, _b * GT + _s:_b * GT + _s + SS],
                            AF.Square), inc=b.sa)
                        b.wait("tensor", b.sa)
                        b.wait("tensor", b.vv, acc_done)
                        b.op("tensor", lambda e, _b=bb2, _s=ls0: e.matmul(
                            pS1[0:1, 0:SS], t_onescol[:, :],
                            t_acc[:, _b * GT + _s:_b * GT + _s + SS],
                            start=(_b == 0), stop=(_b == NB - 1)), inc=None)
                        b.op("tensor", lambda e, _b=bb2: e.matmul(
                            pS2[0:1, 0:SS], t_onescol[:, :],
                            q_sq[_b % 2][:, :],
                            start=(_b == 0), stop=(_b == NB - 1)), inc=b.tt)
                        sq_tt[bb2 % 2] = b.tt.val
                    lnstats = b.tt.val
                    b.wait("vector", b.tt, lnstats)
                    b.op("vector", lambda e: e.tensor_copy(
                        q_row1[:, :], pS1[0:1, 0:SS]), inc=b.vv)
                    b.op("vector", lambda e: e.tensor_copy(
                        q_row2[:, :], pS2[0:1, 0:SS]), inc=b.vv)
                    rows = b.vv.val
                    b.wait("tensor", b.vv, rows)
                    b.op("tensor", lambda e: e.matmul(
                        pS1[:, 0:SS], t_ones1f[0:1, :], q_row1[:, :],
                        start=True, stop=True), inc=None)
                    b.op("tensor", lambda e: e.matmul(
                        pS2[:, 0:SS], t_ones1f[0:1, :], q_row2[:, :],
                        start=True, stop=True), inc=b.tt)
                    lnbc = b.tt.val
                    b.wait("scalar", b.tt, lnbc)
                    b.op("scalar", lambda e: e.activation(
                        q_mu[:, :], pS1[:, 0:SS], AF.Copy, scale=1.0 / D),
                        inc=b.sa)
                    lnmu = b.sa.val
                    b.wait("vector", b.sa, lnmu)
                    b.op("vector", lambda e: e.tensor_tensor(
                        q_mu2[:, :], q_mu[:, :], q_mu[:, :], ALU.mult),
                        inc=b.vv)
                    b.op("vector", lambda e: e.scalar_tensor_tensor(
                        q_rstd[:, :], pS2[:, 0:SS], 1.0 / D, q_mu2[:, :],
                        ALU.mult, ALU.subtract), inc=b.vv)
                    lnq = b.vv.val
                    b.wait("scalar", b.vv, lnq)
                    b.op("scalar", lambda e: e.activation(
                        q_rstd[:, :], q_rstd[:, :], AF.Ln,
                        bias=t_eps[:, 1:2]), inc=b.sa)
                    b.op("scalar", lambda e: e.activation(
                        q_rstd[:, :], q_rstd[:, :], AF.Exp, scale=-0.5),
                        inc=b.sa)
                    lnr = b.sa.val
                    b.wait("vector", b.sa, lnr)
                    for bb2 in range(NB):
                        b.op("vector", lambda e, _b=bb2, _s=ls0:
                             e.tensor_tensor(
                                 t_acc[:, _b * GT + _s:_b * GT + _s + SS],
                                 t_acc[:, _b * GT + _s:_b * GT + _s + SS],
                                 q_mu[:, :], ALU.subtract), inc=b.vv)
                        b.op("vector", lambda e, _b=bb2, _s=ls0:
                             e.tensor_tensor(
                                 t_acc[:, _b * GT + _s:_b * GT + _s + SS],
                                 t_acc[:, _b * GT + _s:_b * GT + _s + SS],
                                 q_rstd[:, :], ALU.mult), inc=b.vv)
                    oln = b.vv.val
                    b.wait("sync", b.vv, oln)
                    b.dma("sync",
                          outT.rearrange("(b p) t -> p b t", p=128)
                          [:, :, g0 + ls0:g0 + ls0 + SS],
                          t_acc[:, :].rearrange("p (b t) -> p b t", b=NB)
                          [:, :, ls0:ls0 + SS])

            b.wait("sync", b.sd)
            b.wait("sync", b.sgd)

        _finish(b, block)

    return nc


def _finish(b, block):
    @block.tensor
    def _(e):
        b.lower(e, "tensor")

    @block.vector
    def _(e):
        b.lower(e, "vector")

    @block.scalar
    def _(e):
        b.lower(e, "scalar")

    @block.gpsimd
    def _(e):
        b.lower(e, "gpsimd")

    @block.sync
    def _(e):
        b.lower(e, "sync")



def _prep_inputs(c, kw):
    g = lambda n: np.asarray(kw["L0_attn_" + n])
    f = lambda n: np.asarray(kw["L0_ffn_" + n])
    x = g("x")
    Bsz, T, D = x.shape
    NB = D // 128

    def vec_tile(v):
        return np.ascontiguousarray(v.reshape(-1)[: D].reshape(NB, 128).T)

    maa_list = [g("time_maa_x"), g("time_maa_w"), g("time_maa_k"),
                g("time_maa_v"), g("time_maa_r"), g("time_maa_g")]
    maa_vecs = np.concatenate([vec_tile(v) for v in maa_list], axis=1)
    maa_w2_in = g("time_maa_w2")
    maa_w2 = np.ascontiguousarray(
        maa_w2_in.transpose(1, 0, 2).reshape(c.DMIX, 5 * D))

    def tile_w_out(WTmat, qch):
        nb = WTmat.shape[0] // 128
        npass = WTmat.shape[1] // qch
        a = WTmat.reshape(nb, 128, npass, qch).transpose(2, 1, 0, 3)
        return np.ascontiguousarray(a.reshape(npass, 128, nb * qch))

    IB, NBc = c.IB, c.NB
    w1_ = f("w1"); w3_ = f("w3"); w2_ = f("w2")
    sw1_ = f("sw1"); sw3_ = f("sw3"); sw2_ = f("sw2")
    w1S = np.ascontiguousarray(
        w1_.reshape(c.NEXP, IB, 128, NBc, 128).transpose(0, 1, 4, 3, 2)
        .reshape(c.NEXP, IB, 128, D))
    w3S = np.ascontiguousarray(
        w3_.reshape(c.NEXP, IB, 128, NBc, 128).transpose(0, 1, 4, 3, 2)
        .reshape(c.NEXP, IB, 128, D))
    w2S = np.ascontiguousarray(
        w2_.reshape(c.NEXP, NBc, 128, IB, 128).transpose(0, 1, 4, 3, 2)
        .reshape(c.NEXP, NBc, 128, c.INTER))
    sw1S = np.ascontiguousarray(
        sw1_.reshape(2, IB, 128, NBc, 128).transpose(0, 1, 4, 3, 2)
        .reshape(2, IB, 128, D))
    sw3S = np.ascontiguousarray(
        sw3_.reshape(2, IB, 128, NBc, 128).transpose(0, 1, 4, 3, 2)
        .reshape(2, IB, 128, D))
    sw2S = np.ascontiguousarray(
        sw2_.reshape(NBc, 128, 2, IB, 128).transpose(2, 0, 4, 3, 1)
        .reshape(2, NBc, 128, c.INTER))
    shared = {
        "maa_vecs": maa_vecs.astype(np.float32),
        "maa_w1": np.ascontiguousarray(g("time_maa_w1")).astype(np.float32),
        "maa_w2": maa_w2.astype(np.float32),
        "decay_vec": vec_tile(g("time_decay")).astype(np.float32),
        "decay_w1": np.ascontiguousarray(g("time_decay_w1")).astype(np.float32),
        "decay_w2": np.ascontiguousarray(g("time_decay_w2")).astype(np.float32),
        "u_vec": vec_tile(g("time_faaaa")).astype(np.float32),
        "WrS": tile_w_out(np.ascontiguousarray(g("W_r").T), c.QCH).astype(np.float32),
        "WkS": tile_w_out(np.ascontiguousarray(g("W_k").T), c.QCH).astype(np.float32),
        "WvS": tile_w_out(np.ascontiguousarray(g("W_v").T), c.QCH).astype(np.float32),
        "WgS": tile_w_out(np.ascontiguousarray(g("W_g").T), c.QCH).astype(np.float32),
        "WoS": np.ascontiguousarray(
            np.ascontiguousarray(g("W_o").T).reshape(NB, 128, D)
            .transpose(1, 0, 2).reshape(128, NB * D)).astype(np.float32),
        "gam": vec_tile(g("ln_gamma")).astype(np.float32),
        "bet": vec_tile(g("ln_beta")).astype(np.float32),
        "gateT": np.ascontiguousarray(f("gate_w").T).astype(np.float32),
        "w1S": w1S.astype(np.float32),
        "w3S": w3S.astype(np.float32),
        "w2S": w2S.astype(np.float32),
        "sw1S": sw1S.astype(np.float32),
        "sw3S": sw3S.astype(np.float32),
        "sw2S": sw2S.astype(np.float32),
    }
    in_maps = []
    for core in range(c.NCORE):
        bi, half = core // 2, core % 2
        t0 = half * c.TC
        xs = x[bi, t0:t0 + c.TC].T
        halo = (np.zeros((D, 1), np.float32) if t0 == 0
                else x[bi, t0 - 1:t0].T)
        xT = np.ascontiguousarray(
            np.concatenate([halo, xs], axis=1)).astype(np.float32)
        cm = np.full((128, 1), 0.0 if half == 0 else 1.0, np.float32)
        m = dict(shared)
        m["xT"] = xT
        m["cmask"] = cm
        in_maps.append(m)
    return in_maps


_CFG = None
_NC = None


def kernel(**inputs):
    global _CFG, _NC
    c = Cfg(small=False)
    in_maps = _prep_inputs(c, inputs)
    if _NC is None:
        _NC = build_program(c)
        _CFG = c
    res = run_bass_kernel_spmd(_NC, in_maps, core_ids=list(range(c.NCORE)))
    x = np.asarray(inputs["L0_attn_x"])
    Bsz, T, D = x.shape
    out = np.empty((Bsz, T, D), np.float32)
    for core in range(c.NCORE):
        bi, half = core // 2, core % 2
        t0 = half * c.TC
        out[bi, t0:t0 + c.TC] = res.results[core]["outT"].T
    return out


# version 08 (final)
# speedup vs baseline: 1.0668x; 1.0668x over previous
import contextlib
import numpy as np
import ml_dtypes

import concourse.bass as bass
import concourse.mybir as mybir
from concourse.bass_utils import run_bass_kernel_spmd

F32 = mybir.dt.float32
BF16 = mybir.dt.bfloat16
AF = mybir.ActivationFunctionType
ALU = mybir.AluOpType
AX = mybir.AxisListType
BF = ml_dtypes.bfloat16


class Cfg:
    def __init__(self, small=False, tiny=False):
        if tiny:
            self.D, self.TC, self.SS, self.QCH = 256, 128, 128, 128
            self.NEXP, self.INTER, self.NCORE = 4, 256, 2
        elif small:
            self.D, self.TC, self.SS, self.QCH = 256, 256, 128, 128
            self.NEXP, self.INTER, self.NCORE = 4, 256, 2
        else:
            self.D, self.TC, self.SS, self.QCH = 2048, 2048, 512, 256
            self.NEXP, self.INTER, self.NCORE = 16, 1408, 8
        self.SINTER = 2 * self.INTER
        self.L = 64
        self.NB = self.D // 128
        self.NST = self.TC // self.SS
        self.NPASS = self.D // self.QCH
        self.PBLK = self.QCH // 128
        self.NCHK = self.SS // self.L
        self.DMIX = 32
        self.DD = 64
        self.IB = self.INTER // 128
        self.GT = self.TC // 2
        self.NSLAB = self.GT // self.SS
        self.EPS_GN = 1e-5 * 64.0
        self.EPS_LN = 1e-5


class Sem:
    __slots__ = ("name", "val", "h", "dense", "targets")

    def __init__(self, name, dense=False):
        self.name, self.val, self.h = name, 0, None
        self.dense = dense
        self.targets = set()


class B:

    ENGS = ("tensor", "vector", "scalar", "gpsimd", "sync")

    def __init__(self):
        self.mark_tensor = None
        self.mark_vector = None
        self.mark_gpsimd = None
        self.q = {e: [] for e in self.ENGS}
        self.sems = [Sem("sd", True), Sem("sgd", True), Sem("tt"), Sem("vv"),
                     Sem("sa"), Sem("scc", True), Sem("gg")]
        (self.sd, self.sgd, self.tt, self.vv,
         self.sa, self.scc, self.gg) = self.sems

    def wait(self, eng, sem, val=None):
        v = sem.val if val is None else val
        if v <= 0:
            return
        if not sem.dense:
            sem.targets.add(v)
        self.q[eng].append(("w", sem, v))

    def op(self, eng, fn, inc=None, n=1):
        if inc is not None:
            inc.val += n
            if not inc.dense:
                inc.targets.add(inc.val)
            self.q[eng].append(("o", fn, inc, inc.val))
        else:
            self.q[eng].append(("o", fn, None, 0))

    def dma(self, eng, out, in_, **kw):
        s = self.sgd if eng == "gpsimd" else self.sd
        self.q[eng].append(
            ("o", lambda e, _o=out, _i=in_, _kw=kw: e.dma_start(
                out=_o, in_=_i, **_kw), s, -16))
        s.val += 16
        return s.val

    def vsync(self):
        self.wait("vector", self.vv)

    def barrier(self):
        self.op("tensor", self.mark_tensor, inc=self.tt)
        self.op("vector", self.mark_vector, inc=self.vv)
        self.op("gpsimd", self.mark_gpsimd, inc=self.gg)
        svals = [(self.tt, self.tt.val), (self.vv, self.vv.val),
                 (self.gg, self.gg.val), (self.sd, self.sd.val),
                 (self.sgd, self.sgd.val), (self.sa, self.sa.val),
                 (self.scc, self.scc.val)]
        for e in self.ENGS:
            for sem, v in svals:
                self.wait(e, sem, v)

    def lower(self, eng_handle, eng_name):
        ranks = {}
        for s in self.sems:
            if not s.dense:
                ranks[s.name] = {t: i + 1 for i, t in
                                 enumerate(sorted(s.targets))}
        for rec in self.q[eng_name]:
            if rec[0] == "w":
                _, sem, v = rec
                if sem.dense:
                    eng_handle.wait_ge(sem.h, v)
                else:
                    r = ranks[sem.name].get(v)
                    assert r is not None
                    eng_handle.wait_ge(sem.h, r)
            else:
                _, fn, sem, cnt = rec
                if sem is None:
                    fn(eng_handle)
                elif cnt == -16:
                    fn(eng_handle).then_inc(sem.h, 16)
                elif sem.dense:
                    fn(eng_handle).then_inc(sem.h, cnt - (cnt - 1))
                else:
                    if cnt in sem.targets:
                        fn(eng_handle).then_inc(sem.h, 1)
                    else:
                        fn(eng_handle)


def build_program(cfg, dbg=False, stop_after=None):
    c = cfg
    nc = bass.Bass("TRN2", target_bir_lowering=False)
    D, TC, SS, L, NB = c.D, c.TC, c.SS, c.L, c.NB
    NST, NPASS, PBLK, NCHK, QCH = c.NST, c.NPASS, c.PBLK, c.NCHK, c.QCH
    NEXP, IB, GT, NSLAB = c.NEXP, c.IB, c.GT, c.NSLAB

    dt = nc.dram_tensor
    xT = dt("xT", [D, TC + 1], F32, kind="ExternalInput").ap()
    maa_vecs = dt("maa_vecs", [128, 6 * NB], F32, kind="ExternalInput").ap()
    maa_w1 = dt("maa_w1", [D, 5 * c.DMIX], F32, kind="ExternalInput").ap()
    maa_w2 = dt("maa_w2", [c.DMIX, 5 * D], F32, kind="ExternalInput").ap()
    decay_vec = dt("decay_vec", [128, NB], F32, kind="ExternalInput").ap()
    decay_w1 = dt("decay_w1", [D, c.DD], F32, kind="ExternalInput").ap()
    decay_w2 = dt("decay_w2", [c.DD, D], F32, kind="ExternalInput").ap()
    u_vec = dt("u_vec", [128, NB], F32, kind="ExternalInput").ap()
    WqS = {s: dt("W%sS" % s, [NPASS, 128, NB * QCH], F32,
                 kind="ExternalInput").ap() for s in ("r", "k", "v", "g")}
    WoS = dt("WoS", [128, NB * D], F32, kind="ExternalInput").ap()
    gam = dt("gam", [128, NB], F32, kind="ExternalInput").ap()
    bet = dt("bet", [128, NB], F32, kind="ExternalInput").ap()
    gateT = dt("gateT", [D, NEXP], F32, kind="ExternalInput").ap()
    w1S = dt("w1S", [NEXP, IB, 128, D], F32, kind="ExternalInput").ap()
    w3S = dt("w3S", [NEXP, IB, 128, D], F32, kind="ExternalInput").ap()
    w2S = dt("w2S", [NEXP, NB, 128, c.INTER], F32,
             kind="ExternalInput").ap()
    sw1S = dt("sw1S", [2, IB, 128, D], F32, kind="ExternalInput").ap()
    sw3S = dt("sw3S", [2, IB, 128, D], F32, kind="ExternalInput").ap()
    sw2S = dt("sw2S", [2, NB, 128, c.INTER], F32, kind="ExternalInput").ap()
    cmask = dt("cmask", [128, 1], F32, kind="ExternalInput").ap()
    outT = dt("outT", [D, TC], F32, kind="ExternalOutput").ap()

    spk = "ExternalOutput" if dbg else "Internal"
    sp_xs = dt("sp_xs", [5, NST, 128, NB * SS], BF16, kind=spk)
    sp_y = dt("sp_y", [D, TC], F32, kind=spk)
    sp_rh = dt("sp_rh", [D, TC], BF16, kind=spk)
    sp_g = dt("sp_g", [D, TC], BF16, kind=spk)
    sp_h = dt("sp_h", [D, TC], F32, kind=spk)
    if dbg:
        d_r = dt("d_r", [D, TC], BF16, kind="ExternalOutput")
        d_k = dt("d_k", [D, TC], BF16, kind="ExternalOutput")
        d_v = dt("d_v", [D, TC], BF16, kind="ExternalOutput")
        d_pd = dt("d_pd", [D, TC], F32, kind="ExternalOutput")
        d_td = dt("d_td", [c.DD, TC], BF16, kind="ExternalOutput")
        d_cs = dt("d_cs", [D, TC], F32, kind="ExternalOutput")
        d_csr = dt("d_csr", [D, TC], F32, kind="ExternalOutput")
        d_bv = dt("d_bv", [D, TC // c.L], F32, kind="ExternalOutput")
        d_e1 = dt("d_e1", [D, TC], F32, kind="ExternalOutput")
        d_e5 = dt("d_e5", [D, TC // c.L], F32, kind="ExternalOutput")
        d_rt = dt("d_rt", [D, TC], BF16, kind="ExternalOutput")
        d_web = dt("d_web", [NEXP, 2 * NSLAB * SS], F32, kind="ExternalOutput")
        d_wtok = dt("d_wtok", [128, 2 * NSLAB * (SS // 128) * NEXP], F32,
                    kind="ExternalOutput")
    cc_in = dt("cc_in", [128, NB * L], F32)
    cc_out = dt("cc_out", [256, NB * L], F32)

    STREAMS = ("w", "k", "v", "r", "g")
    SMAA = {"w": 1, "k": 2, "v": 3, "r": 4, "g": 5}
    SF = {"w": 0, "k": 1, "v": 2, "r": 3, "g": 4}

    b = B()

    def blk3(ap_, nseg, seg):
        return ap_.rearrange("p (a t) -> p a t", a=nseg)

    with contextlib.ExitStack() as es:
        sb = lambda n, s, d=F32: es.enter_context(nc.sbuf_tensor(n, s, d))
        ps = lambda n, s, d=F32: es.enter_context(nc.psum_tensor(n, s, d))

        t_maa = sb("t_maa", [128, 6 * NB])
        t_dec = sb("t_dec", [128, NB])
        t_u = sb("t_u", [128, NB], BF16)
        t_gam = sb("t_gam", [128, NB])
        t_bet = sb("t_bet", [128, NB])
        t_ident = sb("t_ident", [128, 128], BF16)
        t_identf = sb("t_identf", [128, 128])
        t_maskU = sb("t_maskU", [128, L])
        t_ones1f = sb("t_ones1f", [1, 128])
        t_onescol = sb("t_onescol", [128, 1])
        t_hones = sb("t_hones", [128, 2])
        t_hpat = sb("t_hpat", [2, 128])
        t_cmask = sb("t_cmask", [128, 1])
        t_eps = sb("t_eps", [128, 2])
        t_zeroSS = sb("t_zeroSS", [128, SS])
        t_S = sb("t_S", [128, NB * L])
        t_Sin = sb("t_Sin", [128, NB * L], BF16)

        pA = ps("pA", [128, SS])
        pB = ps("pB", [128, SS])
        pY = ps("pY", [128, SS])
        pAT = ps("pAT", [128, SS])
        pTP = ps("pTP", [128, 1024], BF16)
        pU = ps("pU", [128, SS])
        pS1 = ps("pS1", [128, SS])
        pS2 = ps("pS2", [128, SS])

        for s in b.sems:
            s.h = es.enter_context(nc.semaphore(s.name))
        block = es.enter_context(nc.Block())

        maskU_np = np.triu(np.ones((64, 64), np.float32), 1)
        maskU_np = np.concatenate([maskU_np, maskU_np], 0)
        hones_np = np.zeros((128, 2), np.float32)
        hones_np[0:64, 0] = 1.0
        hones_np[64:128, 1] = 1.0
        consts = [
            (t_ident, nc.inline_tensor(np.eye(128, dtype=BF), "c_ident")),
            (t_identf, nc.inline_tensor(np.eye(128, dtype=np.float32), "c_identf")),
            (t_maskU, nc.inline_tensor(maskU_np, "c_maskU")),
            (t_ones1f, nc.inline_tensor(np.ones((1, 128), np.float32), "c_o1f")),
            (t_onescol, nc.inline_tensor(np.ones((128, 1), np.float32), "c_oc")),
            (t_hones, nc.inline_tensor(hones_np, "c_hones")),
            (t_hpat, nc.inline_tensor(np.ascontiguousarray(hones_np.T), "c_hpat")),
            (t_eps, nc.inline_tensor(np.broadcast_to(np.array(
                [c.EPS_GN, c.EPS_LN], np.float32), (128, 2)).copy(), "c_eps")),
        ]

        t_mark = sb("t_mark", [1, 2])
        b.mark_tensor = lambda e: e.matmul(
            pS1[0:1, 0:1], t_ones1f[0:1, 0:1], t_ones1f[0:1, 0:1],
            start=True, stop=True)
        b.mark_vector = lambda e: e.memset(t_mark[0:1, 0:1], 0.0)
        b.mark_gpsimd = lambda e: e.memset(t_mark[0:1, 1:2], 0.0)

        for dst, src in ((t_maa, maa_vecs), (t_dec, decay_vec),
                         (t_gam, gam), (t_bet, bet), (t_cmask, cmask)):
            b.dma("sync", dst[:, :], src[:, :])
        for dst, dr in consts:
            b.dma("sync", dst[:, :], dr.ap()[:, :])
        b.dma("gpsimd", t_u[:, :], u_vec[:, :])
        b.wait("vector", b.sd)
        b.op("vector", lambda e: e.memset(t_zeroSS[:, :], 0.0), inc=b.vv)
        b.op("vector", lambda e: e.memset(t_S[:, :], 0.0), inc=b.vv)

        if stop_after == "P0":
            pass
        with contextlib.ExitStack() as es1:
            sb1 = lambda n, s, d=F32: es1.enter_context(nc.sbuf_tensor(n, s, d))
            t_maaw1 = sb1("t_maaw1", [128, NB * 5 * c.DMIX], BF16)
            t_maaw2 = sb1("t_maaw2", [c.DMIX, 5 * D], BF16)
            t_dw1 = sb1("t_dw1", [128, NB * c.DD], BF16)
            t_dw2 = sb1("t_dw2", [c.DD, D], BF16)
            t_wq = {s: sb1("t_wq_%s" % s, [128, NB * QCH], BF16)
                    for s in ("r", "k", "v", "g")}
            t_xfull = sb1("t_xfull", [128, NB * (SS + 1)], BF16)
            t_xx = sb1("t_xx", [128, NB * SS], BF16)
            t_stream = [sb1("t_stream%d" % i, [128, NB * SS], BF16)
                        for i in range(2)]
            t_xxx = t_stream[1]
            t_a = sb1("t_a", [c.DMIX, 5 * SS], BF16)
            t_td = sb1("t_td", [c.DD, SS], BF16)
            t_rb = [sb1("t_rb%d" % i, [128, PBLK * SS], BF16) for i in range(2)]
            t_kb = [sb1("t_kb%d" % i, [128, PBLK * SS], BF16) for i in range(2)]
            t_vb = [sb1("t_vb%d" % i, [128, PBLK * SS], BF16) for i in range(2)]
            t_gqb = [sb1("t_gqb%d" % i, [128, PBLK * SS], BF16)
                     for i in range(2)]
            t_pdb = [sb1("t_pdb%d" % i, [128, PBLK * SS]) for i in range(2)]
            t_cs = sb1("t_cs", [128, SS])
            t_csr = sb1("t_csr", [128, SS])
            t_csx = sb1("t_csx", [128, SS])
            t_bx = sb1("t_bx", [128, SS])
            t_e1 = sb1("t_e1", [128, SS])
            t_rtb = [sb1("t_rtb%d" % i, [128, SS], BF16) for i in range(2)]
            t_ktb = [sb1("t_ktb%d" % i, [128, SS], BF16) for i in range(2)]
            t_khb = [sb1("t_khb%d" % i, [128, SS], BF16) for i in range(2)]
            t_rkb = [sb1("t_rkb%d" % i, [128, SS], BF16) for i in range(2)]
            t_rhatb = [sb1("t_rhatb%d" % i, [128, SS], BF16) for i in range(2)]
            t_bvec = sb1("t_bvec", [128, NCHK])
            t_evec = sb1("t_evec", [128, NCHK])
            t_plvb = [sb1("t_plvb%d" % i, [128, NCHK]) for i in range(2)]
            t_e5v = sb1("t_e5v", [128, NCHK])
            t_carryb = [sb1("t_carryb%d" % i, [128, PBLK]) for i in range(2)]
            t_Sbf = sb1("t_Sbf", [128, L], BF16)
            t_amt = sb1("t_amt", [128, SS], BF16)
            t_vtok = sb1("t_vtok", [128, 128], BF16)
            t_ktok = sb1("t_ktok", [128, 128], BF16)
            t_sdrow = sb1("t_sdrow", [1, 2 * SS])
            t_dv = sb1("t_dv", [128, SS], BF16)
            t_yob = [sb1("t_yob%d" % i, [128, SS]) for i in range(2)]

            b.dma("gpsimd", t_maaw1[:, :],
                  maa_w1.rearrange("(b p) f -> p b f", p=128))
            b.dma("gpsimd", t_maaw2[:, :], maa_w2[:, :])
            b.dma("gpsimd", t_dw1[:, :],
                  decay_w1.rearrange("(b p) f -> p b f", p=128))
            b.dma("gpsimd", t_dw2[:, :], decay_w2[:, :])

            last_proj = [0]
            wq_ld = [0]
            spill_sd = {}
            proj_tt = {}
            pa_free = {"vv": 0, "sa": 0}
            pb_free = {"vv": 0, "sa": 0}
            pd_free = {0: 0, 1: 0}
            gq_sd = {0: 0, 1: 0}
            post_tt = {0: 0, 1: 0}
            post_vv = [0]
            prods_bb = {0: 0, 1: 0}
            spill_done = {}
            xfull_free = [0]

            def wbank(bk):
                b.wait("tensor", b.vv, bk["vv"])
                b.wait("tensor", b.sa, bk["sa"])

            def emit_prologue(q):
                b.wait("gpsimd", b.tt, last_proj[0])
                for s in ("r", "k", "v", "g"):
                    b.dma("gpsimd", t_wq[s][:, :], WqS[s][q])
                wq_ld[0] = b.sgd.val
                b.wait("vector", b.vv)
                b.op("vector", lambda e, _q=q: e.memset(
                    t_carryb[_q % 2][:, :], 0.0), inc=b.vv)

            def emit_proj(q, st, par):
                qc0 = q * QCH
                tok0 = st * SS
                t_pd = t_pdb[par]
                t_r, t_k, t_v = t_rb[par], t_kb[par], t_vb[par]
                t_gq = t_gqb[par]
                wq_loaded = wq_ld[0]
                if True:
                    if q == 0:
                        b.wait("gpsimd", b.vv, xfull_free[0])
                        xf_d = b.dma("gpsimd", t_xfull[:, :],
                                     xT.rearrange("(b p) t -> p b t", p=128)
                                     [:, :, tok0:tok0 + SS + 1])
                        b.wait("vector", b.sgd, xf_d)
                        b.wait("vector", b.tt, proj_tt.get(1, 0))
                        x3 = blk3(t_xfull[:, :], NB, SS + 1)
                        b.op("vector", lambda e, _x=x3: e.tensor_tensor(
                            blk3(t_xx[:, :], NB, SS), _x[:, :, 0:SS],
                            _x[:, :, 1:SS + 1], ALU.subtract), inc=b.vv)
                        for bb in range(NB):
                            b.op("vector", lambda e, _b=bb: e.scalar_tensor_tensor(
                                t_xxx[:, _b * SS:(_b + 1) * SS],
                                t_xx[:, _b * SS:(_b + 1) * SS],
                                t_maa[:, _b:_b + 1],
                                t_xfull[:, _b * (SS + 1) + 1:(_b + 1) * (SS + 1)],
                                ALU.mult, ALU.add), inc=b.vv)
                        xxx_done = b.vv.val
                        b.wait("tensor", b.vv, xxx_done)
                        wbank(pa_free)
                        for f in range(5):
                            for kc in range(NB):
                                b.op("tensor", lambda e, _f=f, _kc=kc: e.matmul(
                                    pA[0:c.DMIX, 0:SS],
                                    t_maaw1[:, _kc * 5 * c.DMIX + _f * c.DMIX:
                                            _kc * 5 * c.DMIX + (_f + 1) * c.DMIX],
                                    t_xxx[:, _kc * SS:(_kc + 1) * SS],
                                    start=(_kc == 0), stop=(_kc == NB - 1)),
                                    inc=(b.tt if kc == NB - 1 else None))
                            b.wait("scalar", b.tt)
                            b.op("scalar", lambda e, _f=f: e.activation(
                                t_a[:, _f * SS:(_f + 1) * SS],
                                pA[0:c.DMIX, 0:SS], AF.Tanh), inc=b.sa)
                            b.wait("tensor", b.sa)
                        a_done = b.sa.val
                        pa_free["sa"] = a_done

                    rkvg_dst = {"r": t_r, "k": t_k, "v": t_v}
                    for si, s in enumerate(STREAMS):
                        sbuf = t_stream[si % 2]
                        if q == 0:
                            b.wait("vector", b.sd, spill_sd.get(si % 2, 0))
                            b.wait("vector", b.tt, proj_tt.get(si % 2, 0))
                            for bb in range(NB):
                                bpp = pB if bb % 2 == 0 else pA
                                bbk = pb_free if bb % 2 == 0 else pa_free
                                wbank(bbk)
                                b.op("tensor", lambda e, _s=s, _b=bb,
                                     _pp=bpp: e.matmul(
                                    _pp[:, 0:SS],
                                    t_maaw2[:, SF[_s] * D + _b * 128:
                                            SF[_s] * D + (_b + 1) * 128],
                                    t_a[:, SF[_s] * SS:(SF[_s] + 1) * SS],
                                    start=True, stop=True), inc=b.tt)
                                b.wait("vector", b.tt)
                                b.op("vector", lambda e, _s=s, _b=bb,
                                     _pp=bpp: e.scalar_tensor_tensor(
                                         t_bx[:, 0:SS], _pp[:, 0:SS],
                                         t_maa[:, SMAA[_s] * NB + _b:
                                               SMAA[_s] * NB + _b + 1],
                                         t_xx[:, _b * SS:(_b + 1) * SS],
                                         ALU.add, ALU.mult), inc=b.vv)
                                bbk["vv"] = b.vv.val
                                b.op("vector", lambda e, _sb=sbuf, _b=bb:
                                     e.tensor_tensor(
                                         _sb[:, _b * SS:(_b + 1) * SS],
                                         t_bx[:, 0:SS],
                                         t_xfull[:, _b * (SS + 1) + 1:
                                                 (_b + 1) * (SS + 1)],
                                         ALU.add), inc=b.vv)
                            built = b.vv.val
                            b.wait("sync", b.vv, built)
                            spill_sd[si % 2] = b.dma(
                                "sync", sp_xs.ap()[SF[s], st], sbuf[:, :])
                            spill_done[(s, st)] = spill_sd[si % 2]
                            b.wait("tensor", b.vv, built)
                        else:
                            b.wait("gpsimd", b.tt, proj_tt.get(si % 2, 0))
                            b.wait("gpsimd", b.sd, spill_done[(s, st)])
                            b.wait("gpsimd", b.sd, spill_sd.get(si % 2, 0))
                            ld = b.dma("gpsimd", sbuf[:, :],
                                       sp_xs.ap()[SF[s], st])
                            b.wait("tensor", b.sgd, ld)

                        b.wait("tensor", b.sgd, wq_loaded)
                        if s == "w":
                            wbank(pa_free)
                            for kc in range(NB):
                                b.op("tensor", lambda e, _kc=kc, _sb=sbuf:
                                     e.matmul(
                                         pA[0:c.DD, 0:SS],
                                         t_dw1[:, _kc * c.DD:(_kc + 1) * c.DD],
                                         _sb[:, _kc * SS:(_kc + 1) * SS],
                                         start=(_kc == 0), stop=(_kc == NB - 1)),
                                     inc=(b.tt if kc == NB - 1 else None))
                            b.wait("scalar", b.tt)
                            b.op("scalar", lambda e: e.activation(
                                t_td[:, :], pA[0:c.DD, 0:SS], AF.Tanh),
                                inc=b.sa)
                            pa_free["sa"] = b.sa.val
                            b.wait("tensor", b.sa)
                            if dbg:
                                b.wait("sync", b.sa)
                                b.dma("sync", d_td.ap()[:, tok0:tok0 + SS],
                                      t_td[:, :])
                            b.wait("scalar", b.vv, pd_free[par])
                            for ob in range(PBLK):
                                wpp = pB if ob % 2 == 0 else pA
                                wbk = pb_free if ob % 2 == 0 else pa_free
                                wbank(wbk)
                                b.op("tensor", lambda e, _ob=ob, _q0=qc0,
                                     _pp=wpp: e.matmul(
                                         _pp[:, 0:SS],
                                         t_dw2[:, _q0 + _ob * 128:
                                               _q0 + (_ob + 1) * 128],
                                         t_td[:, :], start=True, stop=True),
                                     inc=b.tt)
                                b.wait("scalar", b.tt)
                                b.op("scalar", lambda e, _ob=ob, _q=q,
                                     _pp=wpp: e.activation(
                                         t_pd[:, _ob * SS:(_ob + 1) * SS],
                                         _pp[:, 0:SS], AF.Exp,
                                         bias=t_dec[:, _q * PBLK + _ob:
                                                    _q * PBLK + _ob + 1]),
                                     inc=b.sa)
                                wbk["sa"] = b.sa.val
                        else:
                            for ob in range(PBLK):
                                pp = pA if ob % 2 == 0 else pB
                                pbk = pa_free if ob % 2 == 0 else pb_free
                                wbank(pbk)
                                for kc in range(NB):
                                    b.op("tensor", lambda e, _s=s, _ob=ob,
                                         _kc=kc, _pp=pp, _sb=sbuf: e.matmul(
                                             _pp[:, 0:SS],
                                             t_wq[_s][:, _kc * QCH + _ob * 128:
                                                      _kc * QCH + (_ob + 1) * 128],
                                             _sb[:, _kc * SS:(_kc + 1) * SS],
                                             start=(_kc == 0),
                                             stop=(_kc == NB - 1)),
                                         inc=(b.tt if kc == NB - 1 else None))
                                if s == "g":
                                    if ob == 0:
                                        b.wait("scalar", b.sd, gq_sd[par])
                                    b.wait("scalar", b.tt)
                                    b.op("scalar", lambda e, _ob=ob, _pp=pp,
                                         _gq=t_gq: e.activation(
                                             _gq[:, _ob * SS:(_ob + 1) * SS],
                                             _pp[:, 0:SS], AF.Silu), inc=b.sa)
                                    pbk["sa"] = b.sa.val
                                else:
                                    b.wait("vector", b.tt)
                                    b.op("vector", lambda e, _d=rkvg_dst[s],
                                         _ob=ob, _pp=pp: e.tensor_copy(
                                             _d[:, _ob * SS:(_ob + 1) * SS],
                                             _pp[:, 0:SS]), inc=b.vv)
                                    pbk["vv"] = b.vv.val
                        proj_tt[si % 2] = b.tt.val
                        if si == len(STREAMS) - 1:
                            last_proj[0] = b.tt.val
                    if q == 0:
                        xfull_free[0] = b.vv.val
                    b.wait("sync", b.sa)
                    gq_sd[par] = b.dma(
                        "sync", sp_g.ap().rearrange("(b p) t -> p b t", p=128)
                        [:, q * PBLK:(q + 1) * PBLK, tok0:tok0 + SS],
                        t_gq[:, :])

                    if dbg:
                        b.wait("sync", b.vv)
                        b.wait("sync", b.sa)
                        for _nm, _t in (("d_r", t_r), ("d_k", t_k),
                                        ("d_v", t_v), ("d_pd", t_pd)):
                            b.dma("sync", locals()["_t"] if False else
                                  dict(d_r=d_r, d_k=d_k, d_v=d_v,
                                       d_pd=d_pd)[_nm].ap()
                                  .rearrange("(b p) t -> p b t", p=128)
                                  [:, q * PBLK:(q + 1) * PBLK,
                                   tok0:tok0 + SS], _t[:, :])

            def emit_pre_bb(q, st, par, bb):
                tok0 = st * SS
                gb = q * PBLK + bb
                t_pd = t_pdb[par]
                t_r, t_k = t_rb[par], t_kb[par]
                t_carry = t_carryb[q % 2]
                t_rt, t_kt = t_rtb[bb], t_ktb[bb]
                t_kh, t_rk = t_khb[bb], t_rkb[bb]
                t_rhat, t_plv = t_rhatb[bb], t_plvb[bb]
                b.wait("vector", b.tt, post_tt[bb])
                if True:
                    if True:
                        b.op("vector", lambda e, _b=bb: e.tensor_tensor_scan(
                            t_cs[:, :], t_pd[:, _b * SS:(_b + 1) * SS],
                            t_zeroSS[:, :], 0.0, ALU.add, ALU.add), inc=b.vv)
                        b.vsync()
                        b.op("vector", lambda e: e.memset(t_bvec[:, 0:1], 0.0),
                             inc=b.vv)
                        if NCHK > 1:
                            b.op("vector", lambda e: e.tensor_copy(
                                t_bvec[:, 1:NCHK],
                                blk3(t_cs[:, :], NCHK, L)
                                [:, 0:NCHK - 1, L - 1:L].squeeze(2)), inc=b.vv)
                        b.vsync()
                        b.op("vector", lambda e: e.tensor_tensor(
                            blk3(t_csr[:, :], NCHK, L),
                            blk3(t_cs[:, :], NCHK, L),
                            t_bvec[:, :].unsqueeze(2)
                            .broadcast_to([128, NCHK, L]), ALU.subtract),
                            inc=b.vv)
                        b.op("vector", lambda e, _b=bb: e.tensor_tensor(
                            t_csx[:, :], t_csr[:, :],
                            t_pd[:, _b * SS:(_b + 1) * SS], ALU.subtract),
                            inc=b.vv)
                        if dbg:
                            b.wait("sync", b.vv)
                            b.dma("sync", d_cs.ap()
                                  .rearrange("(b p) t -> p b t", p=128)
                                  [:, gb, tok0:tok0 + SS], t_cs[:, :])
                            b.dma("sync", d_csr.ap()
                                  .rearrange("(b p) t -> p b t", p=128)
                                  [:, gb, tok0:tok0 + SS], t_csr[:, :])
                            b.dma("sync", d_bv.ap()
                                  .rearrange("(b p) t -> p b t", p=128)
                                  [:, gb, tok0 // c.L:tok0 // c.L + NCHK],
                                  t_bvec[:, :])
                        b.op("vector", lambda e: e.tensor_copy(
                            t_evec[:, :], blk3(t_csr[:, :], NCHK, L)
                            [:, :, L - 1:L].squeeze(2)), inc=b.vv)
                        b.op("vector", lambda e, _b=bb: e.tensor_scalar(
                            t_e5v[:, :], t_bvec[:, :],
                            t_carry[:, _b:_b + 1], None, ALU.add), inc=b.vv)
                        b.op("vector", lambda e, _b=bb: e.tensor_tensor(
                            t_carry[:, _b:_b + 1], t_carry[:, _b:_b + 1],
                            t_cs[:, SS - 1:SS], ALU.add), inc=b.vv)
                        pre_exp = b.vv.val
                        b.wait("scalar", b.vv, pre_exp)
                        b.op("scalar", lambda e: e.activation(
                            t_e1[:, :], t_csx[:, :], AF.Exp, scale=-1.0),
                            inc=b.sa)
                        b.op("scalar", lambda e: e.activation(
                            t_csx[:, :], t_csr[:, :], AF.Exp), inc=b.sa)
                        b.op("scalar", lambda e: e.activation(
                            t_plv[:, :], t_evec[:, :], AF.Exp, scale=-1.0),
                            inc=b.sa)
                        b.op("scalar", lambda e: e.activation(
                            t_e5v[:, :], t_e5v[:, :], AF.Exp, scale=-1.0),
                            inc=b.sa)
                        exps1 = b.sa.val
                        b.wait("vector", b.sa, exps1)
                        b.vsync()
                        b.op("vector", lambda e: e.tensor_tensor(
                            blk3(t_csr[:, :], NCHK, L),
                            blk3(t_csr[:, :], NCHK, L),
                            t_evec[:, :].unsqueeze(2)
                            .broadcast_to([128, NCHK, L]), ALU.subtract),
                            inc=b.vv)
                        b.wait("scalar", b.vv)
                        b.op("scalar", lambda e: e.activation(
                            t_csr[:, :], t_csr[:, :], AF.Exp), inc=b.sa)
                        e3_done = b.sa.val
                        if dbg:
                            b.wait("sync", b.sa, e3_done)
                            b.dma("sync", d_e1.ap()
                                  .rearrange("(b p) t -> p b t", p=128)
                                  [:, gb, tok0:tok0 + SS], t_e1[:, :])
                            b.dma("sync", d_e5.ap()
                                  .rearrange("(b p) t -> p b t", p=128)
                                  [:, gb, tok0 // c.L:tok0 // c.L + NCHK],
                                  t_e5v[:, :])
                        b.wait("vector", b.sa, e3_done)
                        b.op("vector", lambda e, _b=bb: e.tensor_tensor(
                            t_rt[:, :], t_r[:, _b * SS:(_b + 1) * SS],
                            t_e1[:, :], ALU.mult), inc=b.vv)
                        b.op("vector", lambda e, _b=bb: e.tensor_tensor(
                            t_kt[:, :], t_k[:, _b * SS:(_b + 1) * SS],
                            t_csx[:, :], ALU.mult), inc=b.vv)
                        b.op("vector", lambda e, _b=bb: e.tensor_tensor(
                            t_kh[:, :], t_k[:, _b * SS:(_b + 1) * SS],
                            t_csr[:, :], ALU.mult), inc=b.vv)
                        b.op("vector", lambda e, _b=bb: e.tensor_tensor(
                            t_rk[:, :], t_r[:, _b * SS:(_b + 1) * SS],
                            t_k[:, _b * SS:(_b + 1) * SS], ALU.mult), inc=b.vv)
                        b.op("vector", lambda e: e.tensor_tensor(
                            blk3(t_rhat[:, :], NCHK, L),
                            blk3(t_rt[:, :], NCHK, L),
                            t_e5v[:, :].unsqueeze(2)
                            .broadcast_to([128, NCHK, L]), ALU.mult), inc=b.vv)
                        prods = b.vv.val
                        prods_bb[bb] = prods
                        pd_free[par] = prods
                        if dbg:
                            b.wait("sync", b.vv, prods)
                            b.dma("sync", d_rt.ap()
                                  .rearrange("(b p) t -> p b t", p=128)
                                  [:, gb, tok0:tok0 + SS], t_rt[:, :])
                        b.wait("sync", b.vv, prods)
                        b.dma("sync", sp_rh.ap()
                              .rearrange("(b p) t -> p b t", p=128)
                              [:, gb, tok0:tok0 + SS], t_rhat[:, :])

            def emit_post_bb(q, st, par, bb):
                tok0 = st * SS
                gb = q * PBLK + bb
                t_v = t_vb[par]
                t_rt, t_kt = t_rtb[bb], t_ktb[bb]
                t_kh, t_rk = t_khb[bb], t_rkb[bb]
                t_plv = t_plvb[bb]
                t_yo = t_yob[bb]
                prods = prods_bb[bb]
                if True:
                    if True:
                        b.wait("tensor", b.vv, prods)
                        b.wait("tensor", b.vv, post_vv[0])
                        b.op("tensor", lambda e, _gb=gb: e.matmul(
                            pS1[0:1, 0:SS], t_u[0:64, _gb:_gb + 1],
                            t_rk[0:64, :], start=True, stop=True), inc=None)
                        b.op("tensor", lambda e, _gb=gb: e.matmul(
                            pS2[0:1, 0:SS], t_u[64:128, _gb:_gb + 1],
                            t_rk[64:128, :], start=True, stop=True,
                            tile_position=(64, 0)), inc=None)
                        for ch in range(NCHK):
                            c0 = ch * L
                            b.op("tensor", lambda e, _c0=c0: e.matmul(
                                pAT[0:64, _c0:_c0 + L],
                                t_kt[0:64, _c0:_c0 + L],
                                t_rt[0:64, _c0:_c0 + L],
                                start=True, stop=True), inc=None)
                            b.op("tensor", lambda e, _c0=c0: e.matmul(
                                pAT[64:128, _c0:_c0 + L],
                                t_kt[64:128, _c0:_c0 + L],
                                t_rt[64:128, _c0:_c0 + L], start=True,
                                stop=True, tile_position=(64, 64)),
                                inc=(b.tt if ch == NCHK - 1 else None))
                        at_done = b.tt.val
                        b.wait("vector", b.tt, at_done)
                        b.op("vector", lambda e: e.tensor_tensor(
                            blk3(t_amt[:, :], NCHK, L),
                            blk3(pAT[:, 0:SS], NCHK, L),
                            t_maskU[:, :].unsqueeze(1)
                            .broadcast_to([128, NCHK, L]), ALU.mult), inc=b.vv)
                        b.op("vector", lambda e: e.tensor_copy(
                            t_sdrow[0:1, 0:SS], pS1[0:1, 0:SS]), inc=b.vv)
                        b.op("vector", lambda e: e.tensor_copy(
                            t_sdrow[0:1, SS:2 * SS], pS2[0:1, 0:SS]), inc=b.vv)
                        amt_done = b.vv.val
                        b.wait("tensor", b.vv, amt_done)
                        b.op("tensor", lambda e: e.matmul(
                            pAT[0:64, 0:SS], t_ones1f[0:1, 0:64],
                            t_sdrow[0:1, 0:SS], start=True, stop=True),
                            inc=None)
                        b.op("tensor", lambda e: e.matmul(
                            pAT[64:128, 0:SS], t_ones1f[0:1, 0:64],
                            t_sdrow[0:1, SS:2 * SS], start=True, stop=True,
                            tile_position=(0, 64)), inc=b.tt)
                        sdbc = b.tt.val
                        b.wait("vector", b.tt, sdbc)
                        b.op("vector", lambda e, _b=bb: e.tensor_tensor(
                            t_dv[:, :], pAT[:, 0:SS],
                            t_v[:, _b * SS:(_b + 1) * SS], ALU.mult),
                            inc=b.vv)
                        dv_done = b.vv.val

                        for ch in range(NCHK):
                            c0 = ch * L
                            b.op("vector", lambda e, _gb=gb: e.tensor_copy(
                                t_Sbf[:, :], t_S[:, _gb * L:(_gb + 1) * L]),
                                inc=b.vv)
                            scast = b.vv.val
                            b.wait("tensor", b.vv, scast)
                            b.op("tensor", lambda e, _c0=c0, _b=bb: e.transpose(
                                pTP[0:64, 0:128],
                                t_v[:, _b * SS + _c0:_b * SS + _c0 + L],
                                t_ident[:, :]), inc=None)
                            b.op("tensor", lambda e, _c0=c0, _b=bb: e.transpose(
                                pTP[64:128, 0:128],
                                t_v[:, _b * SS + _c0:_b * SS + _c0 + L],
                                t_ident[:, :], tile_position=(0, 64)),
                                inc=None)
                            b.op("tensor", lambda e, _c0=c0: e.transpose(
                                pTP[0:64, 128:256], t_kh[:, _c0:_c0 + L],
                                t_ident[:, :]), inc=None)
                            b.op("tensor", lambda e, _c0=c0: e.transpose(
                                pTP[64:128, 128:256], t_kh[:, _c0:_c0 + L],
                                t_ident[:, :], tile_position=(0, 64)),
                                inc=b.tt)
                            tp_done = b.tt.val
                            b.wait("vector", b.tt, tp_done)
                            b.op("vector", lambda e: e.tensor_copy(
                                t_vtok[:, :], pTP[:, 0:128]), inc=b.vv)
                            b.op("vector", lambda e: e.tensor_copy(
                                t_ktok[:, :], pTP[:, 128:256]), inc=b.vv)
                            tok_done = b.vv.val
                            b.wait("tensor", b.vv, tok_done)
                            b.op("tensor", lambda e, _c0=c0: e.matmul(
                                pY[0:64, _c0:_c0 + L], t_vtok[0:64, 0:64],
                                t_amt[0:64, _c0:_c0 + L],
                                start=True, stop=False), inc=None)
                            b.op("tensor", lambda e, _c0=c0: e.matmul(
                                pY[0:64, _c0:_c0 + L], t_Sbf[0:64, :],
                                t_rt[0:64, _c0:_c0 + L],
                                start=False, stop=True), inc=None)
                            b.op("tensor", lambda e, _c0=c0: e.matmul(
                                pY[64:128, _c0:_c0 + L],
                                t_vtok[64:128, 64:128],
                                t_amt[64:128, _c0:_c0 + L], start=True,
                                stop=False, tile_position=(64, 64)), inc=None)
                            b.op("tensor", lambda e, _c0=c0: e.matmul(
                                pY[64:128, _c0:_c0 + L], t_Sbf[64:128, :],
                                t_rt[64:128, _c0:_c0 + L], start=False,
                                stop=True, tile_position=(64, 64)), inc=None)
                            b.op("tensor", lambda e, _c0=c0: e.matmul(
                                pU[0:64, _c0:_c0 + L], t_ktok[0:64, 0:64],
                                t_vtok[0:64, 0:64], start=True, stop=True),
                                inc=None)
                            b.op("tensor", lambda e, _c0=c0: e.matmul(
                                pU[64:128, _c0:_c0 + L],
                                t_ktok[64:128, 64:128],
                                t_vtok[64:128, 64:128], start=True, stop=True,
                                tile_position=(64, 64)), inc=b.tt)
                            yu_done = b.tt.val
                            b.wait("vector", b.tt, yu_done)
                            b.op("vector", lambda e, _gb=gb, _ch=ch:
                                 e.tensor_scalar(
                                     t_S[:, _gb * L:(_gb + 1) * L],
                                     t_S[:, _gb * L:(_gb + 1) * L],
                                     t_plv[:, _ch:_ch + 1], None, ALU.mult),
                                 inc=b.vv)
                            b.op("vector", lambda e, _gb=gb, _c0=c0:
                                 e.tensor_tensor(
                                     t_S[:, _gb * L:(_gb + 1) * L],
                                     t_S[:, _gb * L:(_gb + 1) * L],
                                     pU[:, _c0:_c0 + L], ALU.add), inc=b.vv)
                        b.wait("vector", b.tt)
                        b.op("vector", lambda e: e.tensor_tensor(
                            t_yo[:, :], pY[:, 0:SS], t_dv[:, :], ALU.add),
                            inc=b.vv)
                        yo_done = b.vv.val
                        b.wait("sync", b.vv, yo_done)
                        b.dma("sync", sp_y.ap()
                              .rearrange("(b p) t -> p b t", p=128)
                              [:, gb, tok0:tok0 + SS], t_yo[:, :])
                post_vv[0] = b.vv.val
                post_tt[bb] = b.tt.val

            prev = None
            for q in range(NPASS):
                emit_prologue(q)
                for st in range(NST):
                    par = (q * NST + st) % 2
                    if prev is not None:
                        for bb in range(PBLK):
                            emit_pre_bb(prev[0], prev[1], prev[2], bb)
                    emit_proj(q, st, par)
                    if prev is not None:
                        for bb in range(PBLK):
                            emit_post_bb(prev[0], prev[1], prev[2], bb)
                    prev = (q, st, par)
            for bb in range(PBLK):
                emit_pre_bb(prev[0], prev[1], prev[2], bb)
            for bb in range(PBLK):
                emit_post_bb(prev[0], prev[1], prev[2], bb)

            b.barrier()
            b.dma("gpsimd", cc_in.ap()[:, :], t_S[:, :])
            b.wait("gpsimd", b.sgd)
            if stop_after == "NOCC":
                b.dma("gpsimd", cc_out.ap()[0:128, :], cc_in.ap()[:, :])
                b.dma("gpsimd", cc_out.ap()[128:256, :], cc_in.ap()[:, :])
            else:
                b.op("gpsimd", lambda e: e.collective_compute(
                    "AllGather", ALU.bypass,
                    replica_groups=[[2 * i, 2 * i + 1]
                                    for i in range(c.NCORE // 2)],
                    ins=[cc_in.ap().opt()], outs=[cc_out.ap().opt()]),
                    inc=b.scc)
                b.wait("gpsimd", b.scc)
            b.dma("gpsimd", t_S[:, :], cc_out.ap()[0:128, :])
            b.wait("vector", b.sgd)
            b.op("vector", lambda e: e.tensor_scalar(
                t_S[:, :], t_S[:, :], t_cmask[:, 0:1], None, ALU.mult),
                inc=b.vv)
            b.op("vector", lambda e: e.tensor_copy(t_Sin[:, :], t_S[:, :]),
                 inc=b.vv)
            b.barrier()

        if stop_after in ("P1", "P2", "NOCC"):
            b.wait("sync", b.sd)
            b.wait("sync", b.sgd)
            _finish(b, block)
            return nc
        with contextlib.ExitStack() as es3:
            sb3 = lambda n, s, d=F32: es3.enter_context(nc.sbuf_tensor(n, s, d))
            TS3 = SS // 2
            NST3 = TC // TS3
            t_wo = sb3("t_wo", [128, NB * D], BF16)
            t_ytile = sb3("t_ytile", [128, NB * TS3])
            t_gtile = sb3("t_gtile", [128, NB * TS3], BF16)
            t_rhtile = sb3("t_rhtile", [128, NB * TS3], BF16)
            t_yng = sb3("t_yng", [128, NB * TS3], BF16)
            t_xf3 = sb3("t_xf3", [128, NB * TS3])
            t_sq = [sb3("t_sq%d" % i, [128, TS3]) for i in range(2)]
            t_mu = sb3("t_mu", [128, TS3])
            t_mu2 = sb3("t_mu2", [128, TS3])
            t_rstd = sb3("t_rstd", [128, TS3])
            t_row1 = sb3("t_row1", [1, TS3])
            t_row2 = sb3("t_row2", [1, TS3])
            t_strip1 = sb3("t_strip1", [2, TS3])
            t_strip2 = sb3("t_strip2", [2, TS3])

            b.dma("gpsimd", t_wo[:, :], WoS[:, :])
            wo_loaded = b.sgd.val

            for st in range(NST3):
                tok0 = st * TS3
                b.wait("sync", b.vv)
                b.wait("sync", b.tt)
                b.dma("sync", t_ytile[:, :],
                      sp_y.ap().rearrange("(b p) t -> p b t", p=128)
                      [:, :, tok0:tok0 + TS3])
                b.dma("sync", t_rhtile[:, :],
                      sp_rh.ap().rearrange("(b p) t -> p b t", p=128)
                      [:, :, tok0:tok0 + TS3])
                b.dma("sync", t_gtile[:, :],
                      sp_g.ap().rearrange("(b p) t -> p b t", p=128)
                      [:, :, tok0:tok0 + TS3])
                b.dma("sync", t_xf3[:, :],
                      xT.rearrange("(b p) t -> p b t", p=128)
                      [:, :, tok0 + 1:tok0 + TS3 + 1])
                ld_done = b.sd.val

                for gb in range(NB):
                    b.wait("tensor", b.sd, ld_done)
                    b.wait("tensor", b.vv)
                    b.wait("tensor", b.sa)
                    b.op("tensor", lambda e, _gb=gb: e.matmul(
                        pAT[0:64, 0:TS3],
                        t_Sin[0:64, _gb * L:(_gb + 1) * L],
                        t_rhtile[0:64, _gb * TS3:(_gb + 1) * TS3],
                        start=True, stop=True), inc=None)
                    b.op("tensor", lambda e, _gb=gb: e.matmul(
                        pAT[64:128, 0:TS3],
                        t_Sin[64:128, _gb * L:(_gb + 1) * L],
                        t_rhtile[64:128, _gb * TS3:(_gb + 1) * TS3],
                        start=True, stop=True, tile_position=(64, 64)),
                        inc=b.tt)
                    corr_done = b.tt.val
                    b.wait("vector", b.tt, corr_done)
                    b.op("vector", lambda e, _gb=gb: e.tensor_tensor(
                        t_ytile[:, _gb * TS3:(_gb + 1) * TS3],
                        t_ytile[:, _gb * TS3:(_gb + 1) * TS3],
                        pAT[:, 0:TS3], ALU.add), inc=b.vv)
                    yf_done = b.vv.val
                    b.wait("scalar", b.vv, yf_done)
                    b.op("scalar", lambda e, _gb=gb: e.activation(
                        t_sq[0][:, :], t_ytile[:, _gb * TS3:(_gb + 1) * TS3],
                        AF.Square), inc=b.sa)
                    sq_done = b.sa.val
                    b.wait("tensor", b.vv, yf_done)
                    b.wait("tensor", b.sa, sq_done)
                    b.op("tensor", lambda e, _gb=gb: e.matmul(
                        pS1[0:2, 0:TS3], t_hones[:, :],
                        t_ytile[:, _gb * TS3:(_gb + 1) * TS3],
                        start=True, stop=True), inc=None)
                    b.op("tensor", lambda e: e.matmul(
                        pS2[0:2, 0:TS3], t_hones[:, :], t_sq[0][:, :],
                        start=True, stop=True), inc=b.tt)
                    stats_done = b.tt.val
                    b.wait("vector", b.tt, stats_done)
                    b.op("vector", lambda e: e.tensor_copy(
                        t_strip1[:, :], pS1[0:2, 0:TS3]), inc=b.vv)
                    b.op("vector", lambda e: e.tensor_copy(
                        t_strip2[:, :], pS2[0:2, 0:TS3]), inc=b.vv)
                    strips = b.vv.val
                    b.wait("tensor", b.vv, strips)
                    b.op("tensor", lambda e: e.matmul(
                        pS1[:, 0:TS3], t_hpat[:, :], t_strip1[:, :],
                        start=True, stop=True), inc=None)
                    b.op("tensor", lambda e: e.matmul(
                        pS2[:, 0:TS3], t_hpat[:, :], t_strip2[:, :],
                        start=True, stop=True), inc=b.tt)
                    bc_done = b.tt.val
                    b.wait("scalar", b.tt, bc_done)
                    b.op("scalar", lambda e: e.activation(
                        t_mu[:, :], pS1[:, 0:TS3], AF.Copy,
                        scale=1.0 / 64.0), inc=b.sa)
                    mu_done = b.sa.val
                    b.wait("vector", b.sa, mu_done)
                    b.op("vector", lambda e: e.tensor_tensor(
                        t_mu2[:, :], t_mu[:, :], t_mu[:, :], ALU.mult),
                        inc=b.vv)
                    b.op("vector", lambda e: e.scalar_tensor_tensor(
                        t_rstd[:, :], pS2[:, 0:TS3], 1.0 / 64.0, t_mu2[:, :],
                        ALU.mult, ALU.subtract), inc=b.vv)
                    q_done = b.vv.val
                    b.wait("scalar", b.vv, q_done)
                    b.op("scalar", lambda e: e.activation(
                        t_rstd[:, :], t_rstd[:, :], AF.Ln,
                        bias=t_eps[:, 0:1]), inc=b.sa)
                    b.op("scalar", lambda e: e.activation(
                        t_rstd[:, :], t_rstd[:, :], AF.Exp, scale=-0.5),
                        inc=b.sa)
                    rstd_done = b.sa.val
                    b.wait("vector", b.sa, rstd_done)
                    b.op("vector", lambda e, _gb=gb: e.tensor_tensor(
                        t_mu2[:, :], t_ytile[:, _gb * TS3:(_gb + 1) * TS3],
                        t_mu[:, :], ALU.subtract), inc=b.vv)
                    b.op("vector", lambda e: e.tensor_tensor(
                        t_mu2[:, :], t_mu2[:, :], t_rstd[:, :], ALU.mult),
                        inc=b.vv)
                    b.op("vector", lambda e, _gb=gb: e.tensor_scalar(
                        t_mu2[:, :], t_mu2[:, :], t_gam[:, _gb:_gb + 1],
                        t_bet[:, _gb:_gb + 1], ALU.mult, ALU.add), inc=b.vv)
                    b.op("vector", lambda e, _gb=gb: e.tensor_tensor(
                        t_yng[:, _gb * TS3:(_gb + 1) * TS3], t_mu2[:, :],
                        t_gtile[:, _gb * TS3:(_gb + 1) * TS3], ALU.mult),
                        inc=b.vv)
                yng_done = b.vv.val

                b.wait("tensor", b.sgd, wo_loaded)
                b.wait("tensor", b.vv, yng_done)
                for ob in range(NB):
                    pp = pA if ob % 2 == 0 else pB
                    for kc in range(NB):
                        b.op("tensor", lambda e, _ob=ob, _kc=kc, _pp=pp:
                             e.matmul(
                                 _pp[:, 0:TS3],
                                 t_wo[:, _kc * D + _ob * 128:
                                      _kc * D + (_ob + 1) * 128],
                                 t_yng[:, _kc * TS3:(_kc + 1) * TS3],
                                 start=(_kc == 0), stop=(_kc == NB - 1)),
                             inc=(b.tt if kc == NB - 1 else None))
                    b.wait("vector", b.tt)
                    b.op("vector", lambda e, _ob=ob, _pp=pp:
                         e.tensor_tensor(
                             t_ytile[:, _ob * TS3:(_ob + 1) * TS3],
                             _pp[:, 0:TS3],
                             t_xf3[:, _ob * TS3:(_ob + 1) * TS3], ALU.add),
                         inc=b.vv)
                    b.wait("tensor", b.vv)
                h_done = b.vv.val

                sq_tt = [0, 0]
                for bb2 in range(NB):
                    b.wait("scalar", b.vv, h_done)
                    b.wait("scalar", b.tt, sq_tt[bb2 % 2])
                    b.op("scalar", lambda e, _b=bb2: e.activation(
                        t_sq[_b % 2][:, :],
                        t_ytile[:, _b * TS3:(_b + 1) * TS3], AF.Square),
                        inc=b.sa)
                    b.wait("tensor", b.sa)
                    b.op("tensor", lambda e, _b=bb2: e.matmul(
                        pS1[0:1, 0:TS3], t_onescol[:, :],
                        t_ytile[:, _b * TS3:(_b + 1) * TS3],
                        start=(_b == 0), stop=(_b == NB - 1)), inc=None)
                    b.op("tensor", lambda e, _b=bb2: e.matmul(
                        pS2[0:1, 0:TS3], t_onescol[:, :], t_sq[_b % 2][:, :],
                        start=(_b == 0), stop=(_b == NB - 1)), inc=b.tt)
                    sq_tt[bb2 % 2] = b.tt.val
                lnstats = b.tt.val
                b.wait("vector", b.tt, lnstats)
                b.op("vector", lambda e: e.tensor_copy(
                    t_row1[:, :], pS1[0:1, 0:TS3]), inc=b.vv)
                b.op("vector", lambda e: e.tensor_copy(
                    t_row2[:, :], pS2[0:1, 0:TS3]), inc=b.vv)
                rows = b.vv.val
                b.wait("tensor", b.vv, rows)
                b.op("tensor", lambda e: e.matmul(
                    pS1[:, 0:TS3], t_ones1f[0:1, :], t_row1[:, :],
                    start=True, stop=True), inc=None)
                b.op("tensor", lambda e: e.matmul(
                    pS2[:, 0:TS3], t_ones1f[0:1, :], t_row2[:, :],
                    start=True, stop=True), inc=b.tt)
                lnbc = b.tt.val
                b.wait("scalar", b.tt, lnbc)
                b.op("scalar", lambda e: e.activation(
                    t_mu[:, :], pS1[:, 0:TS3], AF.Copy, scale=1.0 / D),
                    inc=b.sa)
                lnmu = b.sa.val
                b.wait("vector", b.sa, lnmu)
                b.op("vector", lambda e: e.tensor_tensor(
                    t_mu2[:, :], t_mu[:, :], t_mu[:, :], ALU.mult), inc=b.vv)
                b.op("vector", lambda e: e.scalar_tensor_tensor(
                    t_rstd[:, :], pS2[:, 0:TS3], 1.0 / D, t_mu2[:, :],
                    ALU.mult, ALU.subtract), inc=b.vv)
                lnq = b.vv.val
                b.wait("scalar", b.vv, lnq)
                b.op("scalar", lambda e: e.activation(
                    t_rstd[:, :], t_rstd[:, :], AF.Ln,
                    bias=t_eps[:, 1:2]), inc=b.sa)
                b.op("scalar", lambda e: e.activation(
                    t_rstd[:, :], t_rstd[:, :], AF.Exp, scale=-0.5), inc=b.sa)
                lnr = b.sa.val
                b.wait("vector", b.sa, lnr)
                for bb2 in range(NB):
                    b.op("vector", lambda e, _b=bb2: e.tensor_tensor(
                        t_ytile[:, _b * TS3:(_b + 1) * TS3],
                        t_ytile[:, _b * TS3:(_b + 1) * TS3], t_mu[:, :],
                        ALU.subtract), inc=b.vv)
                    b.op("vector", lambda e, _b=bb2: e.tensor_tensor(
                        t_ytile[:, _b * TS3:(_b + 1) * TS3],
                        t_ytile[:, _b * TS3:(_b + 1) * TS3], t_rstd[:, :],
                        ALU.mult), inc=b.vv)
                hln = b.vv.val
                b.wait("sync", b.vv, hln)
                b.dma("sync", sp_h.ap().rearrange("(b p) t -> p b t", p=128)
                      [:, :, tok0:tok0 + TS3], t_ytile[:, :])
            b.barrier()

        if stop_after == "P3":
            b.wait("sync", b.sd)
            b.wait("sync", b.sgd)
            _finish(b, block)
            return nc
        with contextlib.ExitStack() as es4:
            sb4 = lambda n, s, d=F32: es4.enter_context(nc.sbuf_tensor(n, s, d))
            t_acc = sb4("t_acc", [128, NB * GT])
            t_hb = sb4("t_hb", [128, NB * GT], BF16)
            t_gate = sb4("t_gate", [128, NB * NEXP], BF16)
            t_w13 = [sb4("t_w13_%d" % i, [128, NB * 128], BF16)
                     for i in range(4)]
            t_w2ob = [sb4("t_w2ob%d" % i, [128, IB * 128], BF16)
                      for i in range(2)]
            t_p = sb4("t_p", [128, NSLAB * IB * SS], BF16)
            t_q0 = sb4("t_q0", [128, SS], BF16)
            t_sc16 = sb4("t_sc16", [NEXP, SS])
            t_wtok = sb4("t_wtok", [128, (SS // 128) * NEXP])
            t_wem2 = sb4("t_wem2", [NEXP, NSLAB * SS])
            t_wstage = sb4("t_wstage", [1, SS])
            t_webc = [sb4("t_webc%d" % i, [128, SS]) for i in range(NSLAB)]
            t_gneg = sb4("t_gneg", [128, 1])
            t_grec = sb4("t_grec", [128, 1])
            t_gm1 = sb4("t_gm1", [128, 1])
            t_gm2 = sb4("t_gm2", [128, 1])
            t_gsum = sb4("t_gsum", [128, 1])
            t_geq = sb4("t_geq", [128, NEXP])
            t_gmk = sb4("t_gmk", [128, NEXP])
            t_gge = sb4("t_gge", [128, NEXP])
            t_wmask = sb4("t_wmask", [128, (SS // 128) * NEXP])
            q_sq = [sb4("u4_sq%d" % i, [128, SS]) for i in range(2)]
            q_mu = sb4("u4_mu", [128, SS])
            q_mu2 = sb4("u4_mu2", [128, SS])
            q_rstd = sb4("u4_rstd", [128, SS])
            q_row1 = sb4("u4_row1", [1, SS])
            q_row2 = sb4("u4_row2", [1, SS])

            b.dma("gpsimd", t_gate[:, :],
                  gateT.rearrange("(b p) e -> p b e", p=128))
            gate_loaded = b.sgd.val

            sched = [("sh", 0), ("sh", 1)] + [("ex", e) for e in range(NEXP)]
            flat = [(kind, ei, ib) for kind, ei in sched for ib in range(IB)]
            w13_tt = {0: 0, 1: 0}
            w13_sd = {}
            w2ob_tt = [0, 0]

            def issue_w13(j):
                kind, ei, ib = flat[j]
                bi = (j % 2) * 2
                b.wait("gpsimd", b.tt, w13_tt[j % 2])
                if kind == "sh":
                    b.dma("gpsimd", t_w13[bi][:, :], sw1S[ei, ib])
                    b.dma("gpsimd", t_w13[bi + 1][:, :], sw3S[ei, ib])
                else:
                    b.dma("gpsimd", t_w13[bi][:, :], w1S[ei, ib])
                    b.dma("gpsimd", t_w13[bi + 1][:, :], w3S[ei, ib])
                w13_sd[j] = b.sgd.val

            for grp in range(2):
                g0 = grp * GT
                b.wait("sync", b.vv)
                b.wait("sync", b.tt)
                b.wait("sync", b.sa)
                hload = b.dma("sync", t_acc[:, :],
                              sp_h.ap().rearrange("(b p) t -> p b t", p=128)
                              [:, :, g0:g0 + GT])
                issue_w13(0)
                issue_w13(1)
                silu_sa = [b.sa.val, b.sa.val]
                pm_vv = [b.vv.val, b.vv.val]
                acc_vv = [b.vv.val, b.vv.val]
                it_h = 0
                it_w = 0
                b.wait("scalar", b.sd, hload)
                for bb2 in range(NB):
                    b.op("scalar", lambda e, _b=bb2: e.activation(
                        t_hb[:, _b * GT:(_b + 1) * GT],
                        t_acc[:, _b * GT:(_b + 1) * GT], AF.Copy), inc=b.sa)
                hb_done = b.sa.val

                b.wait("tensor", b.sgd, gate_loaded)
                for sl in range(NSLAB):
                    ls0 = sl * SS
                    b.wait("tensor", b.sa, hb_done)
                    b.wait("tensor", b.vv)
                    for kc in range(NB):
                        b.op("tensor", lambda e, _kc=kc, _s=ls0: e.matmul(
                            pU[0:NEXP, 0:SS],
                            t_gate[:, _kc * NEXP:(_kc + 1) * NEXP],
                            t_hb[:, _kc * GT + _s:_kc * GT + _s + SS],
                            start=(_kc == 0), stop=(_kc == NB - 1)),
                            inc=(b.tt if kc == NB - 1 else None))
                    gmm = b.tt.val
                    b.wait("vector", b.tt, gmm)
                    b.op("vector", lambda e: e.tensor_copy(
                        t_sc16[:, :], pU[0:NEXP, 0:SS]), inc=b.vv)
                    scd = b.vv.val
                    b.wait("tensor", b.vv, scd)
                    for sub in range(SS // 128):
                        b.op("tensor", lambda e, _s=sub: e.transpose(
                            pU[:, _s * NEXP:(_s + 1) * NEXP],
                            t_sc16[:, _s * 128:(_s + 1) * 128],
                            t_identf[0:NEXP, 0:NEXP]),
                            inc=(b.tt if sub == SS // 128 - 1 else None))
                    gtp = b.tt.val
                    b.wait("vector", b.tt, gtp)
                    b.op("vector", lambda e: e.tensor_copy(
                        t_wtok[:, :], pU[:, 0:(SS // 128) * NEXP]), inc=b.vv)
                    b.vsync()
                    for sub in range(SS // 128):
                        sl_ap = t_wtok[:, sub * NEXP:(sub + 1) * NEXP]
                        wm_ap = t_wmask[:, sub * NEXP:(sub + 1) * NEXP]
                        b.op("vector", lambda e, _a=sl_ap: e.reduce_max(
                            t_gneg[:, :], _a, axis=AX.X, negate=True),
                            inc=b.vv)
                        b.wait("scalar", b.vv)
                        b.op("scalar", lambda e, _a=sl_ap: e.activation(
                            _a, _a, AF.Exp, bias=t_gneg[:, 0:1],
                            accum_out=t_gsum[:, 0:1]), inc=b.sa)
                        b.wait("vector", b.sa)
                        b.op("vector", lambda e: e.reciprocal(
                            t_grec[:, :], t_gsum[:, :]), inc=b.vv)
                        b.vsync()
                        b.op("vector", lambda e, _a=sl_ap: e.tensor_scalar(
                            _a, _a, t_grec[:, 0:1], None, ALU.mult), inc=b.vv)
                        b.vsync()
                        b.op("vector", lambda e, _a=sl_ap: e.reduce_max(
                            t_gm1[:, :], _a, axis=AX.X), inc=b.vv)
                        b.vsync()
                        b.op("vector", lambda e, _a=sl_ap: e.tensor_scalar(
                            t_geq[:, :], _a, t_gm1[:, 0:1], None,
                            ALU.is_equal), inc=b.vv)
                        b.vsync()
                        b.op("vector", lambda e, _a=sl_ap:
                             e.scalar_tensor_tensor(
                                 t_gmk[:, :], t_geq[:, :], -1e30, _a,
                                 ALU.mult, ALU.add), inc=b.vv)
                        b.vsync()
                        b.op("vector", lambda e: e.reduce_max(
                            t_gm2[:, :], t_gmk[:, :], axis=AX.X), inc=b.vv)
                        b.vsync()
                        b.op("vector", lambda e, _a=sl_ap: e.tensor_scalar(
                            t_gge[:, :], _a, t_gm2[:, 0:1], None,
                            ALU.is_ge), inc=b.vv)
                        b.vsync()
                        b.op("vector", lambda e, _a=sl_ap, _w=wm_ap:
                             e.tensor_tensor(
                                 _w, _a, t_gge[:, :], ALU.mult), inc=b.vv)
                    w_done = b.vv.val
                    if dbg:
                        b.wait("sync", b.vv, w_done)
                        nsub4 = SS // 128
                        b.dma("sync", d_wtok.ap()
                              [:, (grp * NSLAB + sl) * nsub4 * NEXP:
                               (grp * NSLAB + sl + 1) * nsub4 * NEXP],
                              t_wtok[:, :])
                    b.wait("tensor", b.vv, w_done)
                    for sub in range(SS // 128):
                        b.op("tensor", lambda e, _s=sub: e.transpose(
                            pU[0:NEXP, _s * 128:(_s + 1) * 128],
                            t_wmask[:, _s * NEXP:(_s + 1) * NEXP],
                            t_identf[:, :]),
                            inc=(b.tt if sub == SS // 128 - 1 else None))
                    wtb = b.tt.val
                    b.wait("vector", b.tt, wtb)
                    b.op("vector", lambda e, _sl=sl: e.tensor_copy(
                        t_wem2[:, _sl * SS:(_sl + 1) * SS],
                        pU[0:NEXP, 0:SS]), inc=b.vv)
                gate_vv = b.vv.val
                acc_vv = [gate_vv, gate_vv]

                for isched, (kind, ei) in enumerate(sched):
                    shared = kind == "sh"
                    if not shared:
                        for sl in range(NSLAB):
                            b.wait("sync", b.vv, gate_vv)
                            b.wait("sync", b.tt)
                            st_sd = b.dma(
                                "sync", t_wstage[0:1, :],
                                t_wem2[ei:ei + 1, sl * SS:(sl + 1) * SS])
                            b.wait("tensor", b.sd, st_sd)
                            b.wait("tensor", b.vv)
                            b.op("tensor", lambda e: e.matmul(
                                pS1[:, 0:SS], t_ones1f[0:1, :],
                                t_wstage[0:1, :],
                                start=True, stop=True), inc=b.tt)
                            b.wait("vector", b.tt)
                            b.op("vector", lambda e, _sl=sl: e.tensor_copy(
                                t_webc[_sl][:, :], pS1[:, 0:SS]), inc=b.vv)
                    if dbg and not shared:
                        for sl in range(NSLAB):
                            b.wait("sync", b.vv)
                            b.dma("sync",
                                  d_web.ap()[ei:ei + 1,
                                             (grp * NSLAB + sl) * SS:
                                             (grp * NSLAB + sl + 1) * SS],
                                  t_webc[sl][0:1, :])
                    webc_done = b.vv.val

                    for ib in range(IB):
                        j = isched * IB + ib
                        bi = (j % 2) * 2
                        for sl in range(NSLAB):
                            ls0 = sl * SS
                            hp = it_h % 2
                            pH1 = (pA, pY)[hp]
                            pH3 = (pB, pAT)[hp]
                            b.wait("tensor", b.sgd, w13_sd[j])
                            b.wait("tensor", b.sa,
                                   max(hb_done, silu_sa[hp]))
                            b.wait("tensor", b.vv, pm_vv[hp])
                            for kc in range(NB):
                                b.op("tensor", lambda e, _kc=kc, _s=ls0,
                                     _bi=bi, _pp=pH1: e.matmul(
                                         _pp[:, 0:SS],
                                         t_w13[_bi][:, _kc * 128:
                                                    (_kc + 1) * 128],
                                         t_hb[:, _kc * GT + _s:
                                              _kc * GT + _s + SS],
                                         start=(_kc == 0),
                                         stop=(_kc == NB - 1)),
                                     inc=(b.tt if kc == NB - 1 else None))
                            h1_tt = b.tt.val
                            b.wait("scalar", b.tt, h1_tt)
                            b.wait("scalar", b.vv)
                            b.op("scalar", lambda e, _pp=pH1: e.activation(
                                t_q0[:, :], _pp[:, 0:SS], AF.Silu), inc=b.sa)
                            silu_d = b.sa.val
                            silu_sa[hp] = silu_d
                            for kc in range(NB):
                                b.op("tensor", lambda e, _kc=kc, _s=ls0,
                                     _bi=bi, _pp=pH3: e.matmul(
                                         _pp[:, 0:SS],
                                         t_w13[_bi + 1][:, _kc * 128:
                                                        (_kc + 1) * 128],
                                         t_hb[:, _kc * GT + _s:
                                              _kc * GT + _s + SS],
                                         start=(_kc == 0),
                                         stop=(_kc == NB - 1)),
                                     inc=(b.tt if kc == NB - 1 else None))
                            h3_tt = b.tt.val
                            if not shared:
                                b.wait("gpsimd", b.sa, silu_d)
                                b.wait("gpsimd", b.vv, webc_done)
                                b.op("gpsimd", lambda e, _sl=sl:
                                     e.tensor_tensor(
                                         t_q0[:, :], t_q0[:, :],
                                         t_webc[_sl][:, :], ALU.mult),
                                     inc=b.gg)
                                b.wait("vector", b.gg)
                            else:
                                b.wait("vector", b.sa, silu_d)
                            b.wait("vector", b.tt, h3_tt)
                            b.op("vector", lambda e, _sl=sl, _ib=ib,
                                 _pp=pH3: e.tensor_tensor(
                                     t_p[:, (_sl * IB + _ib) * SS:
                                         (_sl * IB + _ib + 1) * SS],
                                     _pp[:, 0:SS], t_q0[:, :], ALU.mult),
                                 inc=b.vv)
                            pm_vv[hp] = b.vv.val
                            it_h += 1
                        w13_tt[j % 2] = b.tt.val
                        if j + 2 < len(flat):
                            issue_w13(j + 2)
                    p_done = b.vv.val

                    b.wait("tensor", b.vv, p_done)
                    for ob in range(NB):
                        wb = t_w2ob[ob % 2]
                        b.wait("gpsimd", b.tt, w2ob_tt[ob % 2])
                        if shared:
                            b.dma("gpsimd", wb[:, :], sw2S[ei, ob])
                        else:
                            b.dma("gpsimd", wb[:, :], w2S[ei, ob])
                        w2ob_sd = b.sgd.val
                        b.wait("tensor", b.sgd, w2ob_sd)
                        for sl in range(NSLAB):
                            ls0 = sl * SS
                            wp = it_w % 2
                            pp = (pU, pS2)[wp]
                            b.wait("tensor", b.vv, acc_vv[wp])
                            for ikc in range(IB):
                                b.op("tensor", lambda e, _ik=ikc,
                                     _sl=sl, _pp=pp, _wb=wb: e.matmul(
                                         _pp[:, 0:SS],
                                         _wb[:, _ik * 128:(_ik + 1) * 128],
                                         t_p[:, (_sl * IB + _ik) * SS:
                                             (_sl * IB + _ik + 1) * SS],
                                         start=(_ik == 0),
                                         stop=(_ik == IB - 1)),
                                     inc=(b.tt if ikc == IB - 1 else None))
                            y_tt = b.tt.val
                            b.wait("vector", b.tt, y_tt)
                            b.op("vector", lambda e, _ob=ob, _s=ls0, _pp=pp:
                                 e.tensor_tensor(
                                     t_acc[:, _ob * GT + _s:
                                           _ob * GT + _s + SS],
                                     t_acc[:, _ob * GT + _s:
                                           _ob * GT + _s + SS],
                                     _pp[:, 0:SS], ALU.add), inc=b.vv)
                            acc_vv[wp] = b.vv.val
                            it_w += 1
                        w2ob_tt[ob % 2] = b.tt.val

                for sl in range(NSLAB):
                    ls0 = sl * SS
                    sq_tt = [0, 0]
                    acc_done = b.vv.val
                    for bb2 in range(NB):
                        b.wait("scalar", b.vv, acc_done)
                        b.wait("scalar", b.tt, sq_tt[bb2 % 2])
                        b.op("scalar", lambda e, _b=bb2, _s=ls0: e.activation(
                            q_sq[_b % 2][:, :],
                            t_acc[:, _b * GT + _s:_b * GT + _s + SS],
                            AF.Square), inc=b.sa)
                        b.wait("tensor", b.sa)
                        b.wait("tensor", b.vv, acc_done)
                        b.op("tensor", lambda e, _b=bb2, _s=ls0: e.matmul(
                            pS1[0:1, 0:SS], t_onescol[:, :],
                            t_acc[:, _b * GT + _s:_b * GT + _s + SS],
                            start=(_b == 0), stop=(_b == NB - 1)), inc=None)
                        b.op("tensor", lambda e, _b=bb2: e.matmul(
                            pS2[0:1, 0:SS], t_onescol[:, :],
                            q_sq[_b % 2][:, :],
                            start=(_b == 0), stop=(_b == NB - 1)), inc=b.tt)
                        sq_tt[bb2 % 2] = b.tt.val
                    lnstats = b.tt.val
                    b.wait("vector", b.tt, lnstats)
                    b.op("vector", lambda e: e.tensor_copy(
                        q_row1[:, :], pS1[0:1, 0:SS]), inc=b.vv)
                    b.op("vector", lambda e: e.tensor_copy(
                        q_row2[:, :], pS2[0:1, 0:SS]), inc=b.vv)
                    rows = b.vv.val
                    b.wait("tensor", b.vv, rows)
                    b.op("tensor", lambda e: e.matmul(
                        pS1[:, 0:SS], t_ones1f[0:1, :], q_row1[:, :],
                        start=True, stop=True), inc=None)
                    b.op("tensor", lambda e: e.matmul(
                        pS2[:, 0:SS], t_ones1f[0:1, :], q_row2[:, :],
                        start=True, stop=True), inc=b.tt)
                    lnbc = b.tt.val
                    b.wait("scalar", b.tt, lnbc)
                    b.op("scalar", lambda e: e.activation(
                        q_mu[:, :], pS1[:, 0:SS], AF.Copy, scale=1.0 / D),
                        inc=b.sa)
                    lnmu = b.sa.val
                    b.wait("vector", b.sa, lnmu)
                    b.op("vector", lambda e: e.tensor_tensor(
                        q_mu2[:, :], q_mu[:, :], q_mu[:, :], ALU.mult),
                        inc=b.vv)
                    b.op("vector", lambda e: e.scalar_tensor_tensor(
                        q_rstd[:, :], pS2[:, 0:SS], 1.0 / D, q_mu2[:, :],
                        ALU.mult, ALU.subtract), inc=b.vv)
                    lnq = b.vv.val
                    b.wait("scalar", b.vv, lnq)
                    b.op("scalar", lambda e: e.activation(
                        q_rstd[:, :], q_rstd[:, :], AF.Ln,
                        bias=t_eps[:, 1:2]), inc=b.sa)
                    b.op("scalar", lambda e: e.activation(
                        q_rstd[:, :], q_rstd[:, :], AF.Exp, scale=-0.5),
                        inc=b.sa)
                    lnr = b.sa.val
                    b.wait("vector", b.sa, lnr)
                    for bb2 in range(NB):
                        b.op("vector", lambda e, _b=bb2, _s=ls0:
                             e.tensor_tensor(
                                 t_acc[:, _b * GT + _s:_b * GT + _s + SS],
                                 t_acc[:, _b * GT + _s:_b * GT + _s + SS],
                                 q_mu[:, :], ALU.subtract), inc=b.vv)
                        b.op("vector", lambda e, _b=bb2, _s=ls0:
                             e.tensor_tensor(
                                 t_acc[:, _b * GT + _s:_b * GT + _s + SS],
                                 t_acc[:, _b * GT + _s:_b * GT + _s + SS],
                                 q_rstd[:, :], ALU.mult), inc=b.vv)
                    oln = b.vv.val
                    b.wait("sync", b.vv, oln)
                    b.dma("sync",
                          outT.rearrange("(b p) t -> p b t", p=128)
                          [:, :, g0 + ls0:g0 + ls0 + SS],
                          t_acc[:, :].rearrange("p (b t) -> p b t", b=NB)
                          [:, :, ls0:ls0 + SS])

            b.wait("sync", b.sd)
            b.wait("sync", b.sgd)

        _finish(b, block)

    return nc


def _finish(b, block):
    @block.tensor
    def _(e):
        b.lower(e, "tensor")

    @block.vector
    def _(e):
        b.lower(e, "vector")

    @block.scalar
    def _(e):
        b.lower(e, "scalar")

    @block.gpsimd
    def _(e):
        b.lower(e, "gpsimd")

    @block.sync
    def _(e):
        b.lower(e, "sync")



def _prep_inputs(c, kw):
    g = lambda n: np.asarray(kw["L0_attn_" + n])
    f = lambda n: np.asarray(kw["L0_ffn_" + n])
    x = g("x")
    Bsz, T, D = x.shape
    NB = D // 128

    def vec_tile(v):
        return np.ascontiguousarray(v.reshape(-1)[: D].reshape(NB, 128).T)

    maa_list = [g("time_maa_x"), g("time_maa_w"), g("time_maa_k"),
                g("time_maa_v"), g("time_maa_r"), g("time_maa_g")]
    maa_vecs = np.concatenate([vec_tile(v) for v in maa_list], axis=1)
    maa_w2_in = g("time_maa_w2")
    maa_w2 = np.ascontiguousarray(
        maa_w2_in.transpose(1, 0, 2).reshape(c.DMIX, 5 * D))

    def tile_w_out(WTmat, qch):
        nb = WTmat.shape[0] // 128
        npass = WTmat.shape[1] // qch
        a = WTmat.reshape(nb, 128, npass, qch).transpose(2, 1, 0, 3)
        return np.ascontiguousarray(a.reshape(npass, 128, nb * qch))

    IB, NBc = c.IB, c.NB
    w1_ = f("w1"); w3_ = f("w3"); w2_ = f("w2")
    sw1_ = f("sw1"); sw3_ = f("sw3"); sw2_ = f("sw2")
    w1S = np.ascontiguousarray(
        w1_.reshape(c.NEXP, IB, 128, NBc, 128).transpose(0, 1, 4, 3, 2)
        .reshape(c.NEXP, IB, 128, D))
    w3S = np.ascontiguousarray(
        w3_.reshape(c.NEXP, IB, 128, NBc, 128).transpose(0, 1, 4, 3, 2)
        .reshape(c.NEXP, IB, 128, D))
    w2S = np.ascontiguousarray(
        w2_.reshape(c.NEXP, NBc, 128, IB, 128).transpose(0, 1, 4, 3, 2)
        .reshape(c.NEXP, NBc, 128, c.INTER))
    sw1S = np.ascontiguousarray(
        sw1_.reshape(2, IB, 128, NBc, 128).transpose(0, 1, 4, 3, 2)
        .reshape(2, IB, 128, D))
    sw3S = np.ascontiguousarray(
        sw3_.reshape(2, IB, 128, NBc, 128).transpose(0, 1, 4, 3, 2)
        .reshape(2, IB, 128, D))
    sw2S = np.ascontiguousarray(
        sw2_.reshape(NBc, 128, 2, IB, 128).transpose(2, 0, 4, 3, 1)
        .reshape(2, NBc, 128, c.INTER))
    shared = {
        "maa_vecs": maa_vecs.astype(np.float32),
        "maa_w1": np.ascontiguousarray(g("time_maa_w1")).astype(np.float32),
        "maa_w2": maa_w2.astype(np.float32),
        "decay_vec": vec_tile(g("time_decay")).astype(np.float32),
        "decay_w1": np.ascontiguousarray(g("time_decay_w1")).astype(np.float32),
        "decay_w2": np.ascontiguousarray(g("time_decay_w2")).astype(np.float32),
        "u_vec": vec_tile(g("time_faaaa")).astype(np.float32),
        "WrS": tile_w_out(np.ascontiguousarray(g("W_r").T), c.QCH).astype(np.float32),
        "WkS": tile_w_out(np.ascontiguousarray(g("W_k").T), c.QCH).astype(np.float32),
        "WvS": tile_w_out(np.ascontiguousarray(g("W_v").T), c.QCH).astype(np.float32),
        "WgS": tile_w_out(np.ascontiguousarray(g("W_g").T), c.QCH).astype(np.float32),
        "WoS": np.ascontiguousarray(
            np.ascontiguousarray(g("W_o").T).reshape(NB, 128, D)
            .transpose(1, 0, 2).reshape(128, NB * D)).astype(np.float32),
        "gam": vec_tile(g("ln_gamma")).astype(np.float32),
        "bet": vec_tile(g("ln_beta")).astype(np.float32),
        "gateT": np.ascontiguousarray(f("gate_w").T).astype(np.float32),
        "w1S": w1S.astype(np.float32),
        "w3S": w3S.astype(np.float32),
        "w2S": w2S.astype(np.float32),
        "sw1S": sw1S.astype(np.float32),
        "sw3S": sw3S.astype(np.float32),
        "sw2S": sw2S.astype(np.float32),
    }
    in_maps = []
    for core in range(c.NCORE):
        bi, half = core // 2, core % 2
        t0 = half * c.TC
        xs = x[bi, t0:t0 + c.TC].T
        halo = (np.zeros((D, 1), np.float32) if t0 == 0
                else x[bi, t0 - 1:t0].T)
        xT = np.ascontiguousarray(
            np.concatenate([halo, xs], axis=1)).astype(np.float32)
        cm = np.full((128, 1), 0.0 if half == 0 else 1.0, np.float32)
        m = dict(shared)
        m["xT"] = xT
        m["cmask"] = cm
        in_maps.append(m)
    return in_maps


_CFG = None
_NC = None


def kernel(**inputs):
    global _CFG, _NC
    c = Cfg(small=False)
    in_maps = _prep_inputs(c, inputs)
    if _NC is None:
        _NC = build_program(c)
        _CFG = c
    res = run_bass_kernel_spmd(_NC, in_maps, core_ids=list(range(c.NCORE)))
    x = np.asarray(inputs["L0_attn_x"])
    Bsz, T, D = x.shape
    out = np.empty((Bsz, T, D), np.float32)
    for core in range(c.NCORE):
        bi, half = core // 2, core % 2
        t0 = half * c.TC
        out[bi, t0:t0 + c.TC] = res.results[core]["outT"].T
    return out


# version 09
# speedup vs baseline: 1.1642x; 1.0913x over previous
import contextlib
import numpy as np
import ml_dtypes

import concourse.bass as bass
import concourse.mybir as mybir
from concourse.bass_utils import run_bass_kernel_spmd

F32 = mybir.dt.float32
BF16 = mybir.dt.bfloat16
AF = mybir.ActivationFunctionType
ALU = mybir.AluOpType
AX = mybir.AxisListType
BF = ml_dtypes.bfloat16


class Cfg:
    def __init__(self, small=False, tiny=False):
        if tiny:
            self.D, self.TC, self.SS, self.QCH = 256, 128, 128, 128
            self.NEXP, self.INTER, self.NCORE = 4, 256, 2
        elif small:
            self.D, self.TC, self.SS, self.QCH = 256, 256, 128, 128
            self.NEXP, self.INTER, self.NCORE = 4, 256, 2
        else:
            self.D, self.TC, self.SS, self.QCH = 2048, 2048, 512, 256
            self.NEXP, self.INTER, self.NCORE = 16, 1408, 8
        self.SINTER = 2 * self.INTER
        self.L = 64
        self.NB = self.D // 128
        self.NST = self.TC // self.SS
        self.NPASS = self.D // self.QCH
        self.PBLK = self.QCH // 128
        self.NCHK = self.SS // self.L
        self.DMIX = 32
        self.DD = 64
        self.IB = self.INTER // 128
        self.GT = self.TC // 2
        self.NSLAB = self.GT // self.SS
        self.EPS_GN = 1e-5 * 64.0
        self.EPS_LN = 1e-5


class Sem:
    __slots__ = ("name", "val", "h", "dense", "targets")

    def __init__(self, name, dense=False):
        self.name, self.val, self.h = name, 0, None
        self.dense = dense
        self.targets = set()


class B:

    ENGS = ("tensor", "vector", "scalar", "gpsimd", "sync")

    def __init__(self):
        self.mark_tensor = None
        self.mark_vector = None
        self.mark_gpsimd = None
        self.q = {e: [] for e in self.ENGS}
        self.sems = [Sem("sd", True), Sem("sgd", True), Sem("tt"), Sem("vv"),
                     Sem("sa"), Sem("scc", True), Sem("gg")]
        (self.sd, self.sgd, self.tt, self.vv,
         self.sa, self.scc, self.gg) = self.sems

    def wait(self, eng, sem, val=None):
        v = sem.val if val is None else val
        if v <= 0:
            return
        if not sem.dense:
            sem.targets.add(v)
        self.q[eng].append(("w", sem, v))

    def op(self, eng, fn, inc=None, n=1):
        if inc is not None:
            inc.val += n
            if not inc.dense:
                inc.targets.add(inc.val)
            self.q[eng].append(("o", fn, inc, inc.val))
        else:
            self.q[eng].append(("o", fn, None, 0))

    def dma(self, eng, out, in_, **kw):
        s = self.sgd if eng == "gpsimd" else self.sd
        self.q[eng].append(
            ("o", lambda e, _o=out, _i=in_, _kw=kw: e.dma_start(
                out=_o, in_=_i, **_kw), s, -16))
        s.val += 16
        return s.val

    def vsync(self):
        self.wait("vector", self.vv)

    def barrier(self):
        self.op("tensor", self.mark_tensor, inc=self.tt)
        self.op("vector", self.mark_vector, inc=self.vv)
        self.op("gpsimd", self.mark_gpsimd, inc=self.gg)
        svals = [(self.tt, self.tt.val), (self.vv, self.vv.val),
                 (self.gg, self.gg.val), (self.sd, self.sd.val),
                 (self.sgd, self.sgd.val), (self.sa, self.sa.val),
                 (self.scc, self.scc.val)]
        for e in self.ENGS:
            for sem, v in svals:
                self.wait(e, sem, v)

    def lower(self, eng_handle, eng_name):
        ranks = {}
        for s in self.sems:
            if not s.dense:
                ranks[s.name] = {t: i + 1 for i, t in
                                 enumerate(sorted(s.targets))}
        for rec in self.q[eng_name]:
            if rec[0] == "w":
                _, sem, v = rec
                if sem.dense:
                    eng_handle.wait_ge(sem.h, v)
                else:
                    r = ranks[sem.name].get(v)
                    assert r is not None
                    eng_handle.wait_ge(sem.h, r)
            else:
                _, fn, sem, cnt = rec
                if sem is None:
                    fn(eng_handle)
                elif cnt == -16:
                    fn(eng_handle).then_inc(sem.h, 16)
                elif sem.dense:
                    fn(eng_handle).then_inc(sem.h, cnt - (cnt - 1))
                else:
                    if cnt in sem.targets:
                        fn(eng_handle).then_inc(sem.h, 1)
                    else:
                        fn(eng_handle)


def build_program(cfg, dbg=False, stop_after=None):
    c = cfg
    nc = bass.Bass("TRN2", target_bir_lowering=False)
    D, TC, SS, L, NB = c.D, c.TC, c.SS, c.L, c.NB
    NST, NPASS, PBLK, NCHK, QCH = c.NST, c.NPASS, c.PBLK, c.NCHK, c.QCH
    NEXP, IB, GT, NSLAB = c.NEXP, c.IB, c.GT, c.NSLAB

    dt = nc.dram_tensor
    xT = dt("xT", [D, TC + 1], F32, kind="ExternalInput").ap()
    maa_vecs = dt("maa_vecs", [128, 6 * NB], F32, kind="ExternalInput").ap()
    maa_w1 = dt("maa_w1", [D, 5 * c.DMIX], F32, kind="ExternalInput").ap()
    maa_w2 = dt("maa_w2", [c.DMIX, 5 * D], F32, kind="ExternalInput").ap()
    decay_vec = dt("decay_vec", [128, NB], F32, kind="ExternalInput").ap()
    decay_w1 = dt("decay_w1", [D, c.DD], F32, kind="ExternalInput").ap()
    decay_w2 = dt("decay_w2", [c.DD, D], F32, kind="ExternalInput").ap()
    u_vec = dt("u_vec", [128, NB], F32, kind="ExternalInput").ap()
    WqS = {s: dt("W%sS" % s, [NPASS, 128, NB * QCH], F32,
                 kind="ExternalInput").ap() for s in ("r", "k", "v", "g")}
    WoS = dt("WoS", [128, NB * D], F32, kind="ExternalInput").ap()
    gam = dt("gam", [128, NB], F32, kind="ExternalInput").ap()
    bet = dt("bet", [128, NB], F32, kind="ExternalInput").ap()
    gateT = dt("gateT", [D, NEXP], F32, kind="ExternalInput").ap()
    w1S = dt("w1S", [NEXP, IB, 128, D], F32, kind="ExternalInput").ap()
    w3S = dt("w3S", [NEXP, IB, 128, D], F32, kind="ExternalInput").ap()
    w2S = dt("w2S", [NEXP, NB, 128, c.INTER], F32,
             kind="ExternalInput").ap()
    sw1S = dt("sw1S", [2, IB, 128, D], F32, kind="ExternalInput").ap()
    sw3S = dt("sw3S", [2, IB, 128, D], F32, kind="ExternalInput").ap()
    sw2S = dt("sw2S", [2, NB, 128, c.INTER], F32, kind="ExternalInput").ap()
    cmask = dt("cmask", [128, 1], F32, kind="ExternalInput").ap()
    outT = dt("outT", [D, TC], F32, kind="ExternalOutput").ap()

    spk = "ExternalOutput" if dbg else "Internal"
    sp_xs = dt("sp_xs", [5, NST, 128, NB * SS], BF16, kind=spk)
    sp_y = dt("sp_y", [D, TC], F32, kind=spk)
    sp_rh = dt("sp_rh", [D, TC], BF16, kind=spk)
    sp_g = dt("sp_g", [D, TC], BF16, kind=spk)
    sp_h = dt("sp_h", [D, TC], F32, kind=spk)
    if dbg:
        d_r = dt("d_r", [D, TC], BF16, kind="ExternalOutput")
        d_k = dt("d_k", [D, TC], BF16, kind="ExternalOutput")
        d_v = dt("d_v", [D, TC], BF16, kind="ExternalOutput")
        d_pd = dt("d_pd", [D, TC], F32, kind="ExternalOutput")
        d_td = dt("d_td", [c.DD, TC], BF16, kind="ExternalOutput")
        d_cs = dt("d_cs", [D, TC], F32, kind="ExternalOutput")
        d_csr = dt("d_csr", [D, TC], F32, kind="ExternalOutput")
        d_bv = dt("d_bv", [D, TC // c.L], F32, kind="ExternalOutput")
        d_e1 = dt("d_e1", [D, TC], F32, kind="ExternalOutput")
        d_e5 = dt("d_e5", [D, TC // c.L], F32, kind="ExternalOutput")
        d_rt = dt("d_rt", [D, TC], BF16, kind="ExternalOutput")
        d_web = dt("d_web", [NEXP, 2 * NSLAB * SS], F32, kind="ExternalOutput")
        d_wtok = dt("d_wtok", [128, 2 * NSLAB * (SS // 128) * NEXP], F32,
                    kind="ExternalOutput")
    cc_in = dt("cc_in", [128, NB * L], F32)
    cc_out = dt("cc_out", [256, NB * L], F32)

    STREAMS = ("w", "k", "v", "r", "g")
    SMAA = {"w": 1, "k": 2, "v": 3, "r": 4, "g": 5}
    SF = {"w": 0, "k": 1, "v": 2, "r": 3, "g": 4}

    b = B()

    def blk3(ap_, nseg, seg):
        return ap_.rearrange("p (a t) -> p a t", a=nseg)

    with contextlib.ExitStack() as es:
        sb = lambda n, s, d=F32: es.enter_context(nc.sbuf_tensor(n, s, d))
        ps = lambda n, s, d=F32: es.enter_context(nc.psum_tensor(n, s, d))

        t_maa = sb("t_maa", [128, 6 * NB])
        t_dec = sb("t_dec", [128, NB])
        t_u = sb("t_u", [128, NB], BF16)
        t_gam = sb("t_gam", [128, NB])
        t_bet = sb("t_bet", [128, NB])
        t_ident = sb("t_ident", [128, 128], BF16)
        t_identf = sb("t_identf", [128, 128])
        t_maskU = sb("t_maskU", [128, L])
        t_ones1f = sb("t_ones1f", [1, 128])
        t_onescol = sb("t_onescol", [128, 1])
        t_hones = sb("t_hones", [128, 2])
        t_hpat = sb("t_hpat", [2, 128])
        t_cmask = sb("t_cmask", [128, 1])
        t_eps = sb("t_eps", [128, 2])
        t_zeroSS = sb("t_zeroSS", [128, SS])
        t_S = sb("t_S", [128, NB * L])
        t_Sin = sb("t_Sin", [128, NB * L], BF16)

        pA = ps("pA", [128, SS])
        pB = ps("pB", [128, SS])
        pY = ps("pY", [128, SS])
        pAT = ps("pAT", [128, SS])
        pTP = ps("pTP", [128, 1024], BF16)
        pU = ps("pU", [128, SS])
        pS1 = ps("pS1", [128, SS])
        pS2 = ps("pS2", [128, SS])

        for s in b.sems:
            s.h = es.enter_context(nc.semaphore(s.name))
        block = es.enter_context(nc.Block())

        maskU_np = np.triu(np.ones((64, 64), np.float32), 1)
        maskU_np = np.concatenate([maskU_np, maskU_np], 0)
        hones_np = np.zeros((128, 2), np.float32)
        hones_np[0:64, 0] = 1.0
        hones_np[64:128, 1] = 1.0
        consts = [
            (t_ident, nc.inline_tensor(np.eye(128, dtype=BF), "c_ident")),
            (t_identf, nc.inline_tensor(np.eye(128, dtype=np.float32), "c_identf")),
            (t_maskU, nc.inline_tensor(maskU_np, "c_maskU")),
            (t_ones1f, nc.inline_tensor(np.ones((1, 128), np.float32), "c_o1f")),
            (t_onescol, nc.inline_tensor(np.ones((128, 1), np.float32), "c_oc")),
            (t_hones, nc.inline_tensor(hones_np, "c_hones")),
            (t_hpat, nc.inline_tensor(np.ascontiguousarray(hones_np.T), "c_hpat")),
            (t_eps, nc.inline_tensor(np.broadcast_to(np.array(
                [c.EPS_GN, c.EPS_LN], np.float32), (128, 2)).copy(), "c_eps")),
        ]

        t_mark = sb("t_mark", [1, 2])
        b.mark_tensor = lambda e: e.matmul(
            pS1[0:1, 0:1], t_ones1f[0:1, 0:1], t_ones1f[0:1, 0:1],
            start=True, stop=True)
        b.mark_vector = lambda e: e.memset(t_mark[0:1, 0:1], 0.0)
        b.mark_gpsimd = lambda e: e.memset(t_mark[0:1, 1:2], 0.0)

        for dst, src in ((t_maa, maa_vecs), (t_dec, decay_vec),
                         (t_gam, gam), (t_bet, bet), (t_cmask, cmask)):
            b.dma("sync", dst[:, :], src[:, :])
        for dst, dr in consts:
            b.dma("sync", dst[:, :], dr.ap()[:, :])
        b.dma("gpsimd", t_u[:, :], u_vec[:, :])
        b.wait("vector", b.sd)
        b.op("vector", lambda e: e.memset(t_zeroSS[:, :], 0.0), inc=b.vv)
        b.op("vector", lambda e: e.memset(t_S[:, :], 0.0), inc=b.vv)

        if stop_after == "P0":
            pass
        with contextlib.ExitStack() as es1:
            sb1 = lambda n, s, d=F32: es1.enter_context(nc.sbuf_tensor(n, s, d))
            t_maaw1 = sb1("t_maaw1", [128, NB * 5 * c.DMIX], BF16)
            t_maaw2 = sb1("t_maaw2", [c.DMIX, 5 * D], BF16)
            t_dw1 = sb1("t_dw1", [128, NB * c.DD], BF16)
            t_dw2 = sb1("t_dw2", [c.DD, D], BF16)
            t_wq = {s: sb1("t_wq_%s" % s, [128, NB * QCH], BF16)
                    for s in ("r", "k", "v", "g")}
            t_xfull = sb1("t_xfull", [128, NB * (SS + 1)], BF16)
            t_xx = sb1("t_xx", [128, NB * SS], BF16)
            t_stream = [sb1("t_stream%d" % i, [128, NB * SS], BF16)
                        for i in range(2)]
            t_xxx = t_stream[1]
            t_a = sb1("t_a", [c.DMIX, 5 * SS], BF16)
            t_td = sb1("t_td", [c.DD, SS], BF16)
            t_rb = [sb1("t_rb%d" % i, [128, PBLK * SS], BF16) for i in range(2)]
            t_kb = [sb1("t_kb%d" % i, [128, PBLK * SS], BF16) for i in range(2)]
            t_vb = [sb1("t_vb%d" % i, [128, PBLK * SS], BF16) for i in range(2)]
            t_gqb = [sb1("t_gqb%d" % i, [128, PBLK * SS], BF16)
                     for i in range(2)]
            t_pdb = [sb1("t_pdb%d" % i, [128, PBLK * SS]) for i in range(2)]
            t_cs = sb1("t_cs", [128, SS])
            t_csr = sb1("t_csr", [128, SS])
            t_csx = sb1("t_csx", [128, SS])
            t_bx = sb1("t_bx", [128, SS])
            t_e1 = sb1("t_e1", [128, SS])
            t_rtb = [sb1("t_rtb%d" % i, [128, SS], BF16) for i in range(2)]
            t_ktb = [sb1("t_ktb%d" % i, [128, SS], BF16) for i in range(2)]
            t_khb = [sb1("t_khb%d" % i, [128, SS], BF16) for i in range(2)]
            t_rkb = [sb1("t_rkb%d" % i, [128, SS], BF16) for i in range(2)]
            t_rhatb = [sb1("t_rhatb%d" % i, [128, SS], BF16) for i in range(2)]
            t_bvec = sb1("t_bvec", [128, NCHK])
            t_evec = sb1("t_evec", [128, NCHK])
            t_plvb = [sb1("t_plvb%d" % i, [128, NCHK]) for i in range(2)]
            t_e5v = sb1("t_e5v", [128, NCHK])
            t_carryb = [sb1("t_carryb%d" % i, [128, PBLK]) for i in range(2)]
            t_Sbf = sb1("t_Sbf", [128, L], BF16)
            t_amt = sb1("t_amt", [128, SS], BF16)
            t_vtok = sb1("t_vtok", [128, 128], BF16)
            t_ktok = sb1("t_ktok", [128, 128], BF16)
            t_sdrow = sb1("t_sdrow", [1, 2 * SS])
            t_dv = sb1("t_dv", [128, SS], BF16)
            t_yob = [sb1("t_yob%d" % i, [128, SS]) for i in range(2)]

            b.dma("gpsimd", t_maaw1[:, :],
                  maa_w1.rearrange("(b p) f -> p b f", p=128))
            b.dma("gpsimd", t_maaw2[:, :], maa_w2[:, :])
            b.dma("gpsimd", t_dw1[:, :],
                  decay_w1.rearrange("(b p) f -> p b f", p=128))
            b.dma("gpsimd", t_dw2[:, :], decay_w2[:, :])

            last_proj = [0]
            wq_ld = [0]
            spill_sd = {}
            proj_tt = {}
            pa_free = {"vv": 0, "sa": 0}
            pb_free = {"vv": 0, "sa": 0}
            pd_free = {0: 0, 1: 0}
            gq_sd = {0: 0, 1: 0}
            post_tt = {0: 0, 1: 0}
            post_vv = [0]
            prods_bb = {0: 0, 1: 0}
            spill_done = {}
            xfull_free = [0]

            def wbank(bk):
                b.wait("tensor", b.vv, bk["vv"])
                b.wait("tensor", b.sa, bk["sa"])

            def emit_prologue(q):
                b.wait("gpsimd", b.tt, last_proj[0])
                for s in ("r", "k", "v", "g"):
                    b.dma("gpsimd", t_wq[s][:, :], WqS[s][q])
                wq_ld[0] = b.sgd.val
                b.wait("vector", b.vv)
                b.op("vector", lambda e, _q=q: e.memset(
                    t_carryb[_q % 2][:, :], 0.0), inc=b.vv)

            def emit_proj(q, st, par):
                qc0 = q * QCH
                tok0 = st * SS
                t_pd = t_pdb[par]
                t_r, t_k, t_v = t_rb[par], t_kb[par], t_vb[par]
                t_gq = t_gqb[par]
                wq_loaded = wq_ld[0]
                if True:
                    if q == 0:
                        b.wait("gpsimd", b.vv, xfull_free[0])
                        xf_d = b.dma("gpsimd", t_xfull[:, :],
                                     xT.rearrange("(b p) t -> p b t", p=128)
                                     [:, :, tok0:tok0 + SS + 1])
                        b.wait("vector", b.sgd, xf_d)
                        b.wait("vector", b.tt, proj_tt.get(1, 0))
                        x3 = blk3(t_xfull[:, :], NB, SS + 1)
                        b.op("vector", lambda e, _x=x3: e.tensor_tensor(
                            blk3(t_xx[:, :], NB, SS), _x[:, :, 0:SS],
                            _x[:, :, 1:SS + 1], ALU.subtract), inc=b.vv)
                        for bb in range(NB):
                            b.op("vector", lambda e, _b=bb: e.scalar_tensor_tensor(
                                t_xxx[:, _b * SS:(_b + 1) * SS],
                                t_xx[:, _b * SS:(_b + 1) * SS],
                                t_maa[:, _b:_b + 1],
                                t_xfull[:, _b * (SS + 1) + 1:(_b + 1) * (SS + 1)],
                                ALU.mult, ALU.add), inc=b.vv)
                        xxx_done = b.vv.val
                        b.wait("tensor", b.vv, xxx_done)
                        wbank(pa_free)
                        for f in range(5):
                            for kc in range(NB):
                                b.op("tensor", lambda e, _f=f, _kc=kc: e.matmul(
                                    pA[0:c.DMIX, 0:SS],
                                    t_maaw1[:, _kc * 5 * c.DMIX + _f * c.DMIX:
                                            _kc * 5 * c.DMIX + (_f + 1) * c.DMIX],
                                    t_xxx[:, _kc * SS:(_kc + 1) * SS],
                                    start=(_kc == 0), stop=(_kc == NB - 1)),
                                    inc=(b.tt if kc == NB - 1 else None))
                            b.wait("scalar", b.tt)
                            b.op("scalar", lambda e, _f=f: e.activation(
                                t_a[:, _f * SS:(_f + 1) * SS],
                                pA[0:c.DMIX, 0:SS], AF.Tanh), inc=b.sa)
                            b.wait("tensor", b.sa)
                        a_done = b.sa.val
                        pa_free["sa"] = a_done

                    rkvg_dst = {"r": t_r, "k": t_k, "v": t_v}
                    for si, s in enumerate(STREAMS):
                        sbuf = t_stream[si % 2]
                        if q == 0:
                            b.wait("vector", b.sd, spill_sd.get(si % 2, 0))
                            b.wait("vector", b.tt, proj_tt.get(si % 2, 0))
                            for bb in range(NB):
                                bpp = pB if bb % 2 == 0 else pA
                                bbk = pb_free if bb % 2 == 0 else pa_free
                                wbank(bbk)
                                b.op("tensor", lambda e, _s=s, _b=bb,
                                     _pp=bpp: e.matmul(
                                    _pp[:, 0:SS],
                                    t_maaw2[:, SF[_s] * D + _b * 128:
                                            SF[_s] * D + (_b + 1) * 128],
                                    t_a[:, SF[_s] * SS:(SF[_s] + 1) * SS],
                                    start=True, stop=True), inc=b.tt)
                                b.wait("vector", b.tt)
                                b.op("vector", lambda e, _s=s, _b=bb,
                                     _pp=bpp: e.scalar_tensor_tensor(
                                         t_bx[:, 0:SS], _pp[:, 0:SS],
                                         t_maa[:, SMAA[_s] * NB + _b:
                                               SMAA[_s] * NB + _b + 1],
                                         t_xx[:, _b * SS:(_b + 1) * SS],
                                         ALU.add, ALU.mult), inc=b.vv)
                                bbk["vv"] = b.vv.val
                                b.op("vector", lambda e, _sb=sbuf, _b=bb:
                                     e.tensor_tensor(
                                         _sb[:, _b * SS:(_b + 1) * SS],
                                         t_bx[:, 0:SS],
                                         t_xfull[:, _b * (SS + 1) + 1:
                                                 (_b + 1) * (SS + 1)],
                                         ALU.add), inc=b.vv)
                            built = b.vv.val
                            b.wait("sync", b.vv, built)
                            spill_sd[si % 2] = b.dma(
                                "sync", sp_xs.ap()[SF[s], st], sbuf[:, :])
                            spill_done[(s, st)] = spill_sd[si % 2]
                            b.wait("tensor", b.vv, built)
                        else:
                            b.wait("gpsimd", b.tt, proj_tt.get(si % 2, 0))
                            b.wait("gpsimd", b.sd, spill_done[(s, st)])
                            b.wait("gpsimd", b.sd, spill_sd.get(si % 2, 0))
                            ld = b.dma("gpsimd", sbuf[:, :],
                                       sp_xs.ap()[SF[s], st])
                            b.wait("tensor", b.sgd, ld)

                        b.wait("tensor", b.sgd, wq_loaded)
                        if s == "w":
                            wbank(pa_free)
                            for kc in range(NB):
                                b.op("tensor", lambda e, _kc=kc, _sb=sbuf:
                                     e.matmul(
                                         pA[0:c.DD, 0:SS],
                                         t_dw1[:, _kc * c.DD:(_kc + 1) * c.DD],
                                         _sb[:, _kc * SS:(_kc + 1) * SS],
                                         start=(_kc == 0), stop=(_kc == NB - 1)),
                                     inc=(b.tt if kc == NB - 1 else None))
                            b.wait("scalar", b.tt)
                            b.op("scalar", lambda e: e.activation(
                                t_td[:, :], pA[0:c.DD, 0:SS], AF.Tanh),
                                inc=b.sa)
                            pa_free["sa"] = b.sa.val
                            b.wait("tensor", b.sa)
                            if dbg:
                                b.wait("sync", b.sa)
                                b.dma("sync", d_td.ap()[:, tok0:tok0 + SS],
                                      t_td[:, :])
                            b.wait("scalar", b.vv, pd_free[par])
                            for ob in range(PBLK):
                                wpp = pB if ob % 2 == 0 else pA
                                wbk = pb_free if ob % 2 == 0 else pa_free
                                wbank(wbk)
                                b.op("tensor", lambda e, _ob=ob, _q0=qc0,
                                     _pp=wpp: e.matmul(
                                         _pp[:, 0:SS],
                                         t_dw2[:, _q0 + _ob * 128:
                                               _q0 + (_ob + 1) * 128],
                                         t_td[:, :], start=True, stop=True),
                                     inc=b.tt)
                                b.wait("scalar", b.tt)
                                b.op("scalar", lambda e, _ob=ob, _q=q,
                                     _pp=wpp: e.activation(
                                         t_pd[:, _ob * SS:(_ob + 1) * SS],
                                         _pp[:, 0:SS], AF.Exp,
                                         bias=t_dec[:, _q * PBLK + _ob:
                                                    _q * PBLK + _ob + 1]),
                                     inc=b.sa)
                                wbk["sa"] = b.sa.val
                        else:
                            for ob in range(PBLK):
                                pp = pA if ob % 2 == 0 else pB
                                pbk = pa_free if ob % 2 == 0 else pb_free
                                wbank(pbk)
                                for kc in range(NB):
                                    b.op("tensor", lambda e, _s=s, _ob=ob,
                                         _kc=kc, _pp=pp, _sb=sbuf: e.matmul(
                                             _pp[:, 0:SS],
                                             t_wq[_s][:, _kc * QCH + _ob * 128:
                                                      _kc * QCH + (_ob + 1) * 128],
                                             _sb[:, _kc * SS:(_kc + 1) * SS],
                                             start=(_kc == 0),
                                             stop=(_kc == NB - 1)),
                                         inc=(b.tt if kc == NB - 1 else None))
                                if s == "g":
                                    if ob == 0:
                                        b.wait("scalar", b.sd, gq_sd[par])
                                    b.wait("scalar", b.tt)
                                    b.op("scalar", lambda e, _ob=ob, _pp=pp,
                                         _gq=t_gq: e.activation(
                                             _gq[:, _ob * SS:(_ob + 1) * SS],
                                             _pp[:, 0:SS], AF.Silu), inc=b.sa)
                                    pbk["sa"] = b.sa.val
                                else:
                                    b.wait("vector", b.tt)
                                    b.op("vector", lambda e, _d=rkvg_dst[s],
                                         _ob=ob, _pp=pp: e.tensor_copy(
                                             _d[:, _ob * SS:(_ob + 1) * SS],
                                             _pp[:, 0:SS]), inc=b.vv)
                                    pbk["vv"] = b.vv.val
                        proj_tt[si % 2] = b.tt.val
                        if si == len(STREAMS) - 1:
                            last_proj[0] = b.tt.val
                    if q == 0:
                        xfull_free[0] = b.vv.val
                    b.wait("sync", b.sa)
                    gq_sd[par] = b.dma(
                        "sync", sp_g.ap().rearrange("(b p) t -> p b t", p=128)
                        [:, q * PBLK:(q + 1) * PBLK, tok0:tok0 + SS],
                        t_gq[:, :])

                    if dbg:
                        b.wait("sync", b.vv)
                        b.wait("sync", b.sa)
                        for _nm, _t in (("d_r", t_r), ("d_k", t_k),
                                        ("d_v", t_v), ("d_pd", t_pd)):
                            b.dma("sync", locals()["_t"] if False else
                                  dict(d_r=d_r, d_k=d_k, d_v=d_v,
                                       d_pd=d_pd)[_nm].ap()
                                  .rearrange("(b p) t -> p b t", p=128)
                                  [:, q * PBLK:(q + 1) * PBLK,
                                   tok0:tok0 + SS], _t[:, :])

            def emit_pre_bb(q, st, par, bb):
                tok0 = st * SS
                gb = q * PBLK + bb
                t_pd = t_pdb[par]
                t_r, t_k = t_rb[par], t_kb[par]
                t_carry = t_carryb[q % 2]
                t_rt, t_kt = t_rtb[bb], t_ktb[bb]
                t_kh, t_rk = t_khb[bb], t_rkb[bb]
                t_rhat, t_plv = t_rhatb[bb], t_plvb[bb]
                b.wait("vector", b.tt, post_tt[bb])
                if True:
                    if True:
                        b.op("vector", lambda e, _b=bb: e.tensor_tensor_scan(
                            t_cs[:, :], t_pd[:, _b * SS:(_b + 1) * SS],
                            t_zeroSS[:, :], 0.0, ALU.add, ALU.add), inc=b.vv)
                        b.vsync()
                        b.op("vector", lambda e: e.memset(t_bvec[:, 0:1], 0.0),
                             inc=b.vv)
                        if NCHK > 1:
                            b.op("vector", lambda e: e.tensor_copy(
                                t_bvec[:, 1:NCHK],
                                blk3(t_cs[:, :], NCHK, L)
                                [:, 0:NCHK - 1, L - 1:L].squeeze(2)), inc=b.vv)
                        b.vsync()
                        b.op("vector", lambda e: e.tensor_tensor(
                            blk3(t_csr[:, :], NCHK, L),
                            blk3(t_cs[:, :], NCHK, L),
                            t_bvec[:, :].unsqueeze(2)
                            .broadcast_to([128, NCHK, L]), ALU.subtract),
                            inc=b.vv)
                        b.op("vector", lambda e, _b=bb: e.tensor_tensor(
                            t_csx[:, :], t_csr[:, :],
                            t_pd[:, _b * SS:(_b + 1) * SS], ALU.subtract),
                            inc=b.vv)
                        if dbg:
                            b.wait("sync", b.vv)
                            b.dma("sync", d_cs.ap()
                                  .rearrange("(b p) t -> p b t", p=128)
                                  [:, gb, tok0:tok0 + SS], t_cs[:, :])
                            b.dma("sync", d_csr.ap()
                                  .rearrange("(b p) t -> p b t", p=128)
                                  [:, gb, tok0:tok0 + SS], t_csr[:, :])
                            b.dma("sync", d_bv.ap()
                                  .rearrange("(b p) t -> p b t", p=128)
                                  [:, gb, tok0 // c.L:tok0 // c.L + NCHK],
                                  t_bvec[:, :])
                        b.op("vector", lambda e: e.tensor_copy(
                            t_evec[:, :], blk3(t_csr[:, :], NCHK, L)
                            [:, :, L - 1:L].squeeze(2)), inc=b.vv)
                        b.op("vector", lambda e, _b=bb: e.tensor_scalar(
                            t_e5v[:, :], t_bvec[:, :],
                            t_carry[:, _b:_b + 1], None, ALU.add), inc=b.vv)
                        b.op("vector", lambda e, _b=bb: e.tensor_tensor(
                            t_carry[:, _b:_b + 1], t_carry[:, _b:_b + 1],
                            t_cs[:, SS - 1:SS], ALU.add), inc=b.vv)
                        pre_exp = b.vv.val
                        b.wait("scalar", b.vv, pre_exp)
                        b.op("scalar", lambda e: e.activation(
                            t_e1[:, :], t_csx[:, :], AF.Exp, scale=-1.0),
                            inc=b.sa)
                        b.op("scalar", lambda e: e.activation(
                            t_csx[:, :], t_csr[:, :], AF.Exp), inc=b.sa)
                        b.op("scalar", lambda e: e.activation(
                            t_plv[:, :], t_evec[:, :], AF.Exp, scale=-1.0),
                            inc=b.sa)
                        b.op("scalar", lambda e: e.activation(
                            t_e5v[:, :], t_e5v[:, :], AF.Exp, scale=-1.0),
                            inc=b.sa)
                        exps1 = b.sa.val
                        b.wait("vector", b.sa, exps1)
                        b.vsync()
                        b.op("vector", lambda e: e.tensor_tensor(
                            blk3(t_csr[:, :], NCHK, L),
                            blk3(t_csr[:, :], NCHK, L),
                            t_evec[:, :].unsqueeze(2)
                            .broadcast_to([128, NCHK, L]), ALU.subtract),
                            inc=b.vv)
                        b.wait("scalar", b.vv)
                        b.op("scalar", lambda e: e.activation(
                            t_csr[:, :], t_csr[:, :], AF.Exp), inc=b.sa)
                        e3_done = b.sa.val
                        if dbg:
                            b.wait("sync", b.sa, e3_done)
                            b.dma("sync", d_e1.ap()
                                  .rearrange("(b p) t -> p b t", p=128)
                                  [:, gb, tok0:tok0 + SS], t_e1[:, :])
                            b.dma("sync", d_e5.ap()
                                  .rearrange("(b p) t -> p b t", p=128)
                                  [:, gb, tok0 // c.L:tok0 // c.L + NCHK],
                                  t_e5v[:, :])
                        b.wait("vector", b.sa, e3_done)
                        b.op("vector", lambda e, _b=bb: e.tensor_tensor(
                            t_rt[:, :], t_r[:, _b * SS:(_b + 1) * SS],
                            t_e1[:, :], ALU.mult), inc=b.vv)
                        b.op("vector", lambda e, _b=bb: e.tensor_tensor(
                            t_kt[:, :], t_k[:, _b * SS:(_b + 1) * SS],
                            t_csx[:, :], ALU.mult), inc=b.vv)
                        b.op("vector", lambda e, _b=bb: e.tensor_tensor(
                            t_kh[:, :], t_k[:, _b * SS:(_b + 1) * SS],
                            t_csr[:, :], ALU.mult), inc=b.vv)
                        b.op("vector", lambda e, _b=bb: e.tensor_tensor(
                            t_rk[:, :], t_r[:, _b * SS:(_b + 1) * SS],
                            t_k[:, _b * SS:(_b + 1) * SS], ALU.mult), inc=b.vv)
                        b.op("vector", lambda e: e.tensor_tensor(
                            blk3(t_rhat[:, :], NCHK, L),
                            blk3(t_rt[:, :], NCHK, L),
                            t_e5v[:, :].unsqueeze(2)
                            .broadcast_to([128, NCHK, L]), ALU.mult), inc=b.vv)
                        prods = b.vv.val
                        prods_bb[bb] = prods
                        pd_free[par] = prods
                        if dbg:
                            b.wait("sync", b.vv, prods)
                            b.dma("sync", d_rt.ap()
                                  .rearrange("(b p) t -> p b t", p=128)
                                  [:, gb, tok0:tok0 + SS], t_rt[:, :])
                        b.wait("sync", b.vv, prods)
                        b.dma("sync", sp_rh.ap()
                              .rearrange("(b p) t -> p b t", p=128)
                              [:, gb, tok0:tok0 + SS], t_rhat[:, :])

            def emit_post_bb(q, st, par, bb):
                tok0 = st * SS
                gb = q * PBLK + bb
                t_v = t_vb[par]
                t_rt, t_kt = t_rtb[bb], t_ktb[bb]
                t_kh, t_rk = t_khb[bb], t_rkb[bb]
                t_plv = t_plvb[bb]
                t_yo = t_yob[bb]
                prods = prods_bb[bb]
                if True:
                    if True:
                        b.wait("tensor", b.vv, prods)
                        b.wait("tensor", b.vv, post_vv[0])
                        b.op("tensor", lambda e, _gb=gb: e.matmul(
                            pS1[0:1, 0:SS], t_u[0:64, _gb:_gb + 1],
                            t_rk[0:64, :], start=True, stop=True), inc=None)
                        b.op("tensor", lambda e, _gb=gb: e.matmul(
                            pS2[0:1, 0:SS], t_u[64:128, _gb:_gb + 1],
                            t_rk[64:128, :], start=True, stop=True,
                            tile_position=(64, 0)), inc=None)
                        for ch in range(NCHK):
                            c0 = ch * L
                            b.op("tensor", lambda e, _c0=c0: e.matmul(
                                pAT[0:64, _c0:_c0 + L],
                                t_kt[0:64, _c0:_c0 + L],
                                t_rt[0:64, _c0:_c0 + L],
                                start=True, stop=True), inc=None)
                            b.op("tensor", lambda e, _c0=c0: e.matmul(
                                pAT[64:128, _c0:_c0 + L],
                                t_kt[64:128, _c0:_c0 + L],
                                t_rt[64:128, _c0:_c0 + L], start=True,
                                stop=True, tile_position=(64, 64)),
                                inc=(b.tt if ch == NCHK - 1 else None))
                        at_done = b.tt.val
                        b.wait("vector", b.tt, at_done)
                        b.op("vector", lambda e: e.tensor_tensor(
                            blk3(t_amt[:, :], NCHK, L),
                            blk3(pAT[:, 0:SS], NCHK, L),
                            t_maskU[:, :].unsqueeze(1)
                            .broadcast_to([128, NCHK, L]), ALU.mult), inc=b.vv)
                        b.op("vector", lambda e: e.tensor_copy(
                            t_sdrow[0:1, 0:SS], pS1[0:1, 0:SS]), inc=b.vv)
                        b.op("vector", lambda e: e.tensor_copy(
                            t_sdrow[0:1, SS:2 * SS], pS2[0:1, 0:SS]), inc=b.vv)
                        amt_done = b.vv.val
                        b.wait("tensor", b.vv, amt_done)
                        b.op("tensor", lambda e: e.matmul(
                            pAT[0:64, 0:SS], t_ones1f[0:1, 0:64],
                            t_sdrow[0:1, 0:SS], start=True, stop=True),
                            inc=None)
                        b.op("tensor", lambda e: e.matmul(
                            pAT[64:128, 0:SS], t_ones1f[0:1, 0:64],
                            t_sdrow[0:1, SS:2 * SS], start=True, stop=True,
                            tile_position=(0, 64)), inc=b.tt)
                        sdbc = b.tt.val
                        b.wait("vector", b.tt, sdbc)
                        b.op("vector", lambda e, _b=bb: e.tensor_tensor(
                            t_dv[:, :], pAT[:, 0:SS],
                            t_v[:, _b * SS:(_b + 1) * SS], ALU.mult),
                            inc=b.vv)
                        dv_done = b.vv.val

                        for ch in range(NCHK):
                            c0 = ch * L
                            b.op("vector", lambda e, _gb=gb: e.tensor_copy(
                                t_Sbf[:, :], t_S[:, _gb * L:(_gb + 1) * L]),
                                inc=b.vv)
                            scast = b.vv.val
                            b.wait("tensor", b.vv, scast)
                            b.op("tensor", lambda e, _c0=c0, _b=bb: e.transpose(
                                pTP[0:64, 0:128],
                                t_v[:, _b * SS + _c0:_b * SS + _c0 + L],
                                t_ident[:, :]), inc=None)
                            b.op("tensor", lambda e, _c0=c0, _b=bb: e.transpose(
                                pTP[64:128, 0:128],
                                t_v[:, _b * SS + _c0:_b * SS + _c0 + L],
                                t_ident[:, :], tile_position=(0, 64)),
                                inc=None)
                            b.op("tensor", lambda e, _c0=c0: e.transpose(
                                pTP[0:64, 128:256], t_kh[:, _c0:_c0 + L],
                                t_ident[:, :]), inc=None)
                            b.op("tensor", lambda e, _c0=c0: e.transpose(
                                pTP[64:128, 128:256], t_kh[:, _c0:_c0 + L],
                                t_ident[:, :], tile_position=(0, 64)),
                                inc=b.tt)
                            tp_done = b.tt.val
                            b.wait("vector", b.tt, tp_done)
                            b.op("vector", lambda e: e.tensor_copy(
                                t_vtok[:, :], pTP[:, 0:128]), inc=b.vv)
                            b.op("vector", lambda e: e.tensor_copy(
                                t_ktok[:, :], pTP[:, 128:256]), inc=b.vv)
                            tok_done = b.vv.val
                            b.wait("tensor", b.vv, tok_done)
                            b.op("tensor", lambda e, _c0=c0: e.matmul(
                                pY[0:64, _c0:_c0 + L], t_vtok[0:64, 0:64],
                                t_amt[0:64, _c0:_c0 + L],
                                start=True, stop=False), inc=None)
                            b.op("tensor", lambda e, _c0=c0: e.matmul(
                                pY[0:64, _c0:_c0 + L], t_Sbf[0:64, :],
                                t_rt[0:64, _c0:_c0 + L],
                                start=False, stop=True), inc=None)
                            b.op("tensor", lambda e, _c0=c0: e.matmul(
                                pY[64:128, _c0:_c0 + L],
                                t_vtok[64:128, 64:128],
                                t_amt[64:128, _c0:_c0 + L], start=True,
                                stop=False, tile_position=(64, 64)), inc=None)
                            b.op("tensor", lambda e, _c0=c0: e.matmul(
                                pY[64:128, _c0:_c0 + L], t_Sbf[64:128, :],
                                t_rt[64:128, _c0:_c0 + L], start=False,
                                stop=True, tile_position=(64, 64)), inc=None)
                            b.op("tensor", lambda e, _c0=c0: e.matmul(
                                pU[0:64, _c0:_c0 + L], t_ktok[0:64, 0:64],
                                t_vtok[0:64, 0:64], start=True, stop=True),
                                inc=None)
                            b.op("tensor", lambda e, _c0=c0: e.matmul(
                                pU[64:128, _c0:_c0 + L],
                                t_ktok[64:128, 64:128],
                                t_vtok[64:128, 64:128], start=True, stop=True,
                                tile_position=(64, 64)), inc=b.tt)
                            yu_done = b.tt.val
                            b.wait("vector", b.tt, yu_done)
                            b.op("vector", lambda e, _gb=gb, _ch=ch:
                                 e.tensor_scalar(
                                     t_S[:, _gb * L:(_gb + 1) * L],
                                     t_S[:, _gb * L:(_gb + 1) * L],
                                     t_plv[:, _ch:_ch + 1], None, ALU.mult),
                                 inc=b.vv)
                            b.op("vector", lambda e, _gb=gb, _c0=c0:
                                 e.tensor_tensor(
                                     t_S[:, _gb * L:(_gb + 1) * L],
                                     t_S[:, _gb * L:(_gb + 1) * L],
                                     pU[:, _c0:_c0 + L], ALU.add), inc=b.vv)
                        b.wait("vector", b.tt)
                        b.op("vector", lambda e: e.tensor_tensor(
                            t_yo[:, :], pY[:, 0:SS], t_dv[:, :], ALU.add),
                            inc=b.vv)
                        yo_done = b.vv.val
                        b.wait("sync", b.vv, yo_done)
                        b.dma("sync", sp_y.ap()
                              .rearrange("(b p) t -> p b t", p=128)
                              [:, gb, tok0:tok0 + SS], t_yo[:, :])
                post_vv[0] = b.vv.val
                post_tt[bb] = b.tt.val

            prev = None
            for q in range(NPASS):
                emit_prologue(q)
                for st in range(NST):
                    par = (q * NST + st) % 2
                    if prev is not None:
                        for bb in range(PBLK):
                            emit_pre_bb(prev[0], prev[1], prev[2], bb)
                    emit_proj(q, st, par)
                    if prev is not None:
                        for bb in range(PBLK):
                            emit_post_bb(prev[0], prev[1], prev[2], bb)
                    prev = (q, st, par)
            for bb in range(PBLK):
                emit_pre_bb(prev[0], prev[1], prev[2], bb)
            for bb in range(PBLK):
                emit_post_bb(prev[0], prev[1], prev[2], bb)

            b.barrier()
            b.dma("gpsimd", cc_in.ap()[:, :], t_S[:, :])
            b.wait("gpsimd", b.sgd)
            if stop_after == "NOCC":
                b.dma("gpsimd", cc_out.ap()[0:128, :], cc_in.ap()[:, :])
                b.dma("gpsimd", cc_out.ap()[128:256, :], cc_in.ap()[:, :])
            else:
                b.op("gpsimd", lambda e: e.collective_compute(
                    "AllGather", ALU.bypass,
                    replica_groups=[[2 * i, 2 * i + 1]
                                    for i in range(c.NCORE // 2)],
                    ins=[cc_in.ap().opt()], outs=[cc_out.ap().opt()]),
                    inc=b.scc)
                b.wait("gpsimd", b.scc)
            b.dma("gpsimd", t_S[:, :], cc_out.ap()[0:128, :])
            b.wait("vector", b.sgd)
            b.op("vector", lambda e: e.tensor_scalar(
                t_S[:, :], t_S[:, :], t_cmask[:, 0:1], None, ALU.mult),
                inc=b.vv)
            b.op("vector", lambda e: e.tensor_copy(t_Sin[:, :], t_S[:, :]),
                 inc=b.vv)
            b.barrier()

        if stop_after in ("P1", "P2", "NOCC"):
            b.wait("sync", b.sd)
            b.wait("sync", b.sgd)
            _finish(b, block)
            return nc
        with contextlib.ExitStack() as es3:
            sb3 = lambda n, s, d=F32: es3.enter_context(nc.sbuf_tensor(n, s, d))
            TS3 = SS
            NST3 = TC // TS3
            t_wo = sb3("t_wo", [128, NB * D], BF16)
            t_ytile = sb3("t_ytile", [128, NB * TS3])
            t_gtile = sb3("t_gtile", [128, NB * TS3], BF16)
            t_rhtile = sb3("t_rhtile", [128, NB * TS3], BF16)
            t_yng = sb3("t_yng", [128, NB * TS3], BF16)
            t_xf3 = sb3("t_xf3", [128, NB * TS3])
            t_sq = [sb3("t_sq%d" % i, [128, TS3]) for i in range(2)]
            t_mu = sb3("t_mu", [128, TS3])
            t_mu2 = sb3("t_mu2", [128, TS3])
            t_rstd = sb3("t_rstd", [128, TS3])
            t_row1 = sb3("t_row1", [1, TS3])
            t_row2 = sb3("t_row2", [1, TS3])
            t_strip1 = sb3("t_strip1", [2, TS3])
            t_strip2 = sb3("t_strip2", [2, TS3])

            b.dma("gpsimd", t_wo[:, :], WoS[:, :])
            wo_loaded = b.sgd.val

            for st in range(NST3):
                tok0 = st * TS3
                b.wait("sync", b.vv)
                b.wait("sync", b.tt)
                b.dma("sync", t_ytile[:, :],
                      sp_y.ap().rearrange("(b p) t -> p b t", p=128)
                      [:, :, tok0:tok0 + TS3])
                b.dma("sync", t_rhtile[:, :],
                      sp_rh.ap().rearrange("(b p) t -> p b t", p=128)
                      [:, :, tok0:tok0 + TS3])
                b.dma("sync", t_gtile[:, :],
                      sp_g.ap().rearrange("(b p) t -> p b t", p=128)
                      [:, :, tok0:tok0 + TS3])
                b.dma("sync", t_xf3[:, :],
                      xT.rearrange("(b p) t -> p b t", p=128)
                      [:, :, tok0 + 1:tok0 + TS3 + 1])
                ld_done = b.sd.val

                for gb in range(NB):
                    b.wait("tensor", b.sd, ld_done)
                    b.wait("tensor", b.vv)
                    b.wait("tensor", b.sa)
                    b.op("tensor", lambda e, _gb=gb: e.matmul(
                        pAT[0:64, 0:TS3],
                        t_Sin[0:64, _gb * L:(_gb + 1) * L],
                        t_rhtile[0:64, _gb * TS3:(_gb + 1) * TS3],
                        start=True, stop=True), inc=None)
                    b.op("tensor", lambda e, _gb=gb: e.matmul(
                        pAT[64:128, 0:TS3],
                        t_Sin[64:128, _gb * L:(_gb + 1) * L],
                        t_rhtile[64:128, _gb * TS3:(_gb + 1) * TS3],
                        start=True, stop=True, tile_position=(64, 64)),
                        inc=b.tt)
                    corr_done = b.tt.val
                    b.wait("vector", b.tt, corr_done)
                    b.op("vector", lambda e, _gb=gb: e.tensor_tensor(
                        t_ytile[:, _gb * TS3:(_gb + 1) * TS3],
                        t_ytile[:, _gb * TS3:(_gb + 1) * TS3],
                        pAT[:, 0:TS3], ALU.add), inc=b.vv)
                    yf_done = b.vv.val
                    b.wait("scalar", b.vv, yf_done)
                    b.op("scalar", lambda e, _gb=gb: e.activation(
                        t_sq[0][:, :], t_ytile[:, _gb * TS3:(_gb + 1) * TS3],
                        AF.Square), inc=b.sa)
                    sq_done = b.sa.val
                    b.wait("tensor", b.vv, yf_done)
                    b.wait("tensor", b.sa, sq_done)
                    b.op("tensor", lambda e, _gb=gb: e.matmul(
                        pS1[0:2, 0:TS3], t_hones[:, :],
                        t_ytile[:, _gb * TS3:(_gb + 1) * TS3],
                        start=True, stop=True), inc=None)
                    b.op("tensor", lambda e: e.matmul(
                        pS2[0:2, 0:TS3], t_hones[:, :], t_sq[0][:, :],
                        start=True, stop=True), inc=b.tt)
                    stats_done = b.tt.val
                    b.wait("vector", b.tt, stats_done)
                    b.op("vector", lambda e: e.tensor_copy(
                        t_strip1[:, :], pS1[0:2, 0:TS3]), inc=b.vv)
                    b.op("vector", lambda e: e.tensor_copy(
                        t_strip2[:, :], pS2[0:2, 0:TS3]), inc=b.vv)
                    strips = b.vv.val
                    b.wait("tensor", b.vv, strips)
                    b.op("tensor", lambda e: e.matmul(
                        pS1[:, 0:TS3], t_hpat[:, :], t_strip1[:, :],
                        start=True, stop=True), inc=None)
                    b.op("tensor", lambda e: e.matmul(
                        pS2[:, 0:TS3], t_hpat[:, :], t_strip2[:, :],
                        start=True, stop=True), inc=b.tt)
                    bc_done = b.tt.val
                    b.wait("scalar", b.tt, bc_done)
                    b.op("scalar", lambda e: e.activation(
                        t_mu[:, :], pS1[:, 0:TS3], AF.Copy,
                        scale=1.0 / 64.0), inc=b.sa)
                    mu_done = b.sa.val
                    b.wait("vector", b.sa, mu_done)
                    b.op("vector", lambda e: e.tensor_tensor(
                        t_mu2[:, :], t_mu[:, :], t_mu[:, :], ALU.mult),
                        inc=b.vv)
                    b.op("vector", lambda e: e.scalar_tensor_tensor(
                        t_rstd[:, :], pS2[:, 0:TS3], 1.0 / 64.0, t_mu2[:, :],
                        ALU.mult, ALU.subtract), inc=b.vv)
                    q_done = b.vv.val
                    b.wait("scalar", b.vv, q_done)
                    b.op("scalar", lambda e: e.activation(
                        t_rstd[:, :], t_rstd[:, :], AF.Ln,
                        bias=t_eps[:, 0:1]), inc=b.sa)
                    b.op("scalar", lambda e: e.activation(
                        t_rstd[:, :], t_rstd[:, :], AF.Exp, scale=-0.5),
                        inc=b.sa)
                    rstd_done = b.sa.val
                    b.wait("vector", b.sa, rstd_done)
                    b.op("vector", lambda e, _gb=gb: e.tensor_tensor(
                        t_mu2[:, :], t_ytile[:, _gb * TS3:(_gb + 1) * TS3],
                        t_mu[:, :], ALU.subtract), inc=b.vv)
                    b.op("vector", lambda e: e.tensor_tensor(
                        t_mu2[:, :], t_mu2[:, :], t_rstd[:, :], ALU.mult),
                        inc=b.vv)
                    b.op("vector", lambda e, _gb=gb: e.tensor_scalar(
                        t_mu2[:, :], t_mu2[:, :], t_gam[:, _gb:_gb + 1],
                        t_bet[:, _gb:_gb + 1], ALU.mult, ALU.add), inc=b.vv)
                    b.op("vector", lambda e, _gb=gb: e.tensor_tensor(
                        t_yng[:, _gb * TS3:(_gb + 1) * TS3], t_mu2[:, :],
                        t_gtile[:, _gb * TS3:(_gb + 1) * TS3], ALU.mult),
                        inc=b.vv)
                yng_done = b.vv.val

                b.wait("tensor", b.sgd, wo_loaded)
                b.wait("tensor", b.vv, yng_done)
                for ob in range(NB):
                    pp = pA if ob % 2 == 0 else pB
                    for kc in range(NB):
                        b.op("tensor", lambda e, _ob=ob, _kc=kc, _pp=pp:
                             e.matmul(
                                 _pp[:, 0:TS3],
                                 t_wo[:, _kc * D + _ob * 128:
                                      _kc * D + (_ob + 1) * 128],
                                 t_yng[:, _kc * TS3:(_kc + 1) * TS3],
                                 start=(_kc == 0), stop=(_kc == NB - 1)),
                             inc=(b.tt if kc == NB - 1 else None))
                    b.wait("vector", b.tt)
                    b.op("vector", lambda e, _ob=ob, _pp=pp:
                         e.tensor_tensor(
                             t_ytile[:, _ob * TS3:(_ob + 1) * TS3],
                             _pp[:, 0:TS3],
                             t_xf3[:, _ob * TS3:(_ob + 1) * TS3], ALU.add),
                         inc=b.vv)
                    b.wait("tensor", b.vv)
                h_done = b.vv.val

                sq_tt = [0, 0]
                for bb2 in range(NB):
                    b.wait("scalar", b.vv, h_done)
                    b.wait("scalar", b.tt, sq_tt[bb2 % 2])
                    b.op("scalar", lambda e, _b=bb2: e.activation(
                        t_sq[_b % 2][:, :],
                        t_ytile[:, _b * TS3:(_b + 1) * TS3], AF.Square),
                        inc=b.sa)
                    b.wait("tensor", b.sa)
                    b.op("tensor", lambda e, _b=bb2: e.matmul(
                        pS1[0:1, 0:TS3], t_onescol[:, :],
                        t_ytile[:, _b * TS3:(_b + 1) * TS3],
                        start=(_b == 0), stop=(_b == NB - 1)), inc=None)
                    b.op("tensor", lambda e, _b=bb2: e.matmul(
                        pS2[0:1, 0:TS3], t_onescol[:, :], t_sq[_b % 2][:, :],
                        start=(_b == 0), stop=(_b == NB - 1)), inc=b.tt)
                    sq_tt[bb2 % 2] = b.tt.val
                lnstats = b.tt.val
                b.wait("vector", b.tt, lnstats)
                b.op("vector", lambda e: e.tensor_copy(
                    t_row1[:, :], pS1[0:1, 0:TS3]), inc=b.vv)
                b.op("vector", lambda e: e.tensor_copy(
                    t_row2[:, :], pS2[0:1, 0:TS3]), inc=b.vv)
                rows = b.vv.val
                b.wait("tensor", b.vv, rows)
                b.op("tensor", lambda e: e.matmul(
                    pS1[:, 0:TS3], t_ones1f[0:1, :], t_row1[:, :],
                    start=True, stop=True), inc=None)
                b.op("tensor", lambda e: e.matmul(
                    pS2[:, 0:TS3], t_ones1f[0:1, :], t_row2[:, :],
                    start=True, stop=True), inc=b.tt)
                lnbc = b.tt.val
                b.wait("scalar", b.tt, lnbc)
                b.op("scalar", lambda e: e.activation(
                    t_mu[:, :], pS1[:, 0:TS3], AF.Copy, scale=1.0 / D),
                    inc=b.sa)
                lnmu = b.sa.val
                b.wait("vector", b.sa, lnmu)
                b.op("vector", lambda e: e.tensor_tensor(
                    t_mu2[:, :], t_mu[:, :], t_mu[:, :], ALU.mult), inc=b.vv)
                b.op("vector", lambda e: e.scalar_tensor_tensor(
                    t_rstd[:, :], pS2[:, 0:TS3], 1.0 / D, t_mu2[:, :],
                    ALU.mult, ALU.subtract), inc=b.vv)
                lnq = b.vv.val
                b.wait("scalar", b.vv, lnq)
                b.op("scalar", lambda e: e.activation(
                    t_rstd[:, :], t_rstd[:, :], AF.Ln,
                    bias=t_eps[:, 1:2]), inc=b.sa)
                b.op("scalar", lambda e: e.activation(
                    t_rstd[:, :], t_rstd[:, :], AF.Exp, scale=-0.5), inc=b.sa)
                lnr = b.sa.val
                b.wait("vector", b.sa, lnr)
                for bb2 in range(NB):
                    b.op("vector", lambda e, _b=bb2: e.tensor_tensor(
                        t_ytile[:, _b * TS3:(_b + 1) * TS3],
                        t_ytile[:, _b * TS3:(_b + 1) * TS3], t_mu[:, :],
                        ALU.subtract), inc=b.vv)
                    b.op("vector", lambda e, _b=bb2: e.tensor_tensor(
                        t_ytile[:, _b * TS3:(_b + 1) * TS3],
                        t_ytile[:, _b * TS3:(_b + 1) * TS3], t_rstd[:, :],
                        ALU.mult), inc=b.vv)
                hln = b.vv.val
                b.wait("sync", b.vv, hln)
                b.dma("sync", sp_h.ap().rearrange("(b p) t -> p b t", p=128)
                      [:, :, tok0:tok0 + TS3], t_ytile[:, :])
            b.barrier()

        if stop_after == "P3":
            b.wait("sync", b.sd)
            b.wait("sync", b.sgd)
            _finish(b, block)
            return nc
        with contextlib.ExitStack() as es4:
            sb4 = lambda n, s, d=F32: es4.enter_context(nc.sbuf_tensor(n, s, d))
            t_acc = sb4("t_acc", [128, NB * GT])
            t_hb = sb4("t_hb", [128, NB * GT], BF16)
            t_gate = sb4("t_gate", [128, NB * NEXP], BF16)
            t_w13 = [sb4("t_w13_%d" % i, [128, NB * 128], BF16)
                     for i in range(4)]
            t_w2ob = [sb4("t_w2ob%d" % i, [128, IB * 128], BF16)
                      for i in range(2)]
            t_p = sb4("t_p", [128, NSLAB * IB * SS], BF16)
            t_q0 = sb4("t_q0", [128, SS], BF16)
            t_sc16 = sb4("t_sc16", [NEXP, SS])
            t_wtok = sb4("t_wtok", [128, (SS // 128) * NEXP])
            t_wem2 = sb4("t_wem2", [NEXP, NSLAB * SS])
            t_wstage = sb4("t_wstage", [1, SS])
            t_webc = [sb4("t_webc%d" % i, [128, SS]) for i in range(NSLAB)]
            t_gneg = sb4("t_gneg", [128, 1])
            t_grec = sb4("t_grec", [128, 1])
            t_gm1 = sb4("t_gm1", [128, 1])
            t_gm2 = sb4("t_gm2", [128, 1])
            t_gsum = sb4("t_gsum", [128, 1])
            t_geq = sb4("t_geq", [128, NEXP])
            t_gmk = sb4("t_gmk", [128, NEXP])
            t_gge = sb4("t_gge", [128, NEXP])
            t_wmask = sb4("t_wmask", [128, (SS // 128) * NEXP])
            q_sq = [sb4("u4_sq%d" % i, [128, SS]) for i in range(2)]
            q_mu = sb4("u4_mu", [128, SS])
            q_mu2 = sb4("u4_mu2", [128, SS])
            q_rstd = sb4("u4_rstd", [128, SS])
            q_row1 = sb4("u4_row1", [1, SS])
            q_row2 = sb4("u4_row2", [1, SS])

            b.dma("gpsimd", t_gate[:, :],
                  gateT.rearrange("(b p) e -> p b e", p=128))
            gate_loaded = b.sgd.val

            sched = [("sh", 0), ("sh", 1)] + [("ex", e) for e in range(NEXP)]
            flat = [(kind, ei, ib) for kind, ei in sched for ib in range(IB)]
            w13_tt = {0: 0, 1: 0}
            w13_sd = {}
            w2ob_tt = [0, 0]

            def issue_w13(j):
                kind, ei, ib = flat[j]
                bi = (j % 2) * 2
                b.wait("gpsimd", b.tt, w13_tt[j % 2])
                if kind == "sh":
                    b.dma("gpsimd", t_w13[bi][:, :], sw1S[ei, ib])
                    b.dma("gpsimd", t_w13[bi + 1][:, :], sw3S[ei, ib])
                else:
                    b.dma("gpsimd", t_w13[bi][:, :], w1S[ei, ib])
                    b.dma("gpsimd", t_w13[bi + 1][:, :], w3S[ei, ib])
                w13_sd[j] = b.sgd.val

            for grp in range(2):
                g0 = grp * GT
                b.wait("sync", b.vv)
                b.wait("sync", b.tt)
                b.wait("sync", b.sa)
                hload = b.dma("sync", t_acc[:, :],
                              sp_h.ap().rearrange("(b p) t -> p b t", p=128)
                              [:, :, g0:g0 + GT])
                issue_w13(0)
                issue_w13(1)
                silu_sa = [b.sa.val, b.sa.val]
                pm_vv = [b.vv.val, b.vv.val]
                acc_vv = [b.vv.val, b.vv.val]
                it_h = 0
                it_w = 0
                b.wait("scalar", b.sd, hload)
                for bb2 in range(NB):
                    b.op("scalar", lambda e, _b=bb2: e.activation(
                        t_hb[:, _b * GT:(_b + 1) * GT],
                        t_acc[:, _b * GT:(_b + 1) * GT], AF.Copy), inc=b.sa)
                hb_done = b.sa.val

                b.wait("tensor", b.sgd, gate_loaded)
                for sl in range(NSLAB):
                    ls0 = sl * SS
                    b.wait("tensor", b.sa, hb_done)
                    b.wait("tensor", b.vv)
                    for kc in range(NB):
                        b.op("tensor", lambda e, _kc=kc, _s=ls0: e.matmul(
                            pU[0:NEXP, 0:SS],
                            t_gate[:, _kc * NEXP:(_kc + 1) * NEXP],
                            t_hb[:, _kc * GT + _s:_kc * GT + _s + SS],
                            start=(_kc == 0), stop=(_kc == NB - 1)),
                            inc=(b.tt if kc == NB - 1 else None))
                    gmm = b.tt.val
                    b.wait("vector", b.tt, gmm)
                    b.op("vector", lambda e: e.tensor_copy(
                        t_sc16[:, :], pU[0:NEXP, 0:SS]), inc=b.vv)
                    scd = b.vv.val
                    b.wait("tensor", b.vv, scd)
                    for sub in range(SS // 128):
                        b.op("tensor", lambda e, _s=sub: e.transpose(
                            pU[:, _s * NEXP:(_s + 1) * NEXP],
                            t_sc16[:, _s * 128:(_s + 1) * 128],
                            t_identf[0:NEXP, 0:NEXP]),
                            inc=(b.tt if sub == SS // 128 - 1 else None))
                    gtp = b.tt.val
                    b.wait("vector", b.tt, gtp)
                    b.op("vector", lambda e: e.tensor_copy(
                        t_wtok[:, :], pU[:, 0:(SS // 128) * NEXP]), inc=b.vv)
                    b.vsync()
                    for sub in range(SS // 128):
                        sl_ap = t_wtok[:, sub * NEXP:(sub + 1) * NEXP]
                        wm_ap = t_wmask[:, sub * NEXP:(sub + 1) * NEXP]
                        b.op("vector", lambda e, _a=sl_ap: e.reduce_max(
                            t_gneg[:, :], _a, axis=AX.X, negate=True),
                            inc=b.vv)
                        b.wait("scalar", b.vv)
                        b.op("scalar", lambda e, _a=sl_ap: e.activation(
                            _a, _a, AF.Exp, bias=t_gneg[:, 0:1],
                            accum_out=t_gsum[:, 0:1]), inc=b.sa)
                        b.wait("vector", b.sa)
                        b.op("vector", lambda e: e.reciprocal(
                            t_grec[:, :], t_gsum[:, :]), inc=b.vv)
                        b.vsync()
                        b.op("vector", lambda e, _a=sl_ap: e.tensor_scalar(
                            _a, _a, t_grec[:, 0:1], None, ALU.mult), inc=b.vv)
                        b.vsync()
                        b.op("vector", lambda e, _a=sl_ap: e.reduce_max(
                            t_gm1[:, :], _a, axis=AX.X), inc=b.vv)
                        b.vsync()
                        b.op("vector", lambda e, _a=sl_ap: e.tensor_scalar(
                            t_geq[:, :], _a, t_gm1[:, 0:1], None,
                            ALU.is_equal), inc=b.vv)
                        b.vsync()
                        b.op("vector", lambda e, _a=sl_ap:
                             e.scalar_tensor_tensor(
                                 t_gmk[:, :], t_geq[:, :], -1e30, _a,
                                 ALU.mult, ALU.add), inc=b.vv)
                        b.vsync()
                        b.op("vector", lambda e: e.reduce_max(
                            t_gm2[:, :], t_gmk[:, :], axis=AX.X), inc=b.vv)
                        b.vsync()
                        b.op("vector", lambda e, _a=sl_ap: e.tensor_scalar(
                            t_gge[:, :], _a, t_gm2[:, 0:1], None,
                            ALU.is_ge), inc=b.vv)
                        b.vsync()
                        b.op("vector", lambda e, _a=sl_ap, _w=wm_ap:
                             e.tensor_tensor(
                                 _w, _a, t_gge[:, :], ALU.mult), inc=b.vv)
                    w_done = b.vv.val
                    if dbg:
                        b.wait("sync", b.vv, w_done)
                        nsub4 = SS // 128
                        b.dma("sync", d_wtok.ap()
                              [:, (grp * NSLAB + sl) * nsub4 * NEXP:
                               (grp * NSLAB + sl + 1) * nsub4 * NEXP],
                              t_wtok[:, :])
                    b.wait("tensor", b.vv, w_done)
                    for sub in range(SS // 128):
                        b.op("tensor", lambda e, _s=sub: e.transpose(
                            pU[0:NEXP, _s * 128:(_s + 1) * 128],
                            t_wmask[:, _s * NEXP:(_s + 1) * NEXP],
                            t_identf[:, :]),
                            inc=(b.tt if sub == SS // 128 - 1 else None))
                    wtb = b.tt.val
                    b.wait("vector", b.tt, wtb)
                    b.op("vector", lambda e, _sl=sl: e.tensor_copy(
                        t_wem2[:, _sl * SS:(_sl + 1) * SS],
                        pU[0:NEXP, 0:SS]), inc=b.vv)
                gate_vv = b.vv.val
                acc_vv = [gate_vv, gate_vv]

                for isched, (kind, ei) in enumerate(sched):
                    shared = kind == "sh"
                    if not shared:
                        for sl in range(NSLAB):
                            b.wait("sync", b.vv, gate_vv)
                            b.wait("sync", b.tt)
                            st_sd = b.dma(
                                "sync", t_wstage[0:1, :],
                                t_wem2[ei:ei + 1, sl * SS:(sl + 1) * SS])
                            b.wait("tensor", b.sd, st_sd)
                            b.wait("tensor", b.vv)
                            b.op("tensor", lambda e: e.matmul(
                                pS1[:, 0:SS], t_ones1f[0:1, :],
                                t_wstage[0:1, :],
                                start=True, stop=True), inc=b.tt)
                            b.wait("vector", b.tt)
                            b.op("vector", lambda e, _sl=sl: e.tensor_copy(
                                t_webc[_sl][:, :], pS1[:, 0:SS]), inc=b.vv)
                    if dbg and not shared:
                        for sl in range(NSLAB):
                            b.wait("sync", b.vv)
                            b.dma("sync",
                                  d_web.ap()[ei:ei + 1,
                                             (grp * NSLAB + sl) * SS:
                                             (grp * NSLAB + sl + 1) * SS],
                                  t_webc[sl][0:1, :])
                    webc_done = b.vv.val

                    for ib in range(IB):
                        j = isched * IB + ib
                        bi = (j % 2) * 2
                        for sl in range(NSLAB):
                            ls0 = sl * SS
                            hp = it_h % 2
                            pH1 = (pA, pY)[hp]
                            pH3 = (pB, pAT)[hp]
                            b.wait("tensor", b.sgd, w13_sd[j])
                            b.wait("tensor", b.sa,
                                   max(hb_done, silu_sa[hp]))
                            b.wait("tensor", b.vv, pm_vv[hp])
                            for kc in range(NB):
                                b.op("tensor", lambda e, _kc=kc, _s=ls0,
                                     _bi=bi, _pp=pH1: e.matmul(
                                         _pp[:, 0:SS],
                                         t_w13[_bi][:, _kc * 128:
                                                    (_kc + 1) * 128],
                                         t_hb[:, _kc * GT + _s:
                                              _kc * GT + _s + SS],
                                         start=(_kc == 0),
                                         stop=(_kc == NB - 1)),
                                     inc=(b.tt if kc == NB - 1 else None))
                            h1_tt = b.tt.val
                            b.wait("scalar", b.tt, h1_tt)
                            b.wait("scalar", b.vv)
                            b.op("scalar", lambda e, _pp=pH1: e.activation(
                                t_q0[:, :], _pp[:, 0:SS], AF.Silu), inc=b.sa)
                            silu_d = b.sa.val
                            silu_sa[hp] = silu_d
                            for kc in range(NB):
                                b.op("tensor", lambda e, _kc=kc, _s=ls0,
                                     _bi=bi, _pp=pH3: e.matmul(
                                         _pp[:, 0:SS],
                                         t_w13[_bi + 1][:, _kc * 128:
                                                        (_kc + 1) * 128],
                                         t_hb[:, _kc * GT + _s:
                                              _kc * GT + _s + SS],
                                         start=(_kc == 0),
                                         stop=(_kc == NB - 1)),
                                     inc=(b.tt if kc == NB - 1 else None))
                            h3_tt = b.tt.val
                            if not shared:
                                b.wait("gpsimd", b.sa, silu_d)
                                b.wait("gpsimd", b.vv, webc_done)
                                b.op("gpsimd", lambda e, _sl=sl:
                                     e.tensor_tensor(
                                         t_q0[:, :], t_q0[:, :],
                                         t_webc[_sl][:, :], ALU.mult),
                                     inc=b.gg)
                                b.wait("vector", b.gg)
                            else:
                                b.wait("vector", b.sa, silu_d)
                            b.wait("vector", b.tt, h3_tt)
                            b.op("vector", lambda e, _sl=sl, _ib=ib,
                                 _pp=pH3: e.tensor_tensor(
                                     t_p[:, (_sl * IB + _ib) * SS:
                                         (_sl * IB + _ib + 1) * SS],
                                     _pp[:, 0:SS], t_q0[:, :], ALU.mult),
                                 inc=b.vv)
                            pm_vv[hp] = b.vv.val
                            it_h += 1
                        w13_tt[j % 2] = b.tt.val
                        if j + 2 < len(flat):
                            issue_w13(j + 2)
                    p_done = b.vv.val

                    b.wait("tensor", b.vv, p_done)
                    for ob in range(NB):
                        wb = t_w2ob[ob % 2]
                        b.wait("gpsimd", b.tt, w2ob_tt[ob % 2])
                        if shared:
                            b.dma("gpsimd", wb[:, :], sw2S[ei, ob])
                        else:
                            b.dma("gpsimd", wb[:, :], w2S[ei, ob])
                        w2ob_sd = b.sgd.val
                        b.wait("tensor", b.sgd, w2ob_sd)
                        for sl in range(NSLAB):
                            ls0 = sl * SS
                            wp = it_w % 2
                            pp = (pU, pS2)[wp]
                            b.wait("tensor", b.vv, acc_vv[wp])
                            for ikc in range(IB):
                                b.op("tensor", lambda e, _ik=ikc,
                                     _sl=sl, _pp=pp, _wb=wb: e.matmul(
                                         _pp[:, 0:SS],
                                         _wb[:, _ik * 128:(_ik + 1) * 128],
                                         t_p[:, (_sl * IB + _ik) * SS:
                                             (_sl * IB + _ik + 1) * SS],
                                         start=(_ik == 0),
                                         stop=(_ik == IB - 1)),
                                     inc=(b.tt if ikc == IB - 1 else None))
                            y_tt = b.tt.val
                            b.wait("vector", b.tt, y_tt)
                            b.op("vector", lambda e, _ob=ob, _s=ls0, _pp=pp:
                                 e.tensor_tensor(
                                     t_acc[:, _ob * GT + _s:
                                           _ob * GT + _s + SS],
                                     t_acc[:, _ob * GT + _s:
                                           _ob * GT + _s + SS],
                                     _pp[:, 0:SS], ALU.add), inc=b.vv)
                            acc_vv[wp] = b.vv.val
                            it_w += 1
                        w2ob_tt[ob % 2] = b.tt.val

                for sl in range(NSLAB):
                    ls0 = sl * SS
                    sq_tt = [0, 0]
                    acc_done = b.vv.val
                    for bb2 in range(NB):
                        b.wait("scalar", b.vv, acc_done)
                        b.wait("scalar", b.tt, sq_tt[bb2 % 2])
                        b.op("scalar", lambda e, _b=bb2, _s=ls0: e.activation(
                            q_sq[_b % 2][:, :],
                            t_acc[:, _b * GT + _s:_b * GT + _s + SS],
                            AF.Square), inc=b.sa)
                        b.wait("tensor", b.sa)
                        b.wait("tensor", b.vv, acc_done)
                        b.op("tensor", lambda e, _b=bb2, _s=ls0: e.matmul(
                            pS1[0:1, 0:SS], t_onescol[:, :],
                            t_acc[:, _b * GT + _s:_b * GT + _s + SS],
                            start=(_b == 0), stop=(_b == NB - 1)), inc=None)
                        b.op("tensor", lambda e, _b=bb2: e.matmul(
                            pS2[0:1, 0:SS], t_onescol[:, :],
                            q_sq[_b % 2][:, :],
                            start=(_b == 0), stop=(_b == NB - 1)), inc=b.tt)
                        sq_tt[bb2 % 2] = b.tt.val
                    lnstats = b.tt.val
                    b.wait("vector", b.tt, lnstats)
                    b.op("vector", lambda e: e.tensor_copy(
                        q_row1[:, :], pS1[0:1, 0:SS]), inc=b.vv)
                    b.op("vector", lambda e: e.tensor_copy(
                        q_row2[:, :], pS2[0:1, 0:SS]), inc=b.vv)
                    rows = b.vv.val
                    b.wait("tensor", b.vv, rows)
                    b.op("tensor", lambda e: e.matmul(
                        pS1[:, 0:SS], t_ones1f[0:1, :], q_row1[:, :],
                        start=True, stop=True), inc=None)
                    b.op("tensor", lambda e: e.matmul(
                        pS2[:, 0:SS], t_ones1f[0:1, :], q_row2[:, :],
                        start=True, stop=True), inc=b.tt)
                    lnbc = b.tt.val
                    b.wait("scalar", b.tt, lnbc)
                    b.op("scalar", lambda e: e.activation(
                        q_mu[:, :], pS1[:, 0:SS], AF.Copy, scale=1.0 / D),
                        inc=b.sa)
                    lnmu = b.sa.val
                    b.wait("vector", b.sa, lnmu)
                    b.op("vector", lambda e: e.tensor_tensor(
                        q_mu2[:, :], q_mu[:, :], q_mu[:, :], ALU.mult),
                        inc=b.vv)
                    b.op("vector", lambda e: e.scalar_tensor_tensor(
                        q_rstd[:, :], pS2[:, 0:SS], 1.0 / D, q_mu2[:, :],
                        ALU.mult, ALU.subtract), inc=b.vv)
                    lnq = b.vv.val
                    b.wait("scalar", b.vv, lnq)
                    b.op("scalar", lambda e: e.activation(
                        q_rstd[:, :], q_rstd[:, :], AF.Ln,
                        bias=t_eps[:, 1:2]), inc=b.sa)
                    b.op("scalar", lambda e: e.activation(
                        q_rstd[:, :], q_rstd[:, :], AF.Exp, scale=-0.5),
                        inc=b.sa)
                    lnr = b.sa.val
                    b.wait("vector", b.sa, lnr)
                    for bb2 in range(NB):
                        b.op("vector", lambda e, _b=bb2, _s=ls0:
                             e.tensor_tensor(
                                 t_acc[:, _b * GT + _s:_b * GT + _s + SS],
                                 t_acc[:, _b * GT + _s:_b * GT + _s + SS],
                                 q_mu[:, :], ALU.subtract), inc=b.vv)
                        b.op("vector", lambda e, _b=bb2, _s=ls0:
                             e.tensor_tensor(
                                 t_acc[:, _b * GT + _s:_b * GT + _s + SS],
                                 t_acc[:, _b * GT + _s:_b * GT + _s + SS],
                                 q_rstd[:, :], ALU.mult), inc=b.vv)
                    oln = b.vv.val
                    b.wait("sync", b.vv, oln)
                    b.dma("sync",
                          outT.rearrange("(b p) t -> p b t", p=128)
                          [:, :, g0 + ls0:g0 + ls0 + SS],
                          t_acc[:, :].rearrange("p (b t) -> p b t", b=NB)
                          [:, :, ls0:ls0 + SS])

            b.wait("sync", b.sd)
            b.wait("sync", b.sgd)

        _finish(b, block)

    return nc


def _finish(b, block):
    @block.tensor
    def _(e):
        b.lower(e, "tensor")

    @block.vector
    def _(e):
        b.lower(e, "vector")

    @block.scalar
    def _(e):
        b.lower(e, "scalar")

    @block.gpsimd
    def _(e):
        b.lower(e, "gpsimd")

    @block.sync
    def _(e):
        b.lower(e, "sync")



def _prep_inputs(c, kw):
    g = lambda n: np.asarray(kw["L0_attn_" + n])
    f = lambda n: np.asarray(kw["L0_ffn_" + n])
    x = g("x")
    Bsz, T, D = x.shape
    NB = D // 128

    def vec_tile(v):
        return np.ascontiguousarray(v.reshape(-1)[: D].reshape(NB, 128).T)

    maa_list = [g("time_maa_x"), g("time_maa_w"), g("time_maa_k"),
                g("time_maa_v"), g("time_maa_r"), g("time_maa_g")]
    maa_vecs = np.concatenate([vec_tile(v) for v in maa_list], axis=1)
    maa_w2_in = g("time_maa_w2")
    maa_w2 = np.ascontiguousarray(
        maa_w2_in.transpose(1, 0, 2).reshape(c.DMIX, 5 * D))

    def tile_w_out(WTmat, qch):
        nb = WTmat.shape[0] // 128
        npass = WTmat.shape[1] // qch
        a = WTmat.reshape(nb, 128, npass, qch).transpose(2, 1, 0, 3)
        return np.ascontiguousarray(a.reshape(npass, 128, nb * qch))

    IB, NBc = c.IB, c.NB
    w1_ = f("w1"); w3_ = f("w3"); w2_ = f("w2")
    sw1_ = f("sw1"); sw3_ = f("sw3"); sw2_ = f("sw2")
    w1S = np.ascontiguousarray(
        w1_.reshape(c.NEXP, IB, 128, NBc, 128).transpose(0, 1, 4, 3, 2)
        .reshape(c.NEXP, IB, 128, D))
    w3S = np.ascontiguousarray(
        w3_.reshape(c.NEXP, IB, 128, NBc, 128).transpose(0, 1, 4, 3, 2)
        .reshape(c.NEXP, IB, 128, D))
    w2S = np.ascontiguousarray(
        w2_.reshape(c.NEXP, NBc, 128, IB, 128).transpose(0, 1, 4, 3, 2)
        .reshape(c.NEXP, NBc, 128, c.INTER))
    sw1S = np.ascontiguousarray(
        sw1_.reshape(2, IB, 128, NBc, 128).transpose(0, 1, 4, 3, 2)
        .reshape(2, IB, 128, D))
    sw3S = np.ascontiguousarray(
        sw3_.reshape(2, IB, 128, NBc, 128).transpose(0, 1, 4, 3, 2)
        .reshape(2, IB, 128, D))
    sw2S = np.ascontiguousarray(
        sw2_.reshape(NBc, 128, 2, IB, 128).transpose(2, 0, 4, 3, 1)
        .reshape(2, NBc, 128, c.INTER))
    shared = {
        "maa_vecs": maa_vecs.astype(np.float32),
        "maa_w1": np.ascontiguousarray(g("time_maa_w1")).astype(np.float32),
        "maa_w2": maa_w2.astype(np.float32),
        "decay_vec": vec_tile(g("time_decay")).astype(np.float32),
        "decay_w1": np.ascontiguousarray(g("time_decay_w1")).astype(np.float32),
        "decay_w2": np.ascontiguousarray(g("time_decay_w2")).astype(np.float32),
        "u_vec": vec_tile(g("time_faaaa")).astype(np.float32),
        "WrS": tile_w_out(np.ascontiguousarray(g("W_r").T), c.QCH).astype(np.float32),
        "WkS": tile_w_out(np.ascontiguousarray(g("W_k").T), c.QCH).astype(np.float32),
        "WvS": tile_w_out(np.ascontiguousarray(g("W_v").T), c.QCH).astype(np.float32),
        "WgS": tile_w_out(np.ascontiguousarray(g("W_g").T), c.QCH).astype(np.float32),
        "WoS": np.ascontiguousarray(
            np.ascontiguousarray(g("W_o").T).reshape(NB, 128, D)
            .transpose(1, 0, 2).reshape(128, NB * D)).astype(np.float32),
        "gam": vec_tile(g("ln_gamma")).astype(np.float32),
        "bet": vec_tile(g("ln_beta")).astype(np.float32),
        "gateT": np.ascontiguousarray(f("gate_w").T).astype(np.float32),
        "w1S": w1S.astype(np.float32),
        "w3S": w3S.astype(np.float32),
        "w2S": w2S.astype(np.float32),
        "sw1S": sw1S.astype(np.float32),
        "sw3S": sw3S.astype(np.float32),
        "sw2S": sw2S.astype(np.float32),
    }
    in_maps = []
    for core in range(c.NCORE):
        bi, half = core // 2, core % 2
        t0 = half * c.TC
        xs = x[bi, t0:t0 + c.TC].T
        halo = (np.zeros((D, 1), np.float32) if t0 == 0
                else x[bi, t0 - 1:t0].T)
        xT = np.ascontiguousarray(
            np.concatenate([halo, xs], axis=1)).astype(np.float32)
        cm = np.full((128, 1), 0.0 if half == 0 else 1.0, np.float32)
        m = dict(shared)
        m["xT"] = xT
        m["cmask"] = cm
        in_maps.append(m)
    return in_maps


_CFG = None
_NC = None


def kernel(**inputs):
    global _CFG, _NC
    c = Cfg(small=False)
    in_maps = _prep_inputs(c, inputs)
    if _NC is None:
        _NC = build_program(c)
        _CFG = c
    res = run_bass_kernel_spmd(_NC, in_maps, core_ids=list(range(c.NCORE)))
    x = np.asarray(inputs["L0_attn_x"])
    Bsz, T, D = x.shape
    out = np.empty((Bsz, T, D), np.float32)
    for core in range(c.NCORE):
        bi, half = core // 2, core % 2
        t0 = half * c.TC
        out[bi, t0:t0 + c.TC] = res.results[core]["outT"].T
    return out
